# version 01
# baseline (speedup 1.0000x reference)
from collections import defaultdict
from contextlib import ExitStack

import numpy as np
import concourse.bass as bass
import concourse.mybir as mybir
from concourse.bass_utils import run_bass_kernel_spmd

F32 = mybir.dt.float32
BF16 = mybir.dt.bfloat16
AF = mybir.ActivationFunctionType
ALU = mybir.AluOpType
AX = mybir.AxisListType

D = 2048
CONVD = 8192
KEY = 2048
VAL = 4096
HK, HV = 16, 32
NE, HID = 8, 1792
KC = 4
CH = 128
AB = 256
LN_EPS = 1e-5
L2_EPS = 1e-6
RMS_EPS = 1e-6
LN128 = float(np.log(128.0))
SUBA = {"mix", "l2n", "z", "core", "smm"}
CORE_STEPS = 99
NHV = HV
ST8 = "all"

M_STRICT = 0
M_TRIU = 1
M_ODT = {s: 2 + i for i, s in enumerate((1, 2, 4, 8, 16, 32, 64))}
M_OD1 = 9
M_ONES = 10
NMASK = 11


class Guard:
    __slots__ = ("w", "r")

    def __init__(self):
        self.w = []
        self.r = []

    def rd(self):
        return list(self.w)

    def wr(self):
        return list(self.w) + list(self.r)

    def setw(self, nm, val):
        self.w = [(nm, val)]
        self.r = []

    def addr(self, nm, val):
        self.r.append((nm, val))
        if len(self.r) > 24:
            best = {}
            for nm2, v2 in self.r:
                best[nm2] = max(best.get(nm2, 0), v2)
            self.r = sorted(best.items())


class Emitter:
    STREAMS = ("t", "v", "s", "g", "y")
    ENG_SEM = {"t": "t", "v": "v", "s": "s"}
    DROP_OWN = False

    def __init__(self):
        self.ops = {k: [] for k in self.STREAMS}
        self.counts = defaultdict(int)
        self.pending = {k: [] for k in self.STREAMS}
        self.dma_sems = set()
        self.hiwater = {k: defaultdict(int) for k in self.STREAMS}

    def op(self, stream, build, waits=(), sem=None, amt=1):
        allw = list(waits) + self.pending[stream]
        self.pending[stream] = []
        ws = {}
        own = self.ENG_SEM.get(stream)
        hw = self.hiwater[stream]
        for nm, vv in allw:
            if vv <= 0 or (self.DROP_OWN and nm == own) or vv <= hw[nm]:
                continue
            ws[nm] = max(ws.get(nm, 0), vv)
        for nm, vv in ws.items():
            assert vv <= self.counts[nm], (stream, nm, vv, self.counts[nm])
            hw[nm] = vv
        self.ops[stream].append((sorted(ws.items()), build, sem, amt))
        if sem is not None:
            self.counts[sem] += amt
            return self.counts[sem]
        return None

    def gop(self, stream, build, reads=(), writes=(), extra=(), sem=None):
        if sem is None:
            sem = self.ENG_SEM[stream]
            amt = 1
        else:
            amt = 16
            self.dma_sems.add(sem)
        waits = []
        for gd in writes:
            waits += gd.wr()
        for gd in reads:
            waits += gd.rd()
        waits += list(extra)
        val = self.op(stream, build, waits, sem=sem, amt=amt)
        for gd in writes:
            gd.setw(sem, val)
        for gd in reads:
            gd.addr(sem, val)
        return val

    def barrier(self):
        snap = [(nm, v) for nm, v in self.counts.items() if v > 0]
        for st in self.STREAMS:
            self.pending[st] += snap

    def replay(self, stream, eng, sems):
        for waits, build, incname, amt in self.ops[stream]:
            for nm, v in waits:
                eng.wait_ge(sems[nm], v)
            inst = build(eng)
            if incname is not None:
                inst.then_inc(sems[incname], amt)


def host_masks():
    n = 128
    i = np.arange(n)
    m = np.zeros((n, 128 * NMASK), np.float32)

    def blk(idx):
        return slice(128 * idx, 128 * (idx + 1))

    m[:, blk(M_STRICT)] = (i[:, None] > i[None, :])
    m[:, blk(M_TRIU)] = (i[None, :] >= i[:, None])
    for s, idx in M_ODT.items():
        od = ((i[:, None] // s) % 2 == 1) & ((i[None, :] // s) % 2 == 0) \
            & ((i[:, None] // (2 * s)) == (i[None, :] // (2 * s)))
        m[:, blk(idx)] = od.T
    s = 1
    od1 = ((i[:, None] // s) % 2 == 1) & ((i[None, :] // s) % 2 == 0) \
        & ((i[:, None] // (2 * s)) == (i[None, :] // (2 * s)))
    m[:, blk(M_OD1)] = od1
    m[:, blk(M_ONES)] = 1.0
    return m


def build_nc(T=2048, phases=(0, 1, 2, 3)):
    TTL = T // 128
    NAB = T // AB
    TBF = min(512, T)
    NFB = T // TBF
    NTB = TBF // 128
    CPAB = AB // CH

    nc = bass.Bass("TRN2", target_bir_lowering=False)

    def din(name, shape, dt=F32):
        return nc.dram_tensor(name, shape, dt, kind="ExternalInput").ap()

    x_d = din("x", [T, D])
    wqkv_d = din("wqkv", [D, CONVD])
    wz_d = din("wz", [D, VAL])
    wb_d = din("wb", [D, HV])
    wa_d = din("wa", [D, HV])
    cw_d = din("cw", [CONVD, KC])
    dtb_d = din("dtb", [HV])
    alog_d = din("alog", [HV])
    nw_d = din("nw", [128])
    wout_d = din("wout", [VAL, D])
    gatew_d = din("gatew", [D, NE])
    w1_d = din("w1", [NE, D, HID])
    w2_d = din("w2", [NE, HID, D])
    w3_d = din("w3", [NE, D, HID])
    sw1_d = din("sw1", [D, HID])
    sw2_d = din("sw2", [HID, D])
    sw3_d = din("sw3", [D, HID])
    ebias_d = din("ebias", [NE])
    consts_d = din("i_consts", [128, 8], F32)
    identb_d = din("i_identb", [128, 128], BF16)
    identf_d = din("i_identf", [128, 128], F32)
    ones1f_d = din("i_ones1f", [1, 128], F32)
    onescolb_d = din("i_onescolb", [128, 1], BF16)
    masks_d = din("i_masks", [128, 128 * NMASK], F32)
    iotaE_d = din("i_iotaE", [128, NE], F32)

    out_d = nc.dram_tensor("out", [T, D], F32, kind="ExternalOutput").ap()
    xnT_d = nc.dram_tensor("xnT", [16, 128, T], BF16, kind="Internal").ap()
    g_d = nc.dram_tensor("gd", [T, D], BF16, kind="Internal").ap()

    es = ExitStack()

    def sb(name, shape, dt):
        return es.enter_context(nc.sbuf_tensor(name, shape, dt))

    constcol = sb("constcol", [128, 8], F32)
    identb = sb("identb", [128, 128], BF16)
    identf = sb("identf", [128, 128], F32)
    ones1f = sb("ones1f", [1, 128], F32)
    onescolb = sb("onescolb", [128, 1], BF16)
    masks = sb("masks", [128, 128 * NMASK], F32)
    iotaE = sb("iotaE", [128, NE], F32)
    nwbc = sb("nwbc", [128, 128], F32)
    ebbc = sb("ebbc", [128, NE], F32)
    nwrow = sb("nwrow", [1, 128], F32)
    ebrow = sb("ebrow", [1, NE], F32)
    gatew_sb = sb("gatew_sb", [128, 16 * NE], F32)
    wb_sb = sb("wb_sb", [128, 16 * HV], BF16)
    wa_sb = sb("wa_sb", [128, 16 * HV], BF16)
    cw_sb = sb("cw_sb", [128, 64 * KC], F32)
    dtb_sb = sb("dtb_sb", [HV, 1], F32)
    alog_sb = sb("alog_sb", [HV, 1], F32)
    negea_sb = sb("negea_sb", [HV, 1], F32)
    st_bf = sb("st_bf", [128, HV * 128], BF16)
    wexp = sb("wexp", [128, TTL * NE], F32)
    mixstash = sb("mixstash", [128, 64 * (KC - 1)], F32)
    gjt = sb("gjt", [128, TTL * HV], F32)
    bt = sb("bt", [128, TTL * HV], F32)
    negbt = sb("negbt", [128, TTL * HV], F32)
    egjt = sb("egjt", [128, TTL * HV], F32)
    eglmt = sb("eglmt", [128, TTL * HV], F32)
    begjt = sb("begjt", [128, TTL * HV], F32)
    colsc = sb("colsc", [128, 16], F32)
    rowsc = sb("rowsc", [1, 512], F32)
    hmA = sb("hmA", [HV, TBF], F32)
    hmB = sb("hmB", [HV, TBF], F32)
    hmC = sb("hmC", [HV, TBF], F32)
    onesHM = sb("onesHM", [HV, CH], F32)
    gsm = sb("gsm", [128, 10 * NE], F32)
    sq_sb = sb("sq_sb", [128, AB], BF16)

    arN1 = sb("arN1", [128, 16384], BF16)
    arN2 = sb("arN2", [128, 16384], BF16)
    arN3 = sb("arN3", [128, 8192], BF16)
    arN4 = sb("arN4", [128, 8192], F32)
    arN5 = sb("arN5", [128, 16384], BF16)

    c_cbc = sb("c_cbc", [128, 128], F32)
    c_e1 = sb("c_e1", [128, 128], F32)
    c_e2 = sb("c_e2", [128, 128], F32)
    c_dcy = sb("c_dcy", [128, 128], F32)
    c_dcyT = sb("c_dcyT", [128, 128], F32)
    c_xg = sb("c_xg", [128, 128], F32)
    c_A = sb("c_A", [128, 128], BF16)
    c_AT = sb("c_AT", [128, 128], BF16)
    c_aod = sb("c_aod", [128, 128], BF16)
    c_W = sb("c_W", [128, 128], BF16)
    c_intraT = sb("c_intraT", [128, 128], BF16)
    c_kdec = sb("c_kdec", [128, 128], BF16)
    c_vnew = sb("c_vnew", [128, 128], BF16)
    c_X0 = sb("c_X0", [128, 256], BF16)
    c_Xsb = sb("c_Xsb", [128, 256], BF16)
    c_T0 = sb("c_T0", [128, 256], BF16)
    c_T1 = sb("c_T1", [128, 256], BF16)
    c_ntok = sb("c_ntok", [128, 128], BF16)
    c_scrap = sb("c_scrap", [128, 128], BF16)
    c_cols = sb("c_cols", [128, 8], F32)

    pBk = [es.enter_context(nc.psum_tensor(f"pb{i}", [128, 512], F32)) for i in range(7)]
    pBF = es.enter_context(nc.psum_tensor("pbf", [128, 1024], BF16))

    em = Emitter()
    gB = [Guard() for _ in range(7)]
    gBF = Guard()
    _g = defaultdict(Guard)

    def G(nm):
        return _g[nm]

    def dma_y(build, sem, reads=(), writes=()):
        return em.gop("y", build, reads=reads, writes=writes, sem=sem)

    def dma_g(build, sem, reads=(), writes=()):
        return em.gop("g", build, reads=reads, writes=writes, sem=sem)

    CONST_VALS = (LN_EPS, RMS_EPS, 1.0, -0.5 * LN128, 0.0)
    for ci, cv in enumerate(CONST_VALS):
        nc.const_aps.register(F32, cv, constcol[:, ci:ci + 1])
    dma_y(lambda e, d=constcol, s=consts_d: e.dma_start(out=d[:], in_=s), "dxi", writes=[G("constcol")])
    dma_y(lambda e, d=identb, s=identb_d: e.dma_start(out=d[:], in_=s), "dxi", writes=[G("identb")])
    dma_y(lambda e, d=identf, s=identf_d: e.dma_start(out=d[:], in_=s), "dxi", writes=[G("identf")])
    dma_y(lambda e, d=ones1f, s=ones1f_d: e.dma_start(out=d[:], in_=s), "dxi", writes=[G("ones1f")])
    dma_y(lambda e, d=onescolb, s=onescolb_d: e.dma_start(out=d[:], in_=s), "dxi", writes=[G("onescolb")])
    dma_y(lambda e, d=masks, s=masks_d: e.dma_start(out=d[:], in_=s), "dxi", writes=[G("masks")])
    dma_y(lambda e, d=iotaE, s=iotaE_d: e.dma_start(out=d[:], in_=s), "dxi", writes=[G("iotaE")])
    dma_y(lambda e, d=nwrow, s=nw_d: e.dma_start(out=d[:], in_=s.rearrange("(o f) -> o f", o=1)),
          "dxi", writes=[G("nwrow")])
    dma_y(lambda e, d=ebrow, s=ebias_d: e.dma_start(out=d[:], in_=s.rearrange("(o f) -> o f", o=1)),
          "dxi", writes=[G("ebrow")])
    dma_y(lambda e, d=gatew_sb, s=gatew_d: e.dma_start(
        out=d[:].rearrange("p (k f) -> p k f", k=16),
        in_=s.rearrange("(k p) f -> p k f", p=128)), "dxi", writes=[G("gatew_sb")])
    dma_g(lambda e, d=wb_sb, s=wb_d: e.dma_start(
        out=d[:].rearrange("p (k f) -> p k f", k=16),
        in_=s.rearrange("(k p) f -> p k f", p=128)), "dwi", writes=[G("wb_sb")])
    dma_g(lambda e, d=wa_sb, s=wa_d: e.dma_start(
        out=d[:].rearrange("p (k f) -> p k f", k=16),
        in_=s.rearrange("(k p) f -> p k f", p=128)), "dwi", writes=[G("wa_sb")])
    dma_y(lambda e, d=cw_sb, s=cw_d: e.dma_start(
        out=d[:].rearrange("p (k f) -> p k f", k=64),
        in_=s.rearrange("(k p) f -> p k f", p=128)), "dxi", writes=[G("cw_sb")])
    dma_y(lambda e, d=dtb_sb, s=dtb_d: e.dma_start(out=d[:], in_=s.rearrange("(p o) -> p o", o=1)),
          "dxi", writes=[G("dtb_sb")])
    dma_y(lambda e, d=alog_sb, s=alog_d: e.dma_start(out=d[:], in_=s.rearrange("(p o) -> p o", o=1)),
          "dxi", writes=[G("alog_sb")])

    em.barrier()
    em.gop("t", lambda e: e.matmul(pBk[5][0:128, 0:128], ones1f[:], nwrow[:], start=True, stop=True),
           reads=[G("ones1f"), G("nwrow")], writes=[gB[5]])
    em.gop("s", lambda e: e.copy(out=nwbc[:], in_=pBk[5][0:128, 0:128]),
           reads=[gB[5]], writes=[G("nwbc")])
    em.gop("t", lambda e: e.matmul(pBk[5][0:128, 128:128 + NE], ones1f[:], ebrow[:],
                                   start=True, stop=True),
           reads=[G("ones1f"), G("ebrow")], writes=[gB[5]])
    em.gop("s", lambda e: e.copy(out=ebbc[:], in_=pBk[5][0:128, 128:128 + NE]),
           reads=[gB[5]], writes=[G("ebbc")])
    em.gop("s", lambda e: e.activation(out=negea_sb[:], in_=alog_sb[:], func=AF.Exp),
           reads=[G("alog_sb")], writes=[G("negea_sb")])
    em.gop("v", lambda e: e.tensor_scalar(out=negea_sb[:], in0=negea_sb[:], scalar1=-1.0,
                                          scalar2=None, op0=ALU.mult),
           reads=[], writes=[G("negea_sb")])
    em.gop("v", lambda e: e.memset(onesHM[:], 1.0), writes=[G("onesHM")])
    em.gop("v", lambda e: e.memset(st_bf[:], 0.0), writes=[G("st_bf")])
    em.gop("v", lambda e: e.memset(mixstash[:], 0.0), writes=[G("mixstash")])
    em.barrier()

    def phase0():
        xa = [arN4[:, 0:2048], arN4[:, 2048:4096]]
        gXa = [G("p0x0"), G("p0x1")]
        xc = arN4[:, 4096:6144]
        xn = arN4[:, 6144:8192]
        gXc, gXn = G("p0xc"), G("p0xn")
        for tt in range(TTL):
            xab, gx = xa[tt % 2], gXa[tt % 2]
            dma_y(lambda e, d=xab, s=x_d[tt * 128:(tt + 1) * 128, :]: e.dma_start(out=d, in_=s),
                  f"dxp0{tt % 2}", writes=[gx])
            em.gop("v", lambda e, s=xab: e.reduce_sum(out=colsc[:, 0:1], in_=s, axis=AX.X),
                   reads=[gx], writes=[G("c0")])
            em.gop("v", lambda e: e.tensor_scalar(out=colsc[:, 1:2], in0=colsc[:, 0:1],
                                                  scalar1=1.0 / D, scalar2=None, op0=ALU.mult),
                   reads=[G("c0")], writes=[G("c1")])
            em.gop("v", lambda e, s=xab: e.tensor_scalar(out=xc, in0=s, scalar1=colsc[:, 1:2],
                                                         scalar2=None, op0=ALU.subtract),
                   reads=[gx, G("c1")], writes=[gXc])
            em.gop("v", lambda e: e.scalar_tensor_tensor(out=xn, in0=xc, scalar=1.0, in1=xc,
                                                         op0=ALU.mult, op1=ALU.mult,
                                                         accum_out=colsc[:, 2:3]),
                   reads=[gXc], writes=[gXn, G("c2")])
            em.gop("s", lambda e: e.activation(out=colsc[:, 3:4], in_=colsc[:, 2:3], func=AF.Ln,
                                               scale=1.0 / D, bias=LN_EPS),
                   reads=[G("c2")], writes=[G("c3")])
            em.gop("s", lambda e: e.activation(out=colsc[:, 4:5], in_=colsc[:, 3:4], func=AF.Exp,
                                               scale=-0.5),
                   reads=[G("c3")], writes=[G("c4")])
            em.gop("v", lambda e: e.tensor_scalar(out=xn, in0=xc, scalar1=colsc[:, 4:5],
                                                  scalar2=None, op0=ALU.mult),
                   reads=[gXc, G("c4")], writes=[gXn])
            for k in range(16):
                stf = xc[:, 128 * (k % 2):128 * (k % 2 + 1)]
                stb = arN5[:, 128 * (k % 2):128 * (k % 2 + 1)]
                gstf, gstb = G(f"p0sf{k % 2}"), G(f"p0sb{k % 2}")
                em.gop("t", lambda e, kk=k: e.transpose(pBk[0][0:128, 0:128],
                                                        xn[:, kk * 128:(kk + 1) * 128], identf[:]),
                       reads=[gXn, G("identf")], writes=[gB[0]])
                em.gop("s", lambda e, d=stf: e.copy(out=d, in_=pBk[0][0:128, 0:128]),
                       reads=[gB[0]], writes=[gstf, gXc])
                em.gop("t", lambda e, d=stf, kk=k: e.matmul(
                    pBk[1][0:128, 0:NE], d, gatew_sb[:, kk * NE:(kk + 1) * NE],
                    start=(kk == 0), stop=(kk == 15)),
                    reads=[gstf, G("gatew_sb")], writes=[gB[1]])
                em.gop("v", lambda e, d=stb, s2=stf: e.tensor_copy(out=d, in_=s2),
                       reads=[gstf], writes=[gstb])
                dma_y(lambda e, s2=stb, d=xnT_d[k, :, tt * 128:(tt + 1) * 128]:
                      e.dma_start(out=d, in_=s2), f"dxst{k % 2}", reads=[gstb])
            gl = gsm[:, 0:NE]; e0 = gsm[:, NE:2 * NE]; t1 = gsm[:, 2 * NE:3 * NE]
            sc = gsm[:, 3 * NE:4 * NE]; bsd = gsm[:, 4 * NE:5 * NE]
            eqv = gsm[:, 5 * NE:6 * NE]; cnd = gsm[:, 6 * NE:7 * NE]
            msk1 = gsm[:, 7 * NE:8 * NE]; msk2 = gsm[:, 8 * NE:9 * NE]
            l0 = gsm[:, 9 * NE:10 * NE]
            gg = G("gsm")
            em.gop("v", lambda e: e.tensor_copy(out=gl, in_=pBk[1][0:128, 0:NE]),
                   reads=[gB[1]], writes=[gg])
            em.gop("v", lambda e: e.tensor_reduce(out=colsc[:, 5:6], in_=gl, axis=AX.X,
                                                  op=ALU.max, negate=True),
                   reads=[gg], writes=[G("c5")])
            em.gop("s", lambda e: e.activation(out=e0, in_=gl, func=AF.Exp, bias=colsc[:, 5:6]),
                   reads=[gg, G("c5")], writes=[gg])
            em.gop("s", lambda e: e.activation(out=l0, in_=e0, func=AF.Ln),
                   reads=[gg], writes=[gg])
            em.gop("v", lambda e: e.tensor_scalar(out=t1, in0=gl, scalar1=colsc[:, 5:6],
                                                  scalar2=None, op0=ALU.add),
                   reads=[gg, G("c5")], writes=[gg])
            em.gop("v", lambda e: e.tensor_tensor(out=t1, in0=t1, in1=l0, op=ALU.subtract),
                   reads=[gg], writes=[gg])
            em.gop("v", lambda e: e.tensor_scalar(out=t1, in0=t1, scalar1=1.0, scalar2=None,
                                                  op0=ALU.add), reads=[gg], writes=[gg])
            em.gop("v", lambda e: e.tensor_tensor(out=e0, in0=e0, in1=t1, op=ALU.mult),
                   reads=[gg], writes=[gg])
            em.gop("v", lambda e: e.reduce_sum(out=colsc[:, 6:7], in_=e0, axis=AX.X),
                   reads=[gg], writes=[G("c6")])
            em.gop("v", lambda e: e.reciprocal(out=colsc[:, 7:8], in_=colsc[:, 6:7]),
                   reads=[G("c6")], writes=[G("c7")])
            em.gop("v", lambda e: e.tensor_scalar(out=sc, in0=e0, scalar1=colsc[:, 7:8],
                                                  scalar2=None, op0=ALU.mult),
                   reads=[gg, G("c7")], writes=[gg])
            em.gop("v", lambda e: e.tensor_tensor(out=bsd, in0=sc, in1=ebbc[:, 0:NE], op=ALU.add),
                   reads=[gg, G("ebbc")], writes=[gg])
            for r, (mskT, c_a, c_b) in enumerate(((msk1, 8, 9), (msk2, 10, 11))):
                em.gop("v", lambda e: e.tensor_reduce(out=colsc[:, 12:13], in_=bsd, axis=AX.X,
                                                      op=ALU.max),
                       reads=[gg], writes=[G("c12")])
                em.gop("v", lambda e: e.tensor_scalar(out=eqv, in0=bsd, scalar1=colsc[:, 12:13],
                                                      scalar2=None, op0=ALU.is_ge),
                       reads=[gg, G("c12")], writes=[gg])
                em.gop("v", lambda e: e.scalar_tensor_tensor(out=cnd, in0=eqv, scalar=-1000.0,
                                                             in1=iotaE[:, 0:NE], op0=ALU.mult,
                                                             op1=ALU.add),
                       reads=[gg, G("iotaE")], writes=[gg])
                em.gop("v", lambda e, ca=c_a: e.tensor_reduce(out=colsc[:, ca:ca + 1], in_=cnd,
                                                              axis=AX.X, op=ALU.min),
                       reads=[gg], writes=[G("c8")])
                em.gop("v", lambda e, ca=c_a, cb=c_b: e.tensor_scalar(
                    out=colsc[:, cb:cb + 1], in0=colsc[:, ca:ca + 1], scalar1=1000.0,
                    scalar2=None, op0=ALU.add), reads=[G("c8")], writes=[G("c9")])
                em.gop("v", lambda e, m=mskT, cb=c_b: e.tensor_scalar(
                    out=m, in0=iotaE[:, 0:NE], scalar1=colsc[:, cb:cb + 1], scalar2=None,
                    op0=ALU.is_equal), reads=[G("iotaE"), G("c9")], writes=[gg])
                if r == 0:
                    em.gop("v", lambda e: e.scalar_tensor_tensor(out=bsd, in0=msk1, scalar=-1e30,
                                                                 in1=bsd, op0=ALU.mult,
                                                                 op1=ALU.add),
                           reads=[gg], writes=[gg])
            em.gop("v", lambda e: e.tensor_tensor(out=msk1, in0=msk1, in1=msk2, op=ALU.add),
                   reads=[gg], writes=[gg])
            em.gop("v", lambda e, tt_=tt: e.tensor_tensor(out=wexp[:, tt_ * NE:(tt_ + 1) * NE],
                                                          in0=sc, in1=msk1, op=ALU.mult),
                   reads=[gg], writes=[G("wexp")])

    def xt_load(dst, gdst, c0, c1, nseg=4, extra_reads=()):
        w = c1 - c0
        per = 16 // nseg
        for i in range(nseg):
            dma_y(lambda e, i_=i, d=dst, w_=w, p_=per: e.dma_start(
                out=d[:, i_ * p_ * w_:(i_ + 1) * p_ * w_].rearrange("p (k c) -> p k c", k=p_),
                in_=xnT_d[i_ * p_:(i_ + 1) * p_, :, c0:c1].rearrange("k p c -> p k c")),
                "dxt", reads=list(extra_reads), writes=[gdst])

    def wblock_load(dst_ap, gdst, src2d, rk, c0, c1, sem, nseg=2):
        w = c1 - c0
        per = (rk + nseg - 1) // nseg
        gds = list(gdst) if isinstance(gdst, (list, tuple)) else [gdst]
        i = 0
        while i * per < rk:
            k0, k1 = i * per, min((i + 1) * per, rk)
            dma_g(lambda e, k0_=k0, k1_=k1, d=dst_ap, s=src2d, w_=w: e.dma_start(
                out=d[:, k0_ * w_:k1_ * w_].rearrange("p (k c) -> p k c", k=k1_ - k0_),
                in_=s[k0_ * 128:k1_ * 128, c0:c1].rearrange("(k p) c -> p k c", p=128)),
                sem, writes=gds)
            i += 1

    def phaseB():
        for tb in range(NFB):
            W = TBF
            c0 = tb * W
            xts = arN3[:, 0:16 * W]
            gx = G("pbxts")
            xt_load(xts, gx, c0, c0 + W)
            for (wsb, gnm, po) in ((wb_sb, "wb_sb", 0), (wa_sb, "wa_sb", 32)):
                for k in range(16):
                    em.gop("t", lambda e, kk=k, ws=wsb, po_=po, W_=W: e.matmul(
                        pBk[2][po_:po_ + 32, 0:W_], ws[:, kk * HV:kk * HV + HV],
                        xts[:, kk * W_:(kk + 1) * W_], start=(kk == 0), stop=(kk == 15)),
                        reads=[gx, G(gnm)], writes=[gB[2]])
            em.gop("s", lambda e, W_=W: e.activation(out=hmA[:, 0:W_], in_=pBk[2][0:32, 0:W_],
                                                     func=AF.Sigmoid),
                   reads=[gB[2]], writes=[G("hmA")])
            em.gop("s", lambda e, W_=W: e.activation(out=hmB[:, 0:W_], in_=pBk[2][32:64, 0:W_],
                                                     func=AF.Exp, bias=dtb_sb[0:32, :]),
                   reads=[gB[2], G("dtb_sb")], writes=[G("hmB")])
            em.gop("s", lambda e, W_=W: e.activation(out=hmC[:, 0:W_], in_=hmB[:, 0:W_],
                                                     func=AF.Ln, bias=1.0),
                   reads=[G("hmB")], writes=[G("hmC")])
            em.gop("v", lambda e, W_=W: e.tensor_scalar(out=hmB[:, 0:W_], in0=hmC[:, 0:W_],
                                                        scalar1=negea_sb[0:32, :], scalar2=None,
                                                        op0=ALU.mult),
                   reads=[G("hmC"), G("negea_sb")], writes=[G("hmB")])
            ncpb = W // CH
            for c in range(ncpb):
                em.gop("v", lambda e, cc=c: e.tensor_tensor_scan(
                    out=hmC[:, cc * CH:(cc + 1) * CH], data0=onesHM[:, 0:CH],
                    data1=hmB[:, cc * CH:(cc + 1) * CH], initial=0.0,
                    op0=ALU.mult, op1=ALU.add),
                    reads=[G("hmB"), G("onesHM")], writes=[G("hmC")])
            for c in range(ncpb):
                em.gop("v", lambda e, cc=c: e.tensor_scalar(
                    out=hmB[:, cc * CH:(cc + 1) * CH], in0=hmC[:, cc * CH:(cc + 1) * CH],
                    scalar1=hmC[:, cc * CH + CH - 1:cc * CH + CH], scalar2=-1.0,
                    op0=ALU.subtract, op1=ALU.mult),
                    reads=[G("hmC")], writes=[G("hmB")])
            em.gop("s", lambda e, W_=W: e.activation(out=hmB[:, 0:W_], in_=hmB[:, 0:W_],
                                                     func=AF.Exp),
                   reads=[G("hmB")], writes=[G("hmB")])
            for c in range(ncpb):
                tt = tb * ncpb + c
                cs = slice(tt * HV, (tt + 1) * HV)
                for (src, gs, dst, gd, ofs) in (
                        (hmA, "hmA", bt, "bt", 0),
                        (hmC, "hmC", gjt, "gjt", 128),
                        (hmB, "hmB", eglmt, "eglmt", 256)):
                    em.gop("t", lambda e, s2=src, cc=c, o=ofs: e.transpose(
                        pBk[0][0:128, o:o + HV], s2[:, cc * CH:(cc + 1) * CH],
                        identf[0:HV, 0:HV]),
                        reads=[G(gs), G("identf")], writes=[gB[0]])
                    em.gop("s", lambda e, d=dst, cs_=cs, o=ofs: e.copy(
                        out=d[:, cs_], in_=pBk[0][0:128, o:o + HV]),
                        reads=[gB[0]], writes=[G(gd)])
                em.gop("v", lambda e, cs_=cs: e.tensor_scalar(out=negbt[:, cs_], in0=bt[:, cs_],
                                                              scalar1=-1.0, scalar2=None,
                                                              op0=ALU.mult),
                       reads=[G("bt")], writes=[G("negbt")])
                em.gop("s", lambda e, cs_=cs: e.activation(out=egjt[:, cs_], in_=gjt[:, cs_],
                                                           func=AF.Exp),
                       reads=[G("gjt")], writes=[G("egjt")])
                em.gop("v", lambda e, cs_=cs: e.tensor_tensor(out=begjt[:, cs_], in0=bt[:, cs_],
                                                              in1=egjt[:, cs_], op=ALU.mult),
                       reads=[G("bt"), G("egjt")], writes=[G("begjt")])

    def phaseF():
        W = TBF
        NT = W // 128
        assert 16 * W <= 8192
        g_acc = arN4[:, 0:16 * W]
        gGacc = G("gacc")
        xts = arN3[:, 0:16 * W]
        gx = G("pfxts")
        Xe = arN1[:, 0:16 * W]
        hbuf = arN1[:, 8192:8192 + 14 * W]
        h1a = arN1[:, 8192 + 14 * W:8192 + 15 * W]
        gXe, gH, gH1 = G("Xe"), G("hbuf"), G("h1a")
        wbc = arN5[:, 0:W]
        gtokst = arN5[:, 2048:4096]
        gWbc, gGt = G("wbc"), G("gtokst")
        wA = [arN2[:, 0:4096], arN2[:, 4096:8192]]
        wB = [arN2[:, 8192:12288], arN2[:, 12288:16384]]
        w2bl = [arN2[:, 0:7168], arN2[:, 8192:8192 + 7168]]
        gwA = [G("wA0"), G("wA1")]
        gwB = [G("wB0"), G("wB1")]
        gw2 = [G("w20"), G("w21")]
        NFG = HID // 256

        for tb in range(NFB):
            c0 = tb * W
            xt_load(xts, gx, c0, c0 + W)
            em.gop("v", lambda e, d=g_acc: e.memset(d, 0.0), reads=[], writes=[gGacc])
            for ei in range(-1, NE):
                shared = ei < 0
                mw1, mw3, mw2 = (sw1_d, sw3_d, sw2_d) if shared else (w1_d[ei], w3_d[ei], w2_d[ei])
                if not shared:
                    for i in range(NT):
                        tt = tb * NT + i
                        em.gop("v", lambda e, tt_=tt, ei_=ei: e.tensor_scalar(
                            out=c_cbc[:], in0=masks[:, 128 * M_ONES:128 * (M_ONES + 1)],
                            scalar1=wexp[:, tt_ * NE + ei_:tt_ * NE + ei_ + 1],
                            scalar2=None, op0=ALU.mult),
                            reads=[G("masks"), G("wexp")], writes=[G("c_cbc")])
                        em.gop("t", lambda e: e.transpose(pBk[5][0:128, 0:128], c_cbc[:],
                                                          identf[:]),
                               reads=[G("c_cbc"), G("identf")], writes=[gB[5]])
                        em.gop("s", lambda e, i_=i: e.copy(out=wbc[:, i_ * 128:(i_ + 1) * 128],
                                                           in_=pBk[5][0:128, 0:128]),
                               reads=[gB[5]], writes=[gWbc])
                    for k in range(16):
                        em.gop("v", lambda e, kk=k, W_=W: e.tensor_tensor(
                            out=Xe[:, kk * W_:(kk + 1) * W_], in0=xts[:, kk * W_:(kk + 1) * W_],
                            in1=wbc[:, 0:W_], op=ALU.mult),
                            reads=[gx, gWbc], writes=[gXe])
                    xin, gxin = Xe, gXe
                else:
                    xin, gxin = xts, gx
                for fg in range(NFG):
                    bsl = fg % 2
                    wblock_load(wA[bsl], gwA[bsl], mw1, 16, fg * 256, (fg + 1) * 256, f"dwW{bsl}")
                    wblock_load(wB[bsl], gwB[bsl], mw3, 16, fg * 256, (fg + 1) * 256, f"dwB{bsl}")
                    for fi in range(2):
                        f = fg * 2 + fi
                        pb1, gp1 = (pBk[2], gB[2]) if fi == 0 else (pBk[4], gB[4])
                        pb3, gp3 = (pBk[3], gB[3]) if fi == 0 else (pBk[6], gB[6])
                        for k in range(16):
                            em.gop("t", lambda e, kk=k, fi_=fi, b=wA[bsl], p=pb1, xi=xin,
                                   W_=W: e.matmul(
                                p[0:128, 0:W_],
                                b[:, kk * 256 + fi_ * 128:kk * 256 + (fi_ + 1) * 128],
                                xi[:, kk * W_:(kk + 1) * W_], start=(kk == 0), stop=(kk == 15)),
                                reads=[gwA[bsl], gxin], writes=[gp1])
                        for k in range(16):
                            em.gop("t", lambda e, kk=k, fi_=fi, b=wB[bsl], p=pb3, xi=xin,
                                   W_=W: e.matmul(
                                p[0:128, 0:W_],
                                b[:, kk * 256 + fi_ * 128:kk * 256 + (fi_ + 1) * 128],
                                xi[:, kk * W_:(kk + 1) * W_], start=(kk == 0), stop=(kk == 15)),
                                reads=[gwB[bsl], gxin], writes=[gp3])
                        em.gop("s", lambda e, p=pb1, W_=W: e.activation(
                            out=h1a, in_=p[0:128, 0:W_], func=AF.Silu),
                            reads=[gp1], writes=[gH1])
                        em.gop("v", lambda e, p=pb3, f_=f, W_=W: e.tensor_tensor(
                            out=hbuf[:, f_ * W_:(f_ + 1) * W_], in0=p[0:128, 0:W_], in1=h1a,
                            op=ALU.mult),
                            reads=[gp3, gH1], writes=[gH])
                for cb in range(4):
                    bsl = cb % 2
                    w2g = [gwA[0], gwA[1]] if bsl == 0 else [gwB[0], gwB[1]]
                    wblock_load(w2bl[bsl], w2g, mw2, 14, cb * 512, (cb + 1) * 512, f"dw2{bsl}")
                    for fi in range(4):
                        f2 = cb * 4 + fi
                        pbo, gpo = (pBk[2], gB[2]) if fi % 2 == 0 else (pBk[3], gB[3])
                        for k in range(14):
                            em.gop("t", lambda e, kk=k, fi_=fi, b=w2bl[bsl], p=pbo,
                                   W_=W: e.matmul(
                                p[0:128, 0:W_],
                                b[:, kk * 512 + fi_ * 128:kk * 512 + (fi_ + 1) * 128],
                                hbuf[:, kk * W_:(kk + 1) * W_], start=(kk == 0), stop=(kk == 13)),
                                reads=w2g + [gH], writes=[gpo])
                        em.gop("v", lambda e, p=pbo, f2_=f2, W_=W: e.tensor_tensor(
                            out=g_acc[:, f2_ * W_:(f2_ + 1) * W_],
                            in0=g_acc[:, f2_ * W_:(f2_ + 1) * W_],
                            in1=p[0:128, 0:W_], op=ALU.add),
                            reads=[gpo], writes=[gGacc])
            for i in range(NT):
                for kf in range(16):
                    em.gop("t", lambda e, i_=i, kf_=kf, W_=W: e.transpose(
                        pBk[5][0:128, 0:128],
                        g_acc[:, kf_ * W_ + i_ * 128:kf_ * W_ + (i_ + 1) * 128], identf[:]),
                        reads=[gGacc, G("identf")], writes=[gB[5]])
                    em.gop("s", lambda e, kf_=kf: e.activation(
                        out=gtokst[:, kf_ * 128:(kf_ + 1) * 128], in_=pBk[5][0:128, 0:128],
                        func=AF.Silu),
                        reads=[gB[5]], writes=[gGt])
                dma_y(lambda e, i_=i, t0=c0: e.dma_start(
                    out=g_d[t0 + i_ * 128:t0 + (i_ + 1) * 128, :], in_=gtokst[:, 0:2048]),
                    "dxgd", reads=[gGt])

    def ladder_and_core(ab, c, hv):
        j = ab * CPAB + c
        tt = j
        kh = hv // 2
        qT = arN1[:, kh * AB + c * 128:kh * AB + (c + 1) * 128]
        kT = arN1[:, (16 + kh) * AB + c * 128:(16 + kh) * AB + (c + 1) * 128]
        vT = arN1[:, (32 + hv) * AB + c * 128:(32 + hv) * AB + (c + 1) * 128]
        gQKV = G("qkvT")
        col = lambda t_, name: t_[:, tt * HV + hv:tt * HV + hv + 1]
        sil = arN5[:, c * 4096 + hv * 128:c * 4096 + (hv + 1) * 128]
        gSil = G("siluz")
        sthv = st_bf[:, hv * 128:(hv + 1) * 128]
        gSt = G(f"st{hv}")
        cb0 = (hv % 2) * 4
        mk = lambda idx: masks[:, 128 * idx:128 * (idx + 1)]

        em.gop("v", lambda e: e.tensor_scalar(out=c_cbc[:], in0=mk(M_ONES),
                                              scalar1=col(gjt, "gjt"), scalar2=None,
                                              op0=ALU.mult),
               reads=[G("masks"), G("gjt")], writes=[G("c_cbc")])
        em.gop("t", lambda e: e.transpose(pBk[0][0:128, 0:128], c_cbc[:], identf[:]),
               reads=[G("c_cbc"), G("identf")], writes=[gB[0]])
        if j > 0:
            em.gop("s", lambda e, cb=cb0: e.activation(out=c_cols[:, cb:cb + 1], in_=pBk[0][0:128, 127:128],
                                               func=AF.Exp),
                   reads=[gB[0]], writes=[G(f"c_eglast{hv % 2}")])
        em.gop("v", lambda e: e.tensor_scalar(out=c_e1[:], in0=pBk[0][0:128, 0:128],
                                              scalar1=col(gjt, "gjt"), scalar2=0.0,
                                              op0=ALU.subtract, op1=ALU.max),
               reads=[gB[0], G("gjt")], writes=[G("c_e1")])
        em.gop("v", lambda e: e.tensor_scalar(out=c_e2[:], in0=pBk[0][0:128, 0:128],
                                              scalar1=col(gjt, "gjt"), scalar2=0.0,
                                              op0=ALU.subtract, op1=ALU.min),
               reads=[gB[0], G("gjt")], writes=[G("c_e2")])
        em.gop("s", lambda e: e.activation(out=c_dcy[:], in_=c_e1[:], func=AF.Exp, scale=-1.0),
               reads=[G("c_e1")], writes=[G("c_dcy")])
        em.gop("s", lambda e: e.activation(out=c_dcyT[:], in_=c_e2[:], func=AF.Exp),
               reads=[G("c_e2")], writes=[G("c_dcyT")])
        if CORE_STEPS < 2:
            return
        em.gop("t", lambda e: e.matmul(pBk[1][0:128, 0:128], kT, kT, start=True, stop=True),
               reads=[gQKV], writes=[gB[1]])
        em.gop("t", lambda e: e.matmul(pBk[1][0:128, 128:256], kT, qT, start=True, stop=True),
               reads=[gQKV], writes=[gB[1]])
        em.gop("v", lambda e: e.scalar_tensor_tensor(out=c_scrap[:], in0=pBk[1][0:128, 0:128],
                                                     scalar=col(negbt, "negbt"), in1=c_dcy[:],
                                                     op0=ALU.mult, op1=ALU.mult),
               reads=[gB[1], G("negbt"), G("c_dcy")], writes=[G("c_scrap")])
        em.gop("v", lambda e: e.tensor_tensor(out=c_A[:], in0=c_scrap[:], in1=mk(M_STRICT),
                                              op=ALU.mult),
               reads=[G("c_scrap"), G("masks")], writes=[G("c_A")])
        em.gop("v", lambda e: e.tensor_tensor(out=c_scrap[:], in0=pBk[1][0:128, 128:256],
                                              in1=c_dcyT[:], op=ALU.mult),
               reads=[gB[1], G("c_dcyT")], writes=[G("c_scrap")])
        em.gop("v", lambda e: e.tensor_tensor(out=c_intraT[:], in0=c_scrap[:], in1=mk(M_TRIU),
                                              op=ALU.mult),
               reads=[G("c_scrap"), G("masks")], writes=[G("c_intraT")])
        em.gop("t", lambda e: e.transpose(pBF[0:128, 0:128], c_A[:], identb[:]),
               reads=[G("c_A"), G("identb")], writes=[gBF])
        em.gop("s", lambda e: e.copy(out=c_AT[:], in_=pBF[0:128, 0:128]),
               reads=[gBF], writes=[G("c_AT")])
        if CORE_STEPS < 3:
            return
        em.gop("t", lambda e: e.transpose(pBF[0:128, 128:256], kT, identb[:]),
               reads=[gQKV, G("identb")], writes=[gBF])
        em.gop("t", lambda e: e.transpose(pBF[0:128, 256:384], vT, identb[:]),
               reads=[gQKV, G("identb")], writes=[gBF])
        em.gop("v", lambda e: e.tensor_scalar(out=c_X0[:, 0:128], in0=pBF[0:128, 256:384],
                                              scalar1=col(bt, "bt"), scalar2=None, op0=ALU.mult),
               reads=[gBF, G("bt")], writes=[G("c_X0")])
        em.gop("v", lambda e: e.tensor_scalar(out=c_X0[:, 128:256], in0=pBF[0:128, 128:256],
                                              scalar1=col(begjt, "begjt"), scalar2=None,
                                              op0=ALU.mult),
               reads=[gBF, G("begjt")], writes=[G("c_X0")])
        em.gop("v", lambda e: e.tensor_scalar(out=c_kdec[:], in0=pBF[0:128, 128:256],
                                              scalar1=col(eglmt, "eglmt"), scalar2=None,
                                              op0=ALU.mult),
               reads=[gBF, G("eglmt")], writes=[G("c_kdec")])
        if CORE_STEPS < 4:
            return
        TB = [c_T0, c_T1]
        gTB = [G("c_T0"), G("c_T1")]
        em.gop("v", lambda e: e.tensor_tensor(out=c_scrap[:], in0=c_A[:], in1=mk(M_OD1),
                                              op=ALU.mult),
               reads=[G("c_A"), G("masks")], writes=[G("c_scrap")])
        em.gop("v", lambda e: e.tensor_tensor(out=TB[0][:, 0:128], in0=c_scrap[:], in1=identf[:],
                                              op=ALU.add),
               reads=[G("c_scrap"), G("identf")], writes=[gTB[0]])
        em.gop("v", lambda e: e.tensor_tensor(out=c_scrap[:], in0=c_AT[:], in1=mk(M_ODT[1]),
                                              op=ALU.mult),
               reads=[G("c_AT"), G("masks")], writes=[G("c_scrap")])
        em.gop("v", lambda e: e.tensor_tensor(out=TB[0][:, 128:256], in0=c_scrap[:], in1=identf[:],
                                              op=ALU.add),
               reads=[G("c_scrap"), G("identf")], writes=[gTB[0]])
        cur = 0
        for s in (2, 4, 8, 16, 32, 64):
            nxt = 1 - cur
            Tc, gTc = TB[cur], gTB[cur]
            Tn, gTn = TB[nxt], gTB[nxt]
            em.gop("v", lambda e, s_=s: e.tensor_tensor(out=c_aod[:], in0=c_AT[:],
                                                        in1=mk(M_ODT[s_]), op=ALU.mult),
                   reads=[G("c_AT"), G("masks")], writes=[G("c_aod")])
            em.gop("t", lambda e, Tc_=Tc: e.matmul(pBk[2][0:128, 0:128], c_aod[:], Tc_[:, 0:128],
                                                   start=True, stop=True),
                   reads=[G("c_aod"), gTc], writes=[gB[2]])
            em.gop("s", lambda e: e.copy(out=c_W[:], in_=pBk[2][0:128, 0:128]),
                   reads=[gB[2]], writes=[G("c_W")])
            em.gop("t", lambda e, Tc_=Tc: e.matmul(pBk[3][0:128, 0:128], Tc_[:, 128:256], c_W[:],
                                                   start=True, stop=False),
                   reads=[G("c_W"), gTc], writes=[gB[3]])
            em.gop("t", lambda e, Tc_=Tc: e.matmul(pBk[3][0:128, 0:128], identb[:], Tc_[:, 0:128],
                                                   start=False, stop=True),
                   reads=[gTc, G("identb")], writes=[gB[3]])
            em.gop("t", lambda e, Tc_=Tc: e.matmul(pBk[3][0:128, 128:256], c_W[:], Tc_[:, 128:256],
                                                   start=True, stop=False),
                   reads=[G("c_W"), gTc], writes=[gB[3]])
            em.gop("t", lambda e, Tc_=Tc: e.matmul(pBk[3][0:128, 128:256], identb[:],
                                                   Tc_[:, 128:256], start=False, stop=True),
                   reads=[gTc, G("identb")], writes=[gB[3]])
            em.gop("v", lambda e, Tn_=Tn: e.tensor_copy(out=Tn_[:, 0:256],
                                                        in_=pBk[3][0:128, 0:256]),
                   reads=[gB[3]], writes=[gTn])
            cur = nxt
        if CORE_STEPS < 5:
            return
        Tfin, gTfin = TB[cur], gTB[cur]
        em.gop("t", lambda e, Tf=Tfin: e.matmul(pBk[3][0:128, 256:512], Tf[:, 128:256], c_X0[:],
                                                start=True, stop=True),
               reads=[gTfin, G("c_X0")], writes=[gB[3]])
        em.gop("v", lambda e: e.tensor_copy(out=c_Xsb[:], in_=pBk[3][0:128, 256:512]),
               reads=[gB[3]], writes=[G("c_Xsb")])
        if CORE_STEPS < 6:
            return
        em.gop("t", lambda e: e.transpose(pBF[0:128, 384:512], c_Xsb[:, 128:256], identb[:]),
               reads=[G("c_Xsb"), G("identb")], writes=[gBF])
        em.gop("s", lambda e: e.copy(out=c_scrap[:], in_=pBF[0:128, 384:512]),
               reads=[gBF], writes=[G("c_scrap")])
        if j > 0:
            em.gop("t", lambda e: e.matmul(pBk[4][0:128, 0:128], c_scrap[:], sthv,
                                           start=True, stop=True),
                   reads=[G("c_scrap"), gSt], writes=[gB[4]])
            em.gop("t", lambda e: e.matmul(pBk[4][0:128, 128:256], qT, sthv,
                                           start=True, stop=True),
                   reads=[gQKV, gSt], writes=[gB[4]])
            em.gop("v", lambda e: e.tensor_tensor(out=c_vnew[:], in0=c_Xsb[:, 0:128],
                                                  in1=pBk[4][0:128, 0:128], op=ALU.subtract),
                   reads=[G("c_Xsb"), gB[4]], writes=[G("c_vnew")])
            em.gop("v", lambda e: e.tensor_scalar(out=c_e1[:], in0=pBk[4][0:128, 128:256],
                                                  scalar1=col(egjt, "egjt"), scalar2=None,
                                                  op0=ALU.mult),
                   reads=[G("egjt"), gB[4]], writes=[G("c_e1")])
            em.gop("t", lambda e: e.matmul(pBk[4][0:128, 384:512], c_intraT[:], c_vnew[:],
                                           start=True, stop=True),
                   reads=[G("c_intraT"), G("c_vnew")], writes=[gB[4]])
            em.gop("v", lambda e: e.scalar_tensor_tensor(out=c_xg[:], in0=pBk[4][0:128, 384:512],
                                                         scalar=1.0, in1=c_e1[:],
                                                         op0=ALU.mult, op1=ALU.add),
                   reads=[gB[4], G("c_e1")], writes=[G("c_xg")])
        else:
            em.gop("v", lambda e: e.tensor_copy(out=c_vnew[:], in_=c_Xsb[:, 0:128]),
                   reads=[G("c_Xsb")], writes=[G("c_vnew")])
            em.gop("t", lambda e: e.matmul(pBk[4][0:128, 384:512], c_intraT[:], c_vnew[:],
                                           start=True, stop=True),
                   reads=[G("c_intraT"), G("c_vnew")], writes=[gB[4]])
            em.gop("v", lambda e: e.tensor_copy(out=c_xg[:], in_=pBk[4][0:128, 384:512]),
                   reads=[gB[4]], writes=[G("c_xg")])
        em.gop("v", lambda e: e.tensor_tensor(out=c_xg[:], in0=c_xg[:], in1=sil,
                                              op=ALU.mult),
               reads=[gSil], writes=[G("c_xg")])
        em.gop("v", lambda e: e.scalar_tensor_tensor(out=c_scrap[:], in0=c_xg[:], scalar=1.0,
                                                     in1=c_xg[:], op0=ALU.mult, op1=ALU.mult,
                                                     accum_out=c_cols[:, cb0 + 1:cb0 + 2]),
               reads=[G("c_xg")], writes=[G("c_scrap"), G(f"c_ssq{hv % 2}")])
        if CORE_STEPS < 7:
            return
        em.gop("s", lambda e, cb=cb0: e.activation(out=c_cols[:, cb + 2:cb + 3],
                                           in_=c_cols[:, cb + 1:cb + 2], func=AF.Ln,
                                           scale=1.0 / 128.0, bias=RMS_EPS),
               reads=[G(f"c_ssq{hv % 2}")], writes=[G(f"c_lnssq{hv % 2}")])
        em.gop("s", lambda e, cb=cb0: e.activation(out=c_cols[:, cb + 3:cb + 4],
                                           in_=c_cols[:, cb + 2:cb + 3], func=AF.Exp,
                                           scale=-0.5),
               reads=[G(f"c_lnssq{hv % 2}")], writes=[G(f"c_rstd{hv % 2}")])
        em.gop("v", lambda e, cb=cb0: e.scalar_tensor_tensor(out=c_ntok[:], in0=c_xg[:],
                                                     scalar=c_cols[:, cb + 3:cb + 4], in1=nwbc[:],
                                                     op0=ALU.mult, op1=ALU.mult),
               reads=[G("c_xg"), G(f"c_rstd{hv % 2}"), G("nwbc")], writes=[G("c_ntok")])
        em.gop("t", lambda e: e.transpose(pBF[0:128, 512:640], c_ntok[:], identb[:]),
               reads=[G("c_ntok"), G("identb")], writes=[gBF])
        em.gop("s", lambda e, d=arN5[:, 8192 + hv * AB + c * 128:8192 + hv * AB + (c + 1) * 128]:
               e.copy(out=d, in_=pBF[0:128, 512:640]),
               reads=[gBF], writes=[G("normedT")])
        if CORE_STEPS < 8:
            return
        em.gop("t", lambda e: e.matmul(pBk[4][0:128, 256:384], c_kdec[:], c_vnew[:],
                                       start=True, stop=True),
               reads=[G("c_kdec"), G("c_vnew")], writes=[gB[4]])
        if ST8 == "mm":
            return
        if j > 0:
            if ST8 == "copy":
                em.gop("v", lambda e: e.tensor_copy(out=sthv, in_=pBk[4][0:128, 256:384]),
                       reads=[gB[4]], writes=[gSt])
            elif ST8 == "sttconst":
                em.gop("v", lambda e: e.scalar_tensor_tensor(out=sthv, in0=sthv,
                                                             scalar=0.5,
                                                             in1=pBk[4][0:128, 256:384],
                                                             op0=ALU.mult, op1=ALU.add),
                       reads=[gB[4]], writes=[gSt])
            elif ST8 == "sttecol":
                em.gop("v", lambda e: e.scalar_tensor_tensor(out=sthv, in0=sthv,
                                                             scalar=c_cols[:, 0:1],
                                                             in1=pBk[4][0:128, 256:384],
                                                             op0=ALU.mult, op1=ALU.add),
                       reads=[gB[4]], writes=[gSt])
            else:
                em.gop("v", lambda e, cb=cb0: e.scalar_tensor_tensor(out=sthv, in0=sthv,
                                                             scalar=c_cols[:, cb:cb + 1],
                                                             in1=pBk[4][0:128, 256:384],
                                                             op0=ALU.mult, op1=ALU.add),
                       reads=[gB[4], G(f"c_eglast{hv % 2}")], writes=[gSt])
        else:
            em.gop("v", lambda e: e.tensor_copy(out=sthv, in_=pBk[4][0:128, 256:384]),
                   reads=[gB[4]], writes=[gSt])

    def phaseA():
        xts = arN3[:, 0:16 * AB]
        gx = G("paxts")
        qkvT = arN1[:, 0:64 * AB]
        gQKV = G("qkvT")
        wA = [arN2[:, 0:8192], arN2[:, 8192:16384]]
        gwA = [G("pawA0"), G("pawA1")]
        mixsb = [arN4[:, 0:AB + 3], arN4[:, 512:512 + AB + 3]]
        gMix = [G("pamix0"), G("pamix1")]
        convacc = arN4[:, 1024:1024 + AB]
        gCv = G("paconv")
        ct0 = arN4[:, 1280:1280 + AB]
        ct1 = arN4[:, 1536:1536 + AB]
        gCt = G("pact")
        xsl = [arN4[:, 2048:4096], arN4[:, 4096:6144]]
        gXsl = [G("paxsl0"), G("paxsl1")]
        outst = [[arN4[:, 6144:6400], arN4[:, 6400:6656]],
                 [arN4[:, 7168:7424], arN4[:, 7424:7680]]]
        gOut = [[G("paout00"), G("paout01")], [G("paout10"), G("paout11")]]
        gbuf = [arN3[:, 4096:6144], arN3[:, 6144:8192]]
        gGbuf = [G("pag0"), G("pag1")]
        siluz = arN5[:, 0:2 * 4096]
        gSil = G("siluz")
        normedT = arN5[:, 8192:8192 + 32 * AB]
        gNT = G("normedT")

        for ab in range(NAB):
            t0 = ab * AB
            xt_load(xts, gx, t0, t0 + AB)
            for m in range(16 if "mix" in SUBA else 0):
                bsl = m % 2
                wblock_load(wA[bsl], gwA[bsl], wqkv_d, 16, m * 512, (m + 1) * 512, f"dwW{bsl}")
                for fl in range(4):
                    f = m * 4 + fl
                    pb, gp = (pBk[0], gB[0]) if fl % 2 == 0 else (pBk[1], gB[1])
                    for k in range(16):
                        em.gop("t", lambda e, kk=k, fl_=fl, b=wA[bsl], p=pb: e.matmul(
                            p[0:128, 0:AB],
                            b[:, kk * 512 + fl_ * 128:kk * 512 + (fl_ + 1) * 128],
                            xts[:, kk * AB:(kk + 1) * AB], start=(kk == 0), stop=(kk == 15)),
                            reads=[gwA[bsl], gx], writes=[gp])
                    ms, gms = mixsb[fl % 2], gMix[fl % 2]
                    em.gop("v", lambda e, p=pb, d=ms: e.tensor_copy(out=d[:, 3:3 + AB],
                                                                    in_=p[0:128, 0:AB]),
                           reads=[gp], writes=[gms])
                    em.gop("v", lambda e, d=ms, f_=f: e.tensor_copy(
                        out=d[:, 0:3], in_=mixstash[:, f_ * 3:(f_ + 1) * 3]),
                        reads=[G("mixstash")], writes=[gms])
                    em.gop("v", lambda e, d=ms, f_=f: e.tensor_copy(
                        out=mixstash[:, f_ * 3:(f_ + 1) * 3], in_=d[:, AB:AB + 3]),
                        reads=[gms], writes=[G("mixstash")])
                    em.gop("s", lambda e, d=ms, f_=f: e.activation(
                        out=ct0, in_=d[:, 0:AB], func=AF.Copy,
                        scale=cw_sb[:, f_ * KC + 0:f_ * KC + 1]),
                        reads=[gms, G("cw_sb")], writes=[gCt])
                    em.gop("s", lambda e, d=ms, f_=f: e.activation(
                        out=ct1, in_=d[:, 1:1 + AB], func=AF.Copy,
                        scale=cw_sb[:, f_ * KC + 1:f_ * KC + 2]),
                        reads=[gms, G("cw_sb")], writes=[gCt])
                    em.gop("v", lambda e: e.tensor_tensor(out=convacc, in0=ct0, in1=ct1,
                                                          op=ALU.add),
                           reads=[gCt], writes=[gCv])
                    em.gop("v", lambda e, d=ms, f_=f: e.scalar_tensor_tensor(
                        out=convacc, in0=d[:, 2:2 + AB],
                        scalar=cw_sb[:, f_ * KC + 2:f_ * KC + 3], in1=convacc,
                        op0=ALU.mult, op1=ALU.add),
                        reads=[gms, G("cw_sb")], writes=[gCv])
                    em.gop("v", lambda e, d=ms, f_=f: e.scalar_tensor_tensor(
                        out=convacc, in0=d[:, 3:3 + AB],
                        scalar=cw_sb[:, f_ * KC + 3:f_ * KC + 4], in1=convacc,
                        op0=ALU.mult, op1=ALU.add),
                        reads=[gms, G("cw_sb")], writes=[gCv])
                    em.gop("s", lambda e, f_=f: e.activation(
                        out=qkvT[:, f_ * AB:(f_ + 1) * AB], in_=convacc, func=AF.Silu),
                        reads=[gCv], writes=[gQKV])
            for f in range(32 if "l2n" in SUBA else 0):
                isq = f < 16
                em.gop("v", lambda e, f_=f: e.tensor_tensor(
                    out=sq_sb[:, 0:AB], in0=qkvT[:, f_ * AB:(f_ + 1) * AB],
                    in1=qkvT[:, f_ * AB:(f_ + 1) * AB], op=ALU.mult),
                    reads=[gQKV], writes=[G("sq_sb")])
                em.gop("t", lambda e: e.matmul(pBk[2][0:1, 0:AB], onescolb[:], sq_sb[:, 0:AB],
                                               start=True, stop=True),
                       reads=[G("sq_sb"), G("onescolb")], writes=[gB[2]])
                em.gop("s", lambda e: e.activation(out=rowsc[0:1, 0:AB], in_=pBk[2][0:1, 0:AB],
                                                   func=AF.Ln, bias=L2_EPS),
                       reads=[gB[2]], writes=[G("rowsc")])
                em.gop("s", lambda e, b_=(-0.5 * LN128 if isq else 0.0): e.activation(
                    out=rowsc[0:1, AB:2 * AB], in_=rowsc[0:1, 0:AB], func=AF.Exp,
                    scale=-0.5, bias=b_),
                    reads=[G("rowsc")], writes=[G("rowsc")])
                em.gop("t", lambda e: e.matmul(pBk[2][0:128, 256:256 + AB], ones1f[:],
                                               rowsc[0:1, AB:2 * AB], start=True, stop=True),
                       reads=[G("rowsc"), G("ones1f")], writes=[gB[2]])
                em.gop("v", lambda e, f_=f: e.tensor_tensor(
                    out=qkvT[:, f_ * AB:(f_ + 1) * AB], in0=qkvT[:, f_ * AB:(f_ + 1) * AB],
                    in1=pBk[2][0:128, 256:256 + AB], op=ALU.mult),
                    reads=[gB[2]], writes=[gQKV])
            for fb in range(8 if "z" in SUBA else 0):
                bsl = fb % 2
                wblock_load(wA[bsl], gwA[bsl], wz_d, 16, fb * 512, (fb + 1) * 512, f"dwW{bsl}")
                for c in range(CPAB):
                    pb, gp = (pBk[3], gB[3]) if c % 2 == 0 else (pBk[4], gB[4])
                    for k in range(16):
                        em.gop("t", lambda e, kk=k, b=wA[bsl], p=pb, c_=c: e.matmul(
                            p[0:128, 0:512], xts[:, kk * AB + c_ * 128:kk * AB + (c_ + 1) * 128],
                            b[:, kk * 512:(kk + 1) * 512], start=(kk == 0), stop=(kk == 15)),
                            reads=[gwA[bsl], gx], writes=[gp])
                    em.gop("s", lambda e, p=pb, c_=c, fb_=fb: e.activation(
                        out=siluz[:, c_ * 4096 + fb_ * 512:c_ * 4096 + (fb_ + 1) * 512],
                        in_=p[0:128, 0:512], func=AF.Silu),
                        reads=[gp], writes=[gSil])
            if "core" in SUBA:
                for c in range(CPAB):
                    for hv in range(NHV):
                        ladder_and_core(ab, c, hv)
            for c in range(CPAB):
                dma_y(lambda e, d=xsl[c], s=x_d[t0 + c * 128:t0 + (c + 1) * 128, :]:
                      e.dma_start(out=d, in_=s), f"dxl{c}", writes=[gXsl[c]])
                dma_y(lambda e, d=gbuf[c], s=g_d[t0 + c * 128:t0 + (c + 1) * 128, :]:
                      e.dma_start(out=d, in_=s), f"dxg{c}", writes=[gGbuf[c]])
            for blk in range(8 if "smm" in SUBA else 0):
                bsl = blk % 2
                wblock_load(wA[bsl], gwA[bsl], wout_d, 32, blk * 256, (blk + 1) * 256, f"dwW{bsl}")
                for c in range(CPAB):
                    pb, gp = (pBk[5], gB[5]) if c % 2 == 0 else (pBk[6], gB[6])
                    for kv in range(32):
                        em.gop("t", lambda e, kv_=kv, b=wA[bsl], p=pb, c_=c: e.matmul(
                            p[0:128, 0:256],
                            normedT[:, kv_ * AB + c_ * 128:kv_ * AB + (c_ + 1) * 128],
                            b[:, kv_ * 256:(kv_ + 1) * 256], start=(kv_ == 0), stop=(kv_ == 31)),
                            reads=[gwA[bsl], gNT], writes=[gp])
                    sl = blk % 2
                    osl = outst[c][sl]
                    go = gOut[c][sl]
                    em.gop("v", lambda e, p=pb, blk_=blk, c_=c, o=osl: e.tensor_tensor(
                        out=o, in0=p[0:128, 0:256],
                        in1=gbuf[c_][:, blk_ * 256:(blk_ + 1) * 256], op=ALU.mult),
                        reads=[gp, gGbuf[c]], writes=[go])
                    em.gop("v", lambda e, blk_=blk, c_=c, o=osl: e.tensor_tensor(
                        out=o, in0=o,
                        in1=xsl[c_][:, blk_ * 256:(blk_ + 1) * 256], op=ALU.add),
                        reads=[gXsl[c]], writes=[go])
                    dma_y(lambda e, blk_=blk, c_=c, o=osl, t0_=t0: e.dma_start(
                        out=out_d[t0_ + c_ * 128:t0_ + (c_ + 1) * 128,
                                  blk_ * 256:(blk_ + 1) * 256], in_=o),
                        f"dxo{c}{sl}", reads=[go])

    if 0 in phases:
        phase0()
    em.barrier()
    if 1 in phases:
        phaseB()
    em.barrier()
    if 2 in phases:
        phaseF()
    em.barrier()
    if 3 in phases:
        phaseA()
    em.op("y", lambda e: None,
          waits=[(nm, em.counts[nm]) for nm in sorted(em.dma_sems)], sem=None)

    sems = {}
    for nm in ("t", "v", "s"):
        sems[nm] = es.enter_context(nc.semaphore(f"sem_{nm}"))
    for nm in sorted(em.dma_sems):
        sems[nm] = es.enter_context(nc.semaphore(f"sem_{nm}"))

    with nc.Block() as block:
        @block.tensor
        def _(t):
            em.replay("t", t, sems)

        @block.vector
        def _(v):
            em.replay("v", v, sems)

        @block.scalar
        def _(s):
            em.replay("s", s, sems)

        @block.gpsimd
        def _(g):
            em.replay("g", g, sems)

        @block.sync
        def _(y):
            em.replay("y", y, sems)

    es.close()
    return nc


_NC_CACHE = {}


def kernel(L0_attn_hidden, L0_attn_W_qkv, L0_attn_W_z, L0_attn_W_b, L0_attn_W_a,
           L0_attn_conv_weight, L0_attn_dt_bias, L0_attn_A_log, L0_attn_norm_w,
           L0_attn_W_out, L0_ffn_gate_w, L0_ffn_w1, L0_ffn_w2, L0_ffn_w3,
           L0_ffn_sw1, L0_ffn_sw2, L0_ffn_sw3, L0_ffn_expert_bias):
    import ml_dtypes
    B, T, _ = L0_attn_hidden.shape
    if T not in _NC_CACHE:
        _NC_CACHE[T] = build_nc(T)
    nc = _NC_CACHE[T]
    bfv = lambda a: np.ascontiguousarray(a).astype(ml_dtypes.bfloat16).view(np.uint16)
    f = lambda a: np.ascontiguousarray(np.asarray(a, dtype=np.float32))
    common = {
        "wqkv": f(L0_attn_W_qkv), "wz": f(L0_attn_W_z), "wb": f(L0_attn_W_b),
        "wa": f(L0_attn_W_a), "cw": f(L0_attn_conv_weight), "dtb": f(L0_attn_dt_bias),
        "alog": f(L0_attn_A_log), "nw": f(L0_attn_norm_w), "wout": f(L0_attn_W_out),
        "gatew": f(L0_ffn_gate_w), "w1": f(L0_ffn_w1), "w2": f(L0_ffn_w2),
        "w3": f(L0_ffn_w3), "sw1": f(L0_ffn_sw1), "sw2": f(L0_ffn_sw2),
        "sw3": f(L0_ffn_sw3), "ebias": f(L0_ffn_expert_bias),
        "i_consts": np.tile(np.array([LN_EPS, RMS_EPS, 1.0, -0.5 * LN128, 0.0, 0.0, 0.0, 0.0],
                                     np.float32)[None, :], (128, 1)),
        "i_identb": bfv(np.eye(128, dtype=np.float32)),
        "i_identf": np.eye(128, dtype=np.float32),
        "i_ones1f": np.ones((1, 128), np.float32),
        "i_onescolb": bfv(np.ones((128, 1), np.float32)),
        "i_masks": host_masks(),
        "i_iotaE": np.tile(np.arange(NE, dtype=np.float32)[None, :], (128, 1)),
    }
    in_maps = []
    for b in range(B):
        m = dict(common)
        m["x"] = f(L0_attn_hidden[b])
        in_maps.append(m)
    res = run_bass_kernel_spmd(nc, in_maps, core_ids=list(range(B)))
    out = np.stack([np.asarray(res.results[i]["out"]) for i in range(B)], axis=0)
    return out.astype(np.float32)


# version 03
# speedup vs baseline: 1.0096x; 1.0096x over previous
from collections import defaultdict
from contextlib import ExitStack

import numpy as np
import concourse.bass as bass
import concourse.mybir as mybir
from concourse.bass_utils import run_bass_kernel_spmd

F32 = mybir.dt.float32
BF16 = mybir.dt.bfloat16
AF = mybir.ActivationFunctionType
ALU = mybir.AluOpType
AX = mybir.AxisListType

D = 2048
CONVD = 8192
KEY = 2048
VAL = 4096
HK, HV = 16, 32
NE, HID = 8, 1792
KC = 4
CH = 128
AB = 256
LN_EPS = 1e-5
L2_EPS = 1e-6
RMS_EPS = 1e-6
LN128 = float(np.log(128.0))
SUBA = {"mix", "l2n", "z", "core", "smm"}
CORE_STEPS = 99
NHV = HV
ST8 = "all"

M_STRICT = 0
M_TRIU = 1
M_ODT = {s: 2 + i for i, s in enumerate((1, 2, 4, 8, 16, 32, 64))}
M_OD1 = 9
M_ONES = 10
NMASK = 11


class Guard:
    __slots__ = ("w", "r")

    def __init__(self):
        self.w = []
        self.r = []

    def rd(self):
        return list(self.w)

    def wr(self):
        return list(self.w) + list(self.r)

    def setw(self, nm, val):
        self.w = [(nm, val)]
        self.r = []

    def addr(self, nm, val):
        self.r.append((nm, val))
        if len(self.r) > 24:
            best = {}
            for nm2, v2 in self.r:
                best[nm2] = max(best.get(nm2, 0), v2)
            self.r = sorted(best.items())


class Emitter:
    STREAMS = ("t", "v", "s", "g", "y")
    ENG_SEM = {"t": "t", "v": "v", "s": "s"}
    DROP_OWN = False

    def __init__(self):
        self.ops = {k: [] for k in self.STREAMS}
        self.counts = defaultdict(int)
        self.pending = {k: [] for k in self.STREAMS}
        self.dma_sems = set()
        self.hiwater = {k: defaultdict(int) for k in self.STREAMS}

    def op(self, stream, build, waits=(), sem=None, amt=1):
        allw = list(waits) + self.pending[stream]
        self.pending[stream] = []
        ws = {}
        own = self.ENG_SEM.get(stream)
        hw = self.hiwater[stream]
        for nm, vv in allw:
            if vv <= 0 or (self.DROP_OWN and nm == own) or vv <= hw[nm]:
                continue
            ws[nm] = max(ws.get(nm, 0), vv)
        for nm, vv in ws.items():
            assert vv <= self.counts[nm], (stream, nm, vv, self.counts[nm])
            hw[nm] = vv
        self.ops[stream].append((sorted(ws.items()), build, sem, amt))
        if sem is not None:
            self.counts[sem] += amt
            return self.counts[sem]
        return None

    def gop(self, stream, build, reads=(), writes=(), extra=(), sem=None):
        if sem is None:
            sem = self.ENG_SEM[stream]
            amt = 1
        else:
            amt = 16
            self.dma_sems.add(sem)
        waits = []
        for gd in writes:
            waits += gd.wr()
        for gd in reads:
            waits += gd.rd()
        waits += list(extra)
        val = self.op(stream, build, waits, sem=sem, amt=amt)
        for gd in writes:
            gd.setw(sem, val)
        for gd in reads:
            gd.addr(sem, val)
        return val

    def barrier(self):
        snap = [(nm, v) for nm, v in self.counts.items() if v > 0]
        for st in self.STREAMS:
            self.pending[st] += snap

    def replay(self, stream, eng, sems):
        for waits, build, incname, amt in self.ops[stream]:
            for nm, v in waits:
                eng.wait_ge(sems[nm], v)
            inst = build(eng)
            if incname is not None:
                inst.then_inc(sems[incname], amt)


def host_masks():
    n = 128
    i = np.arange(n)
    m = np.zeros((n, 128 * NMASK), np.float32)

    def blk(idx):
        return slice(128 * idx, 128 * (idx + 1))

    m[:, blk(M_STRICT)] = (i[:, None] > i[None, :])
    m[:, blk(M_TRIU)] = (i[None, :] >= i[:, None])
    for s, idx in M_ODT.items():
        od = ((i[:, None] // s) % 2 == 1) & ((i[None, :] // s) % 2 == 0) \
            & ((i[:, None] // (2 * s)) == (i[None, :] // (2 * s)))
        m[:, blk(idx)] = od.T
    s = 1
    od1 = ((i[:, None] // s) % 2 == 1) & ((i[None, :] // s) % 2 == 0) \
        & ((i[:, None] // (2 * s)) == (i[None, :] // (2 * s)))
    m[:, blk(M_OD1)] = od1
    m[:, blk(M_ONES)] = 1.0
    return m


def build_nc(T=2048, phases=(0, 1, 2, 3)):
    TTL = T // 128
    NAB = T // AB
    TBF = min(512, T)
    NFB = T // TBF
    NTB = TBF // 128
    CPAB = AB // CH

    nc = bass.Bass("TRN2", target_bir_lowering=False)

    def din(name, shape, dt=F32):
        return nc.dram_tensor(name, shape, dt, kind="ExternalInput").ap()

    x_d = din("x", [T, D])
    wqkv_d = din("wqkv", [D, CONVD])
    wz_d = din("wz", [D, VAL])
    wb_d = din("wb", [D, HV])
    wa_d = din("wa", [D, HV])
    cw_d = din("cw", [CONVD, KC])
    dtb_d = din("dtb", [HV])
    alog_d = din("alog", [HV])
    nw_d = din("nw", [128])
    wout_d = din("wout", [VAL, D])
    gatew_d = din("gatew", [D, NE])
    w1_d = din("w1", [NE, D, HID])
    w2_d = din("w2", [NE, HID, D])
    w3_d = din("w3", [NE, D, HID])
    sw1_d = din("sw1", [D, HID])
    sw2_d = din("sw2", [HID, D])
    sw3_d = din("sw3", [D, HID])
    ebias_d = din("ebias", [NE])
    consts_d = din("i_consts", [128, 8], F32)
    identb_d = din("i_identb", [128, 128], BF16)
    identf_d = din("i_identf", [128, 128], F32)
    ones1f_d = din("i_ones1f", [1, 128], F32)
    onescolb_d = din("i_onescolb", [128, 1], BF16)
    masks_d = din("i_masks", [128, 128 * NMASK], F32)
    iotaE_d = din("i_iotaE", [128, NE], F32)

    out_d = nc.dram_tensor("out", [T, D], F32, kind="ExternalOutput").ap()
    xnT_d = nc.dram_tensor("xnT", [16, 128, T], BF16, kind="Internal").ap()
    g_d = nc.dram_tensor("gd", [T, D], BF16, kind="Internal").ap()

    es = ExitStack()

    def sb(name, shape, dt):
        return es.enter_context(nc.sbuf_tensor(name, shape, dt))

    constcol = sb("constcol", [128, 8], F32)
    identb = sb("identb", [128, 128], BF16)
    identf = sb("identf", [128, 128], F32)
    ones1f = sb("ones1f", [1, 128], F32)
    onescolb = sb("onescolb", [128, 1], BF16)
    masks = sb("masks", [128, 128 * NMASK], F32)
    iotaE = sb("iotaE", [128, NE], F32)
    nwbc = sb("nwbc", [128, 128], F32)
    ebbc = sb("ebbc", [128, NE], F32)
    nwrow = sb("nwrow", [1, 128], F32)
    ebrow = sb("ebrow", [1, NE], F32)
    gatew_sb = sb("gatew_sb", [128, 16 * NE], F32)
    wb_sb = sb("wb_sb", [128, 16 * HV], BF16)
    wa_sb = sb("wa_sb", [128, 16 * HV], BF16)
    cw_sb = sb("cw_sb", [128, 64 * KC], F32)
    dtb_sb = sb("dtb_sb", [HV, 1], F32)
    alog_sb = sb("alog_sb", [HV, 1], F32)
    negea_sb = sb("negea_sb", [HV, 1], F32)
    st_bf = sb("st_bf", [128, HV * 128], BF16)
    wexp = sb("wexp", [128, TTL * NE], F32)
    mixstash = sb("mixstash", [128, 64 * (KC - 1)], F32)
    gjt = sb("gjt", [128, TTL * HV], F32)
    bt = sb("bt", [128, TTL * HV], F32)
    negbt = sb("negbt", [128, TTL * HV], F32)
    egjt = sb("egjt", [128, TTL * HV], F32)
    eglmt = sb("eglmt", [128, TTL * HV], F32)
    begjt = sb("begjt", [128, TTL * HV], F32)
    eglall = sb("eglall", [128, TTL * HV], F32)
    colsc = sb("colsc", [128, 16], F32)
    rowsc = sb("rowsc", [1, 512], F32)
    hmA = sb("hmA", [HV, TBF], F32)
    hmB = sb("hmB", [HV, TBF], F32)
    hmC = sb("hmC", [HV, TBF], F32)
    onesHM = sb("onesHM", [HV, CH], F32)
    gsm = sb("gsm", [128, 10 * NE], F32)
    sq_sb = sb("sq_sb", [128, AB], BF16)

    arN1 = sb("arN1", [128, 16384], BF16)
    arN2 = sb("arN2", [128, 16384], BF16)
    arN3 = sb("arN3", [128, 8192], BF16)
    arN4 = sb("arN4", [128, 8192], F32)
    arN5 = sb("arN5", [128, 16384], BF16)

    c_cbc = sb("c_cbc", [128, 256], F32)
    c_e1 = sb("c_e1", [128, 256], F32)
    c_e2 = sb("c_e2", [128, 256], F32)
    c_dcy = sb("c_dcy", [128, 256], F32)
    c_dcyT = sb("c_dcyT", [128, 256], F32)
    c_xg = sb("c_xg", [128, 256], F32)
    c_A = sb("c_A", [128, 256], BF16)
    c_AT = sb("c_AT", [128, 256], BF16)
    c_aod = sb("c_aod", [128, 256], BF16)
    c_W = sb("c_W", [128, 256], BF16)
    c_intraT = sb("c_intraT", [128, 256], BF16)
    c_kdec = sb("c_kdec", [128, 256], BF16)
    c_vnew = sb("c_vnew", [128, 256], BF16)
    c_X0 = sb("c_X0", [128, 512], BF16)
    c_Xsb = sb("c_Xsb", [128, 512], BF16)
    c_T0 = sb("c_T0", [128, 512], BF16)
    c_T1 = sb("c_T1", [128, 512], BF16)
    c_ntok = sb("c_ntok", [128, 256], BF16)
    c_scrap = sb("c_scrap", [128, 256], BF16)
    c_kcdT = sb("c_kcdT", [128, 256], BF16)
    c_cols = sb("c_cols", [128, 8], F32)

    pBk = [es.enter_context(nc.psum_tensor(f"pb{i}", [128, 512], F32)) for i in range(7)]
    pBF = es.enter_context(nc.psum_tensor("pbf", [128, 1024], BF16))

    em = Emitter()
    gB = [Guard() for _ in range(7)]
    gBF = Guard()
    _g = defaultdict(Guard)

    def G(nm):
        return _g[nm]

    def dma_y(build, sem, reads=(), writes=()):
        return em.gop("y", build, reads=reads, writes=writes, sem=sem)

    def dma_g(build, sem, reads=(), writes=()):
        return em.gop("g", build, reads=reads, writes=writes, sem=sem)

    CONST_VALS = (LN_EPS, RMS_EPS, 1.0, -0.5 * LN128, 0.0)
    for ci, cv in enumerate(CONST_VALS):
        nc.const_aps.register(F32, cv, constcol[:, ci:ci + 1])
    dma_y(lambda e, d=constcol, s=consts_d: e.dma_start(out=d[:], in_=s), "dxi", writes=[G("constcol")])
    dma_y(lambda e, d=identb, s=identb_d: e.dma_start(out=d[:], in_=s), "dxi", writes=[G("identb")])
    dma_y(lambda e, d=identf, s=identf_d: e.dma_start(out=d[:], in_=s), "dxi", writes=[G("identf")])
    dma_y(lambda e, d=ones1f, s=ones1f_d: e.dma_start(out=d[:], in_=s), "dxi", writes=[G("ones1f")])
    dma_y(lambda e, d=onescolb, s=onescolb_d: e.dma_start(out=d[:], in_=s), "dxi", writes=[G("onescolb")])
    dma_y(lambda e, d=masks, s=masks_d: e.dma_start(out=d[:], in_=s), "dxi", writes=[G("masks")])
    dma_y(lambda e, d=iotaE, s=iotaE_d: e.dma_start(out=d[:], in_=s), "dxi", writes=[G("iotaE")])
    dma_y(lambda e, d=nwrow, s=nw_d: e.dma_start(out=d[:], in_=s.rearrange("(o f) -> o f", o=1)),
          "dxi", writes=[G("nwrow")])
    dma_y(lambda e, d=ebrow, s=ebias_d: e.dma_start(out=d[:], in_=s.rearrange("(o f) -> o f", o=1)),
          "dxi", writes=[G("ebrow")])
    dma_y(lambda e, d=gatew_sb, s=gatew_d: e.dma_start(
        out=d[:].rearrange("p (k f) -> p k f", k=16),
        in_=s.rearrange("(k p) f -> p k f", p=128)), "dxi", writes=[G("gatew_sb")])
    dma_g(lambda e, d=wb_sb, s=wb_d: e.dma_start(
        out=d[:].rearrange("p (k f) -> p k f", k=16),
        in_=s.rearrange("(k p) f -> p k f", p=128)), "dwi", writes=[G("wb_sb")])
    dma_g(lambda e, d=wa_sb, s=wa_d: e.dma_start(
        out=d[:].rearrange("p (k f) -> p k f", k=16),
        in_=s.rearrange("(k p) f -> p k f", p=128)), "dwi", writes=[G("wa_sb")])
    dma_y(lambda e, d=cw_sb, s=cw_d: e.dma_start(
        out=d[:].rearrange("p (k f) -> p k f", k=64),
        in_=s.rearrange("(k p) f -> p k f", p=128)), "dxi", writes=[G("cw_sb")])
    dma_y(lambda e, d=dtb_sb, s=dtb_d: e.dma_start(out=d[:], in_=s.rearrange("(p o) -> p o", o=1)),
          "dxi", writes=[G("dtb_sb")])
    dma_y(lambda e, d=alog_sb, s=alog_d: e.dma_start(out=d[:], in_=s.rearrange("(p o) -> p o", o=1)),
          "dxi", writes=[G("alog_sb")])

    em.barrier()
    em.gop("t", lambda e: e.matmul(pBk[5][0:128, 0:128], ones1f[:], nwrow[:], start=True, stop=True),
           reads=[G("ones1f"), G("nwrow")], writes=[gB[5]])
    em.gop("s", lambda e: e.copy(out=nwbc[:], in_=pBk[5][0:128, 0:128]),
           reads=[gB[5]], writes=[G("nwbc")])
    em.gop("t", lambda e: e.matmul(pBk[5][0:128, 128:128 + NE], ones1f[:], ebrow[:],
                                   start=True, stop=True),
           reads=[G("ones1f"), G("ebrow")], writes=[gB[5]])
    em.gop("s", lambda e: e.copy(out=ebbc[:], in_=pBk[5][0:128, 128:128 + NE]),
           reads=[gB[5]], writes=[G("ebbc")])
    em.gop("s", lambda e: e.activation(out=negea_sb[:], in_=alog_sb[:], func=AF.Exp),
           reads=[G("alog_sb")], writes=[G("negea_sb")])
    em.gop("v", lambda e: e.tensor_scalar(out=negea_sb[:], in0=negea_sb[:], scalar1=-1.0,
                                          scalar2=None, op0=ALU.mult),
           reads=[], writes=[G("negea_sb")])
    em.gop("v", lambda e: e.memset(onesHM[:], 1.0), writes=[G("onesHM")])
    em.gop("v", lambda e: e.memset(st_bf[:], 0.0), writes=[G("st_bf")])
    em.gop("v", lambda e: e.memset(mixstash[:], 0.0), writes=[G("mixstash")])
    em.barrier()

    def phase0():
        xa = [arN4[:, 0:2048], arN4[:, 2048:4096]]
        gXa = [G("p0x0"), G("p0x1")]
        xc = arN4[:, 4096:6144]
        xn = arN4[:, 6144:8192]
        gXc, gXn = G("p0xc"), G("p0xn")
        for tt in range(TTL):
            xab, gx = xa[tt % 2], gXa[tt % 2]
            dma_y(lambda e, d=xab, s=x_d[tt * 128:(tt + 1) * 128, :]: e.dma_start(out=d, in_=s),
                  f"dxp0{tt % 2}", writes=[gx])
            em.gop("v", lambda e, s=xab: e.reduce_sum(out=colsc[:, 0:1], in_=s, axis=AX.X),
                   reads=[gx], writes=[G("c0")])
            em.gop("v", lambda e: e.tensor_scalar(out=colsc[:, 1:2], in0=colsc[:, 0:1],
                                                  scalar1=1.0 / D, scalar2=None, op0=ALU.mult),
                   reads=[G("c0")], writes=[G("c1")])
            em.gop("v", lambda e, s=xab: e.tensor_scalar(out=xc, in0=s, scalar1=colsc[:, 1:2],
                                                         scalar2=None, op0=ALU.subtract),
                   reads=[gx, G("c1")], writes=[gXc])
            em.gop("v", lambda e: e.scalar_tensor_tensor(out=xn, in0=xc, scalar=1.0, in1=xc,
                                                         op0=ALU.mult, op1=ALU.mult,
                                                         accum_out=colsc[:, 2:3]),
                   reads=[gXc], writes=[gXn, G("c2")])
            em.gop("s", lambda e: e.activation(out=colsc[:, 3:4], in_=colsc[:, 2:3], func=AF.Ln,
                                               scale=1.0 / D, bias=LN_EPS),
                   reads=[G("c2")], writes=[G("c3")])
            em.gop("s", lambda e: e.activation(out=colsc[:, 4:5], in_=colsc[:, 3:4], func=AF.Exp,
                                               scale=-0.5),
                   reads=[G("c3")], writes=[G("c4")])
            em.gop("v", lambda e: e.tensor_scalar(out=xn, in0=xc, scalar1=colsc[:, 4:5],
                                                  scalar2=None, op0=ALU.mult),
                   reads=[gXc, G("c4")], writes=[gXn])
            for k in range(16):
                stf = xc[:, 128 * (k % 2):128 * (k % 2 + 1)]
                stb = arN5[:, 128 * (k % 2):128 * (k % 2 + 1)]
                gstf, gstb = G(f"p0sf{k % 2}"), G(f"p0sb{k % 2}")
                em.gop("t", lambda e, kk=k: e.transpose(pBk[0][0:128, 0:128],
                                                        xn[:, kk * 128:(kk + 1) * 128], identf[:]),
                       reads=[gXn, G("identf")], writes=[gB[0]])
                em.gop("s", lambda e, d=stf: e.copy(out=d, in_=pBk[0][0:128, 0:128]),
                       reads=[gB[0]], writes=[gstf, gXc])
                em.gop("t", lambda e, d=stf, kk=k: e.matmul(
                    pBk[1][0:128, 0:NE], d, gatew_sb[:, kk * NE:(kk + 1) * NE],
                    start=(kk == 0), stop=(kk == 15)),
                    reads=[gstf, G("gatew_sb")], writes=[gB[1]])
                em.gop("v", lambda e, d=stb, s2=stf: e.tensor_copy(out=d, in_=s2),
                       reads=[gstf], writes=[gstb])
                dma_y(lambda e, s2=stb, d=xnT_d[k, :, tt * 128:(tt + 1) * 128]:
                      e.dma_start(out=d, in_=s2), f"dxst{k % 2}", reads=[gstb])
            gl = gsm[:, 0:NE]; e0 = gsm[:, NE:2 * NE]; t1 = gsm[:, 2 * NE:3 * NE]
            sc = gsm[:, 3 * NE:4 * NE]; bsd = gsm[:, 4 * NE:5 * NE]
            eqv = gsm[:, 5 * NE:6 * NE]; cnd = gsm[:, 6 * NE:7 * NE]
            msk1 = gsm[:, 7 * NE:8 * NE]; msk2 = gsm[:, 8 * NE:9 * NE]
            l0 = gsm[:, 9 * NE:10 * NE]
            gg = G("gsm")
            em.gop("v", lambda e: e.tensor_copy(out=gl, in_=pBk[1][0:128, 0:NE]),
                   reads=[gB[1]], writes=[gg])
            em.gop("v", lambda e: e.tensor_reduce(out=colsc[:, 5:6], in_=gl, axis=AX.X,
                                                  op=ALU.max, negate=True),
                   reads=[gg], writes=[G("c5")])
            em.gop("s", lambda e: e.activation(out=e0, in_=gl, func=AF.Exp, bias=colsc[:, 5:6]),
                   reads=[gg, G("c5")], writes=[gg])
            em.gop("s", lambda e: e.activation(out=l0, in_=e0, func=AF.Ln),
                   reads=[gg], writes=[gg])
            em.gop("v", lambda e: e.tensor_scalar(out=t1, in0=gl, scalar1=colsc[:, 5:6],
                                                  scalar2=None, op0=ALU.add),
                   reads=[gg, G("c5")], writes=[gg])
            em.gop("v", lambda e: e.tensor_tensor(out=t1, in0=t1, in1=l0, op=ALU.subtract),
                   reads=[gg], writes=[gg])
            em.gop("v", lambda e: e.tensor_scalar(out=t1, in0=t1, scalar1=1.0, scalar2=None,
                                                  op0=ALU.add), reads=[gg], writes=[gg])
            em.gop("v", lambda e: e.tensor_tensor(out=e0, in0=e0, in1=t1, op=ALU.mult),
                   reads=[gg], writes=[gg])
            em.gop("v", lambda e: e.reduce_sum(out=colsc[:, 6:7], in_=e0, axis=AX.X),
                   reads=[gg], writes=[G("c6")])
            em.gop("v", lambda e: e.reciprocal(out=colsc[:, 7:8], in_=colsc[:, 6:7]),
                   reads=[G("c6")], writes=[G("c7")])
            em.gop("v", lambda e: e.tensor_scalar(out=sc, in0=e0, scalar1=colsc[:, 7:8],
                                                  scalar2=None, op0=ALU.mult),
                   reads=[gg, G("c7")], writes=[gg])
            em.gop("v", lambda e: e.tensor_tensor(out=bsd, in0=sc, in1=ebbc[:, 0:NE], op=ALU.add),
                   reads=[gg, G("ebbc")], writes=[gg])
            for r, (mskT, c_a, c_b) in enumerate(((msk1, 8, 9), (msk2, 10, 11))):
                em.gop("v", lambda e: e.tensor_reduce(out=colsc[:, 12:13], in_=bsd, axis=AX.X,
                                                      op=ALU.max),
                       reads=[gg], writes=[G("c12")])
                em.gop("v", lambda e: e.tensor_scalar(out=eqv, in0=bsd, scalar1=colsc[:, 12:13],
                                                      scalar2=None, op0=ALU.is_ge),
                       reads=[gg, G("c12")], writes=[gg])
                em.gop("v", lambda e: e.scalar_tensor_tensor(out=cnd, in0=eqv, scalar=-1000.0,
                                                             in1=iotaE[:, 0:NE], op0=ALU.mult,
                                                             op1=ALU.add),
                       reads=[gg, G("iotaE")], writes=[gg])
                em.gop("v", lambda e, ca=c_a: e.tensor_reduce(out=colsc[:, ca:ca + 1], in_=cnd,
                                                              axis=AX.X, op=ALU.min),
                       reads=[gg], writes=[G("c8")])
                em.gop("v", lambda e, ca=c_a, cb=c_b: e.tensor_scalar(
                    out=colsc[:, cb:cb + 1], in0=colsc[:, ca:ca + 1], scalar1=1000.0,
                    scalar2=None, op0=ALU.add), reads=[G("c8")], writes=[G("c9")])
                em.gop("v", lambda e, m=mskT, cb=c_b: e.tensor_scalar(
                    out=m, in0=iotaE[:, 0:NE], scalar1=colsc[:, cb:cb + 1], scalar2=None,
                    op0=ALU.is_equal), reads=[G("iotaE"), G("c9")], writes=[gg])
                if r == 0:
                    em.gop("v", lambda e: e.scalar_tensor_tensor(out=bsd, in0=msk1, scalar=-1e30,
                                                                 in1=bsd, op0=ALU.mult,
                                                                 op1=ALU.add),
                           reads=[gg], writes=[gg])
            em.gop("v", lambda e: e.tensor_tensor(out=msk1, in0=msk1, in1=msk2, op=ALU.add),
                   reads=[gg], writes=[gg])
            em.gop("v", lambda e, tt_=tt: e.tensor_tensor(out=wexp[:, tt_ * NE:(tt_ + 1) * NE],
                                                          in0=sc, in1=msk1, op=ALU.mult),
                   reads=[gg], writes=[G("wexp")])

    def xt_load(dst, gdst, c0, c1, nseg=4, extra_reads=()):
        w = c1 - c0
        per = 16 // nseg
        for i in range(nseg):
            dma_y(lambda e, i_=i, d=dst, w_=w, p_=per: e.dma_start(
                out=d[:, i_ * p_ * w_:(i_ + 1) * p_ * w_].rearrange("p (k c) -> p k c", k=p_),
                in_=xnT_d[i_ * p_:(i_ + 1) * p_, :, c0:c1].rearrange("k p c -> p k c")),
                "dxt", reads=list(extra_reads), writes=[gdst])

    def wblock_load(dst_ap, gdst, src2d, rk, c0, c1, sem, nseg=2):
        w = c1 - c0
        per = (rk + nseg - 1) // nseg
        gds = list(gdst) if isinstance(gdst, (list, tuple)) else [gdst]
        i = 0
        while i * per < rk:
            k0, k1 = i * per, min((i + 1) * per, rk)
            dma_g(lambda e, k0_=k0, k1_=k1, d=dst_ap, s=src2d, w_=w: e.dma_start(
                out=d[:, k0_ * w_:k1_ * w_].rearrange("p (k c) -> p k c", k=k1_ - k0_),
                in_=s[k0_ * 128:k1_ * 128, c0:c1].rearrange("(k p) c -> p k c", p=128)),
                sem, writes=gds)
            i += 1

    def phaseB():
        for tb in range(NFB):
            W = TBF
            c0 = tb * W
            xts = arN3[:, 0:16 * W]
            gx = G("pbxts")
            xt_load(xts, gx, c0, c0 + W)
            for (wsb, gnm, po) in ((wb_sb, "wb_sb", 0), (wa_sb, "wa_sb", 32)):
                for k in range(16):
                    em.gop("t", lambda e, kk=k, ws=wsb, po_=po, W_=W: e.matmul(
                        pBk[2][po_:po_ + 32, 0:W_], ws[:, kk * HV:kk * HV + HV],
                        xts[:, kk * W_:(kk + 1) * W_], start=(kk == 0), stop=(kk == 15)),
                        reads=[gx, G(gnm)], writes=[gB[2]])
            em.gop("s", lambda e, W_=W: e.activation(out=hmA[:, 0:W_], in_=pBk[2][0:32, 0:W_],
                                                     func=AF.Sigmoid),
                   reads=[gB[2]], writes=[G("hmA")])
            em.gop("s", lambda e, W_=W: e.activation(out=hmB[:, 0:W_], in_=pBk[2][32:64, 0:W_],
                                                     func=AF.Exp, bias=dtb_sb[0:32, :]),
                   reads=[gB[2], G("dtb_sb")], writes=[G("hmB")])
            em.gop("s", lambda e, W_=W: e.activation(out=hmC[:, 0:W_], in_=hmB[:, 0:W_],
                                                     func=AF.Ln, bias=1.0),
                   reads=[G("hmB")], writes=[G("hmC")])
            em.gop("v", lambda e, W_=W: e.tensor_scalar(out=hmB[:, 0:W_], in0=hmC[:, 0:W_],
                                                        scalar1=negea_sb[0:32, :], scalar2=None,
                                                        op0=ALU.mult),
                   reads=[G("hmC"), G("negea_sb")], writes=[G("hmB")])
            ncpb = W // CH
            for c in range(ncpb):
                em.gop("v", lambda e, cc=c: e.tensor_tensor_scan(
                    out=hmC[:, cc * CH:(cc + 1) * CH], data0=onesHM[:, 0:CH],
                    data1=hmB[:, cc * CH:(cc + 1) * CH], initial=0.0,
                    op0=ALU.mult, op1=ALU.add),
                    reads=[G("hmB"), G("onesHM")], writes=[G("hmC")])
            for c in range(ncpb):
                em.gop("v", lambda e, cc=c: e.tensor_scalar(
                    out=hmB[:, cc * CH:(cc + 1) * CH], in0=hmC[:, cc * CH:(cc + 1) * CH],
                    scalar1=hmC[:, cc * CH + CH - 1:cc * CH + CH], scalar2=-1.0,
                    op0=ALU.subtract, op1=ALU.mult),
                    reads=[G("hmC")], writes=[G("hmB")])
            em.gop("s", lambda e, W_=W: e.activation(out=hmB[:, 0:W_], in_=hmB[:, 0:W_],
                                                     func=AF.Exp),
                   reads=[G("hmB")], writes=[G("hmB")])
            for c in range(ncpb):
                tt = tb * ncpb + c
                cs = slice(tt * HV, (tt + 1) * HV)
                for (src, gs, dst, gd, ofs) in (
                        (hmA, "hmA", bt, "bt", 0),
                        (hmC, "hmC", gjt, "gjt", 128),
                        (hmB, "hmB", eglmt, "eglmt", 256)):
                    em.gop("t", lambda e, s2=src, cc=c, o=ofs: e.transpose(
                        pBk[0][0:128, o:o + HV], s2[:, cc * CH:(cc + 1) * CH],
                        identf[0:HV, 0:HV]),
                        reads=[G(gs), G("identf")], writes=[gB[0]])
                    em.gop("s", lambda e, d=dst, cs_=cs, o=ofs: e.copy(
                        out=d[:, cs_], in_=pBk[0][0:128, o:o + HV]),
                        reads=[gB[0]], writes=[G(gd)])
                em.gop("t", lambda e, cc=c: e.transpose(
                    pBk[0][0:1, 384:384 + HV],
                    hmC[:, cc * CH + CH - 1:cc * CH + CH], identf[0:HV, 0:HV]),
                    reads=[G("hmC"), G("identf")], writes=[gB[0]])
                em.gop("s", lambda e: e.copy(out=rowsc[0:1, 256:256 + HV],
                                             in_=pBk[0][0:1, 384:384 + HV]),
                       reads=[gB[0]], writes=[G("rowsc")])
                em.gop("t", lambda e: e.matmul(pBk[0][0:128, 384:384 + HV], ones1f[:],
                                               rowsc[0:1, 256:256 + HV],
                                               start=True, stop=True),
                       reads=[G("rowsc"), G("ones1f")], writes=[gB[0]])
                em.gop("s", lambda e, cs_=cs: e.activation(out=eglall[:, cs_],
                                                           in_=pBk[0][0:128, 384:384 + HV],
                                                           func=AF.Exp),
                       reads=[gB[0]], writes=[G("eglall")])
                em.gop("v", lambda e, cs_=cs: e.tensor_scalar(out=negbt[:, cs_], in0=bt[:, cs_],
                                                              scalar1=-1.0, scalar2=None,
                                                              op0=ALU.mult),
                       reads=[G("bt")], writes=[G("negbt")])
                em.gop("s", lambda e, cs_=cs: e.activation(out=egjt[:, cs_], in_=gjt[:, cs_],
                                                           func=AF.Exp),
                       reads=[G("gjt")], writes=[G("egjt")])
                em.gop("v", lambda e, cs_=cs: e.tensor_tensor(out=begjt[:, cs_], in0=bt[:, cs_],
                                                              in1=egjt[:, cs_], op=ALU.mult),
                       reads=[G("bt"), G("egjt")], writes=[G("begjt")])

    def phaseF():
        W = TBF
        NT = W // 128
        assert 16 * W <= 8192
        g_acc = arN4[:, 0:16 * W]
        gGacc = G("gacc")
        xts = arN3[:, 0:16 * W]
        gx = G("pfxts")
        Xe = arN1[:, 0:16 * W]
        hbuf = arN1[:, 8192:8192 + 14 * W]
        h1a = arN1[:, 8192 + 14 * W:8192 + 15 * W]
        gXe, gH, gH1 = G("Xe"), G("hbuf"), G("h1a")
        wbc = [arN5[:, 0:W], arN5[:, W:2 * W]]
        gtokst = arN5[:, 2048:4096]
        gWbc = [G("wbc0"), G("wbc1")]
        gGt = G("gtokst")
        Xe2 = [Xe, arN5[:, 4096:4096 + 16 * W]]
        gXe2 = [gXe, G("Xe1")]
        wA = [arN2[:, 0:4096], arN2[:, 4096:8192]]
        wB = [arN2[:, 8192:12288], arN2[:, 12288:16384]]
        w2bl = [arN2[:, 0:7168], arN2[:, 8192:8192 + 7168]]
        gwA = [G("wA0"), G("wA1")]
        gwB = [G("wB0"), G("wB1")]
        gw2 = [G("w20"), G("w21")]
        NFG = HID // 256

        for tb in range(NFB):
            c0 = tb * W
            xt_load(xts, gx, c0, c0 + W)
            em.gop("v", lambda e, d=g_acc: e.memset(d, 0.0), reads=[], writes=[gGacc])
            def emit_xe(ei2):
                par = ei2 % 2
                for i in range(NT):
                    tt = tb * NT + i
                    em.gop("v", lambda e, tt_=tt, ei_=ei2: e.tensor_scalar(
                        out=c_cbc[:, 0:128], in0=masks[:, 128 * M_ONES:128 * (M_ONES + 1)],
                        scalar1=wexp[:, tt_ * NE + ei_:tt_ * NE + ei_ + 1],
                        scalar2=None, op0=ALU.mult),
                        reads=[G("masks"), G("wexp")], writes=[G("c_cbc")])
                    em.gop("t", lambda e: e.transpose(pBk[5][0:128, 0:128], c_cbc[:, 0:128],
                                                      identf[:]),
                           reads=[G("c_cbc"), G("identf")], writes=[gB[5]])
                    em.gop("s", lambda e, i_=i, w_=wbc[par]: e.copy(
                        out=w_[:, i_ * 128:(i_ + 1) * 128], in_=pBk[5][0:128, 0:128]),
                           reads=[gB[5]], writes=[gWbc[par]])
                for k in range(16):
                    em.gop("v", lambda e, kk=k, W_=W, d=Xe2[par], w_=wbc[par]: e.tensor_tensor(
                        out=d[:, kk * W_:(kk + 1) * W_], in0=xts[:, kk * W_:(kk + 1) * W_],
                        in1=w_[:, 0:W_], op=ALU.mult),
                        reads=[gx, gWbc[par]], writes=[gXe2[par]])

            for ei in range(-1, NE):
                shared = ei < 0
                mw1, mw3, mw2 = (sw1_d, sw3_d, sw2_d) if shared else (w1_d[ei], w3_d[ei], w2_d[ei])
                if ei + 1 < NE:
                    emit_xe(ei + 1)
                if not shared:
                    xin, gxin = Xe2[ei % 2], gXe2[ei % 2]
                else:
                    xin, gxin = xts, gx
                for fg in range(NFG):
                    bsl = fg % 2
                    wblock_load(wA[bsl], gwA[bsl], mw1, 16, fg * 256, (fg + 1) * 256, f"dwW{bsl}")
                    wblock_load(wB[bsl], gwB[bsl], mw3, 16, fg * 256, (fg + 1) * 256, f"dwB{bsl}")
                    for fi in range(2):
                        f = fg * 2 + fi
                        pb1, gp1 = (pBk[2], gB[2]) if fi == 0 else (pBk[4], gB[4])
                        pb3, gp3 = (pBk[3], gB[3]) if fi == 0 else (pBk[6], gB[6])
                        for k in range(16):
                            em.gop("t", lambda e, kk=k, fi_=fi, b=wA[bsl], p=pb1, xi=xin,
                                   W_=W: e.matmul(
                                p[0:128, 0:W_],
                                b[:, kk * 256 + fi_ * 128:kk * 256 + (fi_ + 1) * 128],
                                xi[:, kk * W_:(kk + 1) * W_], start=(kk == 0), stop=(kk == 15)),
                                reads=[gwA[bsl], gxin], writes=[gp1])
                        for k in range(16):
                            em.gop("t", lambda e, kk=k, fi_=fi, b=wB[bsl], p=pb3, xi=xin,
                                   W_=W: e.matmul(
                                p[0:128, 0:W_],
                                b[:, kk * 256 + fi_ * 128:kk * 256 + (fi_ + 1) * 128],
                                xi[:, kk * W_:(kk + 1) * W_], start=(kk == 0), stop=(kk == 15)),
                                reads=[gwB[bsl], gxin], writes=[gp3])
                        em.gop("s", lambda e, p=pb1, W_=W: e.activation(
                            out=h1a, in_=p[0:128, 0:W_], func=AF.Silu),
                            reads=[gp1], writes=[gH1])
                        em.gop("v", lambda e, p=pb3, f_=f, W_=W: e.tensor_tensor(
                            out=hbuf[:, f_ * W_:(f_ + 1) * W_], in0=p[0:128, 0:W_], in1=h1a,
                            op=ALU.mult),
                            reads=[gp3, gH1], writes=[gH])
                for cb in range(4):
                    bsl = cb % 2
                    w2g = [gwA[0], gwA[1]] if bsl == 0 else [gwB[0], gwB[1]]
                    wblock_load(w2bl[bsl], w2g, mw2, 14, cb * 512, (cb + 1) * 512, f"dw2{bsl}")
                    for fi in range(4):
                        f2 = cb * 4 + fi
                        pbo, gpo = (pBk[0], gB[0]) if fi % 2 == 0 else (pBk[1], gB[1])
                        for k in range(14):
                            em.gop("t", lambda e, kk=k, fi_=fi, b=w2bl[bsl], p=pbo,
                                   W_=W: e.matmul(
                                p[0:128, 0:W_],
                                b[:, kk * 512 + fi_ * 128:kk * 512 + (fi_ + 1) * 128],
                                hbuf[:, kk * W_:(kk + 1) * W_], start=(kk == 0), stop=(kk == 13)),
                                reads=w2g + [gH], writes=[gpo])
                        em.gop("v", lambda e, p=pbo, f2_=f2, W_=W: e.tensor_tensor(
                            out=g_acc[:, f2_ * W_:(f2_ + 1) * W_],
                            in0=g_acc[:, f2_ * W_:(f2_ + 1) * W_],
                            in1=p[0:128, 0:W_], op=ALU.add),
                            reads=[gpo], writes=[gGacc])
            for i in range(NT):
                for kf in range(16):
                    em.gop("t", lambda e, i_=i, kf_=kf, W_=W: e.transpose(
                        pBk[5][0:128, 0:128],
                        g_acc[:, kf_ * W_ + i_ * 128:kf_ * W_ + (i_ + 1) * 128], identf[:]),
                        reads=[gGacc, G("identf")], writes=[gB[5]])
                    em.gop("s", lambda e, kf_=kf: e.activation(
                        out=gtokst[:, kf_ * 128:(kf_ + 1) * 128], in_=pBk[5][0:128, 0:128],
                        func=AF.Silu),
                        reads=[gB[5]], writes=[gGt])
                dma_y(lambda e, i_=i, t0=c0: e.dma_start(
                    out=g_d[t0 + i_ * 128:t0 + (i_ + 1) * 128, :], in_=gtokst[:, 0:2048]),
                    "dxgd", reads=[gGt])

    def core_front(ab, c, hv):
        j = ab * CPAB + c
        tt = j
        kh = hv // 2
        par = hv % 2
        qT = arN1[:, kh * AB + c * 128:kh * AB + (c + 1) * 128]
        kT = arN1[:, (16 + kh) * AB + c * 128:(16 + kh) * AB + (c + 1) * 128]
        vT = arN1[:, (32 + hv) * AB + c * 128:(32 + hv) * AB + (c + 1) * 128]
        gQKV = G("qkvT")
        col = lambda t_: t_[:, tt * HV + hv:tt * HV + hv + 1]
        sil = arN5[:, c * 4096 + hv * 128:c * 4096 + (hv + 1) * 128]
        sthv = st_bf[:, hv * 128:(hv + 1) * 128]
        cb0 = par * 4
        mk = lambda idx: masks[:, 128 * idx:128 * (idx + 1)]
        P = lambda t_: t_[:, par * 128:(par + 1) * 128]
        P2 = lambda t_: t_[:, par * 256:(par + 1) * 256]
        Gn = lambda nm: G(f"{nm}{par}")
        bRGQ, gRGQ = (pBk[0], gB[0]) if par == 0 else (pBk[1], gB[1])
        bWX, gWX = (pBk[2], gB[2]) if par == 0 else (pBk[5], gB[5])
        bLAD, gLAD = (pBk[3], gB[3]) if par == 0 else (pBk[6], gB[6])
        ctx = dict(j=j, tt=tt, c=c, hv=hv, par=par, qT=qT, kT=kT, sil=sil, sthv=sthv,
                   cb0=cb0, bWX=bWX, gWX=gWX, col=col, mk=mk, P=P, P2=P2, Gn=Gn)

        em.gop("v", lambda e: e.tensor_scalar(out=P(c_cbc), in0=mk(M_ONES),
                                              scalar1=col(gjt), scalar2=None, op0=ALU.mult),
               reads=[G("masks"), G("gjt")], writes=[Gn("c_cbc")])
        em.gop("t", lambda e: e.transpose(bRGQ[0:128, 0:128], P(c_cbc), identf[:]),
               reads=[Gn("c_cbc"), G("identf")], writes=[gRGQ])
        em.gop("t", lambda e: e.matmul(bRGQ[0:128, 128:256], kT, kT, start=True, stop=True),
               reads=[gQKV], writes=[gRGQ])
        em.gop("t", lambda e: e.matmul(bRGQ[0:128, 256:384], kT, qT, start=True, stop=True),
               reads=[gQKV], writes=[gRGQ])
        em.gop("v", lambda e: e.tensor_scalar(out=P(c_e1), in0=bRGQ[0:128, 0:128],
                                              scalar1=col(gjt), scalar2=0.0,
                                              op0=ALU.subtract, op1=ALU.max),
               reads=[gRGQ, G("gjt")], writes=[Gn("c_e1")])
        em.gop("v", lambda e: e.tensor_scalar(out=P(c_e2), in0=bRGQ[0:128, 0:128],
                                              scalar1=col(gjt), scalar2=0.0,
                                              op0=ALU.subtract, op1=ALU.min),
               reads=[gRGQ, G("gjt")], writes=[Gn("c_e2")])
        em.gop("s", lambda e: e.activation(out=P(c_dcy), in_=P(c_e1), func=AF.Exp, scale=-1.0),
               reads=[Gn("c_e1")], writes=[Gn("c_dcy")])
        em.gop("s", lambda e: e.activation(out=P(c_dcyT), in_=P(c_e2), func=AF.Exp),
               reads=[Gn("c_e2")], writes=[Gn("c_dcyT")])
        if CORE_STEPS < 2:
            return ctx
        em.gop("v", lambda e: e.scalar_tensor_tensor(out=P(c_scrap), in0=bRGQ[0:128, 128:256],
                                                     scalar=col(negbt), in1=P(c_dcy),
                                                     op0=ALU.mult, op1=ALU.mult),
               reads=[gRGQ, G("negbt"), Gn("c_dcy")], writes=[Gn("c_scrap")])
        em.gop("v", lambda e: e.tensor_tensor(out=P(c_A), in0=P(c_scrap), in1=mk(M_STRICT),
                                              op=ALU.mult),
               reads=[Gn("c_scrap"), G("masks")], writes=[Gn("c_A")])
        em.gop("v", lambda e: e.tensor_tensor(out=P(c_scrap), in0=bRGQ[0:128, 256:384],
                                              in1=P(c_dcyT), op=ALU.mult),
               reads=[gRGQ, Gn("c_dcyT")], writes=[Gn("c_scrap")])
        em.gop("v", lambda e: e.tensor_tensor(out=P(c_intraT), in0=P(c_scrap), in1=mk(M_TRIU),
                                              op=ALU.mult),
               reads=[Gn("c_scrap"), G("masks")], writes=[Gn("c_intraT")])
        if CORE_STEPS < 3:
            return ctx
        oAT, oKT, oVT = par * 128, 256 + par * 128, 512 + par * 128
        em.gop("t", lambda e, o=oAT: e.transpose(pBF[0:128, o:o + 128], P(c_A), identb[:]),
               reads=[Gn("c_A"), G("identb")], writes=[gBF])
        em.gop("s", lambda e, o=oAT: e.copy(out=P(c_AT), in_=pBF[0:128, o:o + 128]),
               reads=[gBF], writes=[Gn("c_AT")])
        em.gop("t", lambda e, o=oKT: e.transpose(pBF[0:128, o:o + 128], kT, identb[:]),
               reads=[gQKV, G("identb")], writes=[gBF])
        em.gop("t", lambda e, o=oVT: e.transpose(pBF[0:128, o:o + 128], vT, identb[:]),
               reads=[gQKV, G("identb")], writes=[gBF])
        em.gop("v", lambda e, o=oVT: e.tensor_scalar(out=P2(c_X0)[:, 0:128],
                                                     in0=pBF[0:128, o:o + 128],
                                                     scalar1=col(bt), scalar2=None, op0=ALU.mult),
               reads=[gBF, G("bt")], writes=[Gn("c_X0")])
        em.gop("v", lambda e, o=oKT: e.tensor_scalar(out=P2(c_X0)[:, 128:256],
                                                     in0=pBF[0:128, o:o + 128],
                                                     scalar1=col(begjt), scalar2=None,
                                                     op0=ALU.mult),
               reads=[gBF, G("begjt")], writes=[Gn("c_X0")])
        em.gop("v", lambda e, o=oKT: e.tensor_scalar(out=P(c_kdec), in0=pBF[0:128, o:o + 128],
                                                     scalar1=col(eglmt), scalar2=None,
                                                     op0=ALU.mult),
               reads=[gBF, G("eglmt")], writes=[Gn("c_kdec")])
        if CORE_STEPS < 4:
            return ctx
        TB = [P2(c_T0), P2(c_T1)]
        gTB = [Gn("c_T0"), Gn("c_T1")]
        em.gop("v", lambda e: e.tensor_tensor(out=P(c_scrap), in0=P(c_A), in1=mk(M_OD1),
                                              op=ALU.mult),
               reads=[Gn("c_A"), G("masks")], writes=[Gn("c_scrap")])
        em.gop("v", lambda e: e.tensor_tensor(out=TB[0][:, 0:128], in0=P(c_scrap), in1=identf[:],
                                              op=ALU.add),
               reads=[Gn("c_scrap"), G("identf")], writes=[gTB[0]])
        em.gop("v", lambda e: e.tensor_tensor(out=P(c_scrap), in0=P(c_AT), in1=mk(M_ODT[1]),
                                              op=ALU.mult),
               reads=[Gn("c_AT"), G("masks")], writes=[Gn("c_scrap")])
        em.gop("v", lambda e: e.tensor_tensor(out=TB[0][:, 128:256], in0=P(c_scrap), in1=identf[:],
                                              op=ALU.add),
               reads=[Gn("c_scrap"), G("identf")], writes=[gTB[0]])
        cur = 0
        for s in (2, 4, 8, 16, 32, 64):
            nxt = 1 - cur
            Tc, gTc = TB[cur], gTB[cur]
            Tn, gTn = TB[nxt], gTB[nxt]
            em.gop("v", lambda e, s_=s: e.tensor_tensor(out=P(c_aod), in0=P(c_AT),
                                                        in1=mk(M_ODT[s_]), op=ALU.mult),
                   reads=[Gn("c_AT"), G("masks")], writes=[Gn("c_aod")])
            em.gop("t", lambda e, Tc_=Tc: e.matmul(bWX[0:128, 0:128], P(c_aod), Tc_[:, 0:128],
                                                   start=True, stop=True),
                   reads=[Gn("c_aod"), gTc], writes=[gWX])
            em.gop("s", lambda e: e.copy(out=P(c_W), in_=bWX[0:128, 0:128]),
                   reads=[gWX], writes=[Gn("c_W")])
            em.gop("t", lambda e, Tc_=Tc: e.matmul(bLAD[0:128, 0:128], Tc_[:, 128:256], P(c_W),
                                                   start=True, stop=False),
                   reads=[Gn("c_W"), gTc], writes=[gLAD])
            em.gop("t", lambda e, Tc_=Tc: e.matmul(bLAD[0:128, 0:128], identb[:], Tc_[:, 0:128],
                                                   start=False, stop=True),
                   reads=[gTc, G("identb")], writes=[gLAD])
            em.gop("t", lambda e, Tc_=Tc: e.matmul(bLAD[0:128, 128:256], P(c_W), Tc_[:, 128:256],
                                                   start=True, stop=False),
                   reads=[Gn("c_W"), gTc], writes=[gLAD])
            em.gop("t", lambda e, Tc_=Tc: e.matmul(bLAD[0:128, 128:256], identb[:],
                                                   Tc_[:, 128:256], start=False, stop=True),
                   reads=[gTc, G("identb")], writes=[gLAD])
            em.gop("v", lambda e, Tn_=Tn: e.tensor_copy(out=Tn_[:, 0:256],
                                                        in_=bLAD[0:128, 0:256]),
                   reads=[gLAD], writes=[gTn])
            cur = nxt
        if CORE_STEPS < 5:
            return ctx
        Tfin, gTfin = TB[cur], gTB[cur]
        em.gop("t", lambda e, Tf=Tfin: e.matmul(bWX[0:128, 256:512], Tf[:, 128:256], P2(c_X0),
                                                start=True, stop=True),
               reads=[gTfin, Gn("c_X0")], writes=[gWX])
        em.gop("v", lambda e: e.tensor_copy(out=P2(c_Xsb), in_=bWX[0:128, 256:512]),
               reads=[gWX], writes=[Gn("c_Xsb")])
        return ctx

    def core_back(ctx):
        if CORE_STEPS < 6:
            return
        j = ctx["j"]; par = ctx["par"]; hv = ctx["hv"]; c = ctx["c"]; cb0 = ctx["cb0"]
        qT = ctx["qT"]; sil = ctx["sil"]; sthv = ctx["sthv"]; col = ctx["col"]
        P = ctx["P"]; P2 = ctx["P2"]; Gn = ctx["Gn"]
        gQKV = G("qkvT")
        gSil = G("siluz")
        gSt = G(f"st{hv}")
        gNT = G("normedT")
        oKC = 768 + par * 128
        em.gop("t", lambda e, o=oKC: e.transpose(pBF[0:128, o:o + 128], P2(c_Xsb)[:, 128:256],
                                                 identb[:]),
               reads=[Gn("c_Xsb"), G("identb")], writes=[gBF])
        em.gop("s", lambda e, o=oKC: e.copy(out=P(c_kcdT), in_=pBF[0:128, o:o + 128]),
               reads=[gBF], writes=[Gn("c_kcdT")])
        if j > 0:
            em.gop("t", lambda e: e.matmul(pBk[4][0:128, 0:128], P(c_kcdT), sthv,
                                           start=True, stop=True),
                   reads=[Gn("c_kcdT"), gSt], writes=[gB[4]])
            em.gop("t", lambda e: e.matmul(pBk[4][0:128, 128:256], qT, sthv,
                                           start=True, stop=True),
                   reads=[gQKV, gSt], writes=[gB[4]])
            em.gop("v", lambda e: e.tensor_tensor(out=P(c_vnew), in0=P2(c_Xsb)[:, 0:128],
                                                  in1=pBk[4][0:128, 0:128], op=ALU.subtract),
                   reads=[Gn("c_Xsb"), gB[4]], writes=[Gn("c_vnew")])
            em.gop("v", lambda e: e.tensor_scalar(out=P(c_e1), in0=pBk[4][0:128, 128:256],
                                                  scalar1=col(egjt), scalar2=None,
                                                  op0=ALU.mult),
                   reads=[G("egjt"), gB[4]], writes=[Gn("c_e1")])
            em.gop("t", lambda e: e.matmul(pBk[4][0:128, 384:512], P(c_intraT), P(c_vnew),
                                           start=True, stop=True),
                   reads=[Gn("c_intraT"), Gn("c_vnew")], writes=[gB[4]])
            em.gop("v", lambda e: e.scalar_tensor_tensor(out=P(c_xg), in0=pBk[4][0:128, 384:512],
                                                         scalar=1.0, in1=P(c_e1),
                                                         op0=ALU.mult, op1=ALU.add),
                   reads=[gB[4], Gn("c_e1")], writes=[Gn("c_xg")])
        else:
            em.gop("v", lambda e: e.tensor_copy(out=P(c_vnew), in_=P2(c_Xsb)[:, 0:128]),
                   reads=[Gn("c_Xsb")], writes=[Gn("c_vnew")])
            em.gop("t", lambda e: e.matmul(pBk[4][0:128, 384:512], P(c_intraT), P(c_vnew),
                                           start=True, stop=True),
                   reads=[Gn("c_intraT"), Gn("c_vnew")], writes=[gB[4]])
            em.gop("v", lambda e: e.tensor_copy(out=P(c_xg), in_=pBk[4][0:128, 384:512]),
                   reads=[gB[4]], writes=[Gn("c_xg")])
        if CORE_STEPS < 7:
            return
        em.gop("v", lambda e: e.tensor_tensor(out=P(c_xg), in0=P(c_xg), in1=sil, op=ALU.mult),
               reads=[gSil], writes=[Gn("c_xg")])
        em.gop("v", lambda e, cb=cb0: e.scalar_tensor_tensor(out=P(c_scrap), in0=P(c_xg),
                                                             scalar=1.0, in1=P(c_xg),
                                                             op0=ALU.mult, op1=ALU.mult,
                                                             accum_out=c_cols[:, cb + 1:cb + 2]),
               reads=[Gn("c_xg")], writes=[Gn("c_scrap"), G(f"c_ssq{par}")])
        em.gop("s", lambda e, cb=cb0: e.activation(out=c_cols[:, cb + 2:cb + 3],
                                                   in_=c_cols[:, cb + 1:cb + 2], func=AF.Ln,
                                                   scale=1.0 / 128.0, bias=RMS_EPS),
               reads=[G(f"c_ssq{par}")], writes=[G(f"c_lnssq{par}")])
        em.gop("s", lambda e, cb=cb0: e.activation(out=c_cols[:, cb + 3:cb + 4],
                                                   in_=c_cols[:, cb + 2:cb + 3], func=AF.Exp,
                                                   scale=-0.5),
               reads=[G(f"c_lnssq{par}")], writes=[G(f"c_rstd{par}")])
        em.gop("v", lambda e, cb=cb0: e.scalar_tensor_tensor(out=P(c_ntok), in0=P(c_xg),
                                                             scalar=c_cols[:, cb + 3:cb + 4],
                                                             in1=nwbc[:], op0=ALU.mult,
                                                             op1=ALU.mult),
               reads=[Gn("c_xg"), G(f"c_rstd{par}"), G("nwbc")], writes=[Gn("c_ntok")])
        em.gop("t", lambda e, o=oKC: e.transpose(pBF[0:128, o:o + 128], P(c_ntok), identb[:]),
               reads=[Gn("c_ntok"), G("identb")], writes=[gBF])
        em.gop("s", lambda e, o=oKC,
               d=arN5[:, 8192 + hv * AB + c * 128:8192 + hv * AB + (c + 1) * 128]:
               e.copy(out=d, in_=pBF[0:128, o:o + 128]),
               reads=[gBF], writes=[gNT])
        if CORE_STEPS < 8:
            return
        em.gop("t", lambda e: e.matmul(pBk[4][0:128, 256:384], P(c_kdec), P(c_vnew),
                                       start=True, stop=True),
               reads=[Gn("c_kdec"), Gn("c_vnew")], writes=[gB[4]])
        if j > 0:
            em.gop("v", lambda e: e.scalar_tensor_tensor(out=sthv, in0=sthv,
                                                         scalar=col(eglall),
                                                         in1=pBk[4][0:128, 256:384],
                                                         op0=ALU.mult, op1=ALU.add),
                   reads=[gB[4], G("eglall")], writes=[gSt])
        else:
            em.gop("v", lambda e: e.tensor_copy(out=sthv, in_=pBk[4][0:128, 256:384]),
                   reads=[gB[4]], writes=[gSt])

    def phaseA():
        xts = arN3[:, 0:16 * AB]
        gx = G("paxts")
        qkvT = arN1[:, 0:64 * AB]
        gQKV = G("qkvT")
        wA = [arN2[:, 0:8192], arN2[:, 8192:16384]]
        gwA = [G("pawA0"), G("pawA1")]
        mixsb = [arN4[:, 0:AB + 3], arN4[:, 512:512 + AB + 3]]
        gMix = [G("pamix0"), G("pamix1")]
        convacc = arN4[:, 1024:1024 + AB]
        gCv = G("paconv")
        ct0 = arN4[:, 1280:1280 + AB]
        ct1 = arN4[:, 1536:1536 + AB]
        gCt = G("pact")
        xsl = [arN4[:, 2048:4096], arN4[:, 4096:6144]]
        gXsl = [G("paxsl0"), G("paxsl1")]
        outst = [[arN4[:, 6144:6400], arN4[:, 6400:6656]],
                 [arN4[:, 7168:7424], arN4[:, 7424:7680]]]
        gOut = [[G("paout00"), G("paout01")], [G("paout10"), G("paout11")]]
        gbuf = [arN3[:, 4096:6144], arN3[:, 6144:8192]]
        gGbuf = [G("pag0"), G("pag1")]
        siluz = arN5[:, 0:2 * 4096]
        gSil = G("siluz")
        normedT = arN5[:, 8192:8192 + 32 * AB]
        gNT = G("normedT")

        for ab in range(NAB):
            t0 = ab * AB
            xt_load(xts, gx, t0, t0 + AB)
            for m in range(16 if "mix" in SUBA else 0):
                bsl = m % 2
                wblock_load(wA[bsl], gwA[bsl], wqkv_d, 16, m * 512, (m + 1) * 512, f"dwW{bsl}")
                for fl in range(4):
                    f = m * 4 + fl
                    pb, gp = (pBk[0], gB[0]) if fl % 2 == 0 else (pBk[1], gB[1])
                    for k in range(16):
                        em.gop("t", lambda e, kk=k, fl_=fl, b=wA[bsl], p=pb: e.matmul(
                            p[0:128, 0:AB],
                            b[:, kk * 512 + fl_ * 128:kk * 512 + (fl_ + 1) * 128],
                            xts[:, kk * AB:(kk + 1) * AB], start=(kk == 0), stop=(kk == 15)),
                            reads=[gwA[bsl], gx], writes=[gp])
                    ms, gms = mixsb[fl % 2], gMix[fl % 2]
                    em.gop("v", lambda e, p=pb, d=ms: e.tensor_copy(out=d[:, 3:3 + AB],
                                                                    in_=p[0:128, 0:AB]),
                           reads=[gp], writes=[gms])
                    em.gop("v", lambda e, d=ms, f_=f: e.tensor_copy(
                        out=d[:, 0:3], in_=mixstash[:, f_ * 3:(f_ + 1) * 3]),
                        reads=[G("mixstash")], writes=[gms])
                    em.gop("v", lambda e, d=ms, f_=f: e.tensor_copy(
                        out=mixstash[:, f_ * 3:(f_ + 1) * 3], in_=d[:, AB:AB + 3]),
                        reads=[gms], writes=[G("mixstash")])
                    em.gop("s", lambda e, d=ms, f_=f: e.activation(
                        out=ct0, in_=d[:, 0:AB], func=AF.Copy,
                        scale=cw_sb[:, f_ * KC + 0:f_ * KC + 1]),
                        reads=[gms, G("cw_sb")], writes=[gCt])
                    em.gop("s", lambda e, d=ms, f_=f: e.activation(
                        out=ct1, in_=d[:, 1:1 + AB], func=AF.Copy,
                        scale=cw_sb[:, f_ * KC + 1:f_ * KC + 2]),
                        reads=[gms, G("cw_sb")], writes=[gCt])
                    em.gop("v", lambda e: e.tensor_tensor(out=convacc, in0=ct0, in1=ct1,
                                                          op=ALU.add),
                           reads=[gCt], writes=[gCv])
                    em.gop("v", lambda e, d=ms, f_=f: e.scalar_tensor_tensor(
                        out=convacc, in0=d[:, 2:2 + AB],
                        scalar=cw_sb[:, f_ * KC + 2:f_ * KC + 3], in1=convacc,
                        op0=ALU.mult, op1=ALU.add),
                        reads=[gms, G("cw_sb")], writes=[gCv])
                    em.gop("v", lambda e, d=ms, f_=f: e.scalar_tensor_tensor(
                        out=convacc, in0=d[:, 3:3 + AB],
                        scalar=cw_sb[:, f_ * KC + 3:f_ * KC + 4], in1=convacc,
                        op0=ALU.mult, op1=ALU.add),
                        reads=[gms, G("cw_sb")], writes=[gCv])
                    em.gop("s", lambda e, f_=f: e.activation(
                        out=qkvT[:, f_ * AB:(f_ + 1) * AB], in_=convacc, func=AF.Silu),
                        reads=[gCv], writes=[gQKV])
            for f in range(32 if "l2n" in SUBA else 0):
                isq = f < 16
                em.gop("v", lambda e, f_=f: e.tensor_tensor(
                    out=sq_sb[:, 0:AB], in0=qkvT[:, f_ * AB:(f_ + 1) * AB],
                    in1=qkvT[:, f_ * AB:(f_ + 1) * AB], op=ALU.mult),
                    reads=[gQKV], writes=[G("sq_sb")])
                em.gop("t", lambda e: e.matmul(pBk[2][0:1, 0:AB], onescolb[:], sq_sb[:, 0:AB],
                                               start=True, stop=True),
                       reads=[G("sq_sb"), G("onescolb")], writes=[gB[2]])
                em.gop("s", lambda e: e.activation(out=rowsc[0:1, 0:AB], in_=pBk[2][0:1, 0:AB],
                                                   func=AF.Ln, bias=L2_EPS),
                       reads=[gB[2]], writes=[G("rowsc")])
                em.gop("s", lambda e, b_=(-0.5 * LN128 if isq else 0.0): e.activation(
                    out=rowsc[0:1, AB:2 * AB], in_=rowsc[0:1, 0:AB], func=AF.Exp,
                    scale=-0.5, bias=b_),
                    reads=[G("rowsc")], writes=[G("rowsc")])
                em.gop("t", lambda e: e.matmul(pBk[2][0:128, 256:256 + AB], ones1f[:],
                                               rowsc[0:1, AB:2 * AB], start=True, stop=True),
                       reads=[G("rowsc"), G("ones1f")], writes=[gB[2]])
                em.gop("v", lambda e, f_=f: e.tensor_tensor(
                    out=qkvT[:, f_ * AB:(f_ + 1) * AB], in0=qkvT[:, f_ * AB:(f_ + 1) * AB],
                    in1=pBk[2][0:128, 256:256 + AB], op=ALU.mult),
                    reads=[gB[2]], writes=[gQKV])
            for fb in range(8 if "z" in SUBA else 0):
                bsl = fb % 2
                wblock_load(wA[bsl], gwA[bsl], wz_d, 16, fb * 512, (fb + 1) * 512, f"dwW{bsl}")
                for c in range(CPAB):
                    pb, gp = (pBk[3], gB[3]) if c % 2 == 0 else (pBk[4], gB[4])
                    for k in range(16):
                        em.gop("t", lambda e, kk=k, b=wA[bsl], p=pb, c_=c: e.matmul(
                            p[0:128, 0:512], xts[:, kk * AB + c_ * 128:kk * AB + (c_ + 1) * 128],
                            b[:, kk * 512:(kk + 1) * 512], start=(kk == 0), stop=(kk == 15)),
                            reads=[gwA[bsl], gx], writes=[gp])
                    em.gop("s", lambda e, p=pb, c_=c, fb_=fb: e.activation(
                        out=siluz[:, c_ * 4096 + fb_ * 512:c_ * 4096 + (fb_ + 1) * 512],
                        in_=p[0:128, 0:512], func=AF.Silu),
                        reads=[gp], writes=[gSil])
            if "core" in SUBA:
                pend = None
                for c in range(CPAB):
                    for hv in range(NHV):
                        ctx = core_front(ab, c, hv)
                        if pend is not None:
                            core_back(pend)
                        pend = ctx
                core_back(pend)
            for c in range(CPAB):
                dma_y(lambda e, d=xsl[c], s=x_d[t0 + c * 128:t0 + (c + 1) * 128, :]:
                      e.dma_start(out=d, in_=s), f"dxl{c}", writes=[gXsl[c]])
                dma_y(lambda e, d=gbuf[c], s=g_d[t0 + c * 128:t0 + (c + 1) * 128, :]:
                      e.dma_start(out=d, in_=s), f"dxg{c}", writes=[gGbuf[c]])
            for blk in range(8 if "smm" in SUBA else 0):
                bsl = blk % 2
                wblock_load(wA[bsl], gwA[bsl], wout_d, 32, blk * 256, (blk + 1) * 256, f"dwW{bsl}")
                for c in range(CPAB):
                    pb, gp = (pBk[5], gB[5]) if c % 2 == 0 else (pBk[6], gB[6])
                    for kv in range(32):
                        em.gop("t", lambda e, kv_=kv, b=wA[bsl], p=pb, c_=c: e.matmul(
                            p[0:128, 0:256],
                            normedT[:, kv_ * AB + c_ * 128:kv_ * AB + (c_ + 1) * 128],
                            b[:, kv_ * 256:(kv_ + 1) * 256], start=(kv_ == 0), stop=(kv_ == 31)),
                            reads=[gwA[bsl], gNT], writes=[gp])
                    sl = blk % 2
                    osl = outst[c][sl]
                    go = gOut[c][sl]
                    em.gop("v", lambda e, p=pb, blk_=blk, c_=c, o=osl: e.tensor_tensor(
                        out=o, in0=p[0:128, 0:256],
                        in1=gbuf[c_][:, blk_ * 256:(blk_ + 1) * 256], op=ALU.mult),
                        reads=[gp, gGbuf[c]], writes=[go])
                    em.gop("v", lambda e, blk_=blk, c_=c, o=osl: e.tensor_tensor(
                        out=o, in0=o,
                        in1=xsl[c_][:, blk_ * 256:(blk_ + 1) * 256], op=ALU.add),
                        reads=[gXsl[c]], writes=[go])
                    dma_y(lambda e, blk_=blk, c_=c, o=osl, t0_=t0: e.dma_start(
                        out=out_d[t0_ + c_ * 128:t0_ + (c_ + 1) * 128,
                                  blk_ * 256:(blk_ + 1) * 256], in_=o),
                        f"dxo{c}{sl}", reads=[go])

    if 0 in phases:
        phase0()
    em.barrier()
    if 1 in phases:
        phaseB()
    em.barrier()
    if 2 in phases:
        phaseF()
    em.barrier()
    if 3 in phases:
        phaseA()
    em.op("y", lambda e: None,
          waits=[(nm, em.counts[nm]) for nm in sorted(em.dma_sems)], sem=None)

    sems = {}
    for nm in ("t", "v", "s"):
        sems[nm] = es.enter_context(nc.semaphore(f"sem_{nm}"))
    for nm in sorted(em.dma_sems):
        sems[nm] = es.enter_context(nc.semaphore(f"sem_{nm}"))

    with nc.Block() as block:
        @block.tensor
        def _(t):
            em.replay("t", t, sems)

        @block.vector
        def _(v):
            em.replay("v", v, sems)

        @block.scalar
        def _(s):
            em.replay("s", s, sems)

        @block.gpsimd
        def _(g):
            em.replay("g", g, sems)

        @block.sync
        def _(y):
            em.replay("y", y, sems)

    es.close()
    return nc


_NC_CACHE = {}


def kernel(L0_attn_hidden, L0_attn_W_qkv, L0_attn_W_z, L0_attn_W_b, L0_attn_W_a,
           L0_attn_conv_weight, L0_attn_dt_bias, L0_attn_A_log, L0_attn_norm_w,
           L0_attn_W_out, L0_ffn_gate_w, L0_ffn_w1, L0_ffn_w2, L0_ffn_w3,
           L0_ffn_sw1, L0_ffn_sw2, L0_ffn_sw3, L0_ffn_expert_bias):
    import ml_dtypes
    B, T, _ = L0_attn_hidden.shape
    if T not in _NC_CACHE:
        _NC_CACHE[T] = build_nc(T)
    nc = _NC_CACHE[T]
    bfv = lambda a: np.ascontiguousarray(a).astype(ml_dtypes.bfloat16).view(np.uint16)
    f = lambda a: np.ascontiguousarray(np.asarray(a, dtype=np.float32))
    common = {
        "wqkv": f(L0_attn_W_qkv), "wz": f(L0_attn_W_z), "wb": f(L0_attn_W_b),
        "wa": f(L0_attn_W_a), "cw": f(L0_attn_conv_weight), "dtb": f(L0_attn_dt_bias),
        "alog": f(L0_attn_A_log), "nw": f(L0_attn_norm_w), "wout": f(L0_attn_W_out),
        "gatew": f(L0_ffn_gate_w), "w1": f(L0_ffn_w1), "w2": f(L0_ffn_w2),
        "w3": f(L0_ffn_w3), "sw1": f(L0_ffn_sw1), "sw2": f(L0_ffn_sw2),
        "sw3": f(L0_ffn_sw3), "ebias": f(L0_ffn_expert_bias),
        "i_consts": np.tile(np.array([LN_EPS, RMS_EPS, 1.0, -0.5 * LN128, 0.0, 0.0, 0.0, 0.0],
                                     np.float32)[None, :], (128, 1)),
        "i_identb": bfv(np.eye(128, dtype=np.float32)),
        "i_identf": np.eye(128, dtype=np.float32),
        "i_ones1f": np.ones((1, 128), np.float32),
        "i_onescolb": bfv(np.ones((128, 1), np.float32)),
        "i_masks": host_masks(),
        "i_iotaE": np.tile(np.arange(NE, dtype=np.float32)[None, :], (128, 1)),
    }
    in_maps = []
    for b in range(B):
        m = dict(common)
        m["x"] = f(L0_attn_hidden[b])
        in_maps.append(m)
    res = run_bass_kernel_spmd(nc, in_maps, core_ids=list(range(B)))
    out = np.stack([np.asarray(res.results[i]["out"]) for i in range(B)], axis=0)
    return out.astype(np.float32)


# version 05
# speedup vs baseline: 1.6962x; 1.6801x over previous
from collections import defaultdict
from contextlib import ExitStack

import numpy as np
import concourse.bass as bass
import concourse.mybir as mybir
from concourse.bass_utils import run_bass_kernel_spmd

F32 = mybir.dt.float32
BF16 = mybir.dt.bfloat16
AF = mybir.ActivationFunctionType
ALU = mybir.AluOpType
AX = mybir.AxisListType

D = 2048
CONVD = 8192
KEY = 2048
VAL = 4096
HK, HV = 16, 32
NE, HID = 8, 1792
KC = 4
CH = 128
AB = 256
LN_EPS = 1e-5
L2_EPS = 1e-6
RMS_EPS = 1e-6
LN128 = float(np.log(128.0))
SUBA = {"mix", "l2n", "z", "core", "smm"}
CORE_STEPS = 99
NHV = HV
ST8 = "all"

M_STRICT = 0
M_TRIU = 1
M_ODT = {s: 2 + i for i, s in enumerate((1, 2, 4, 8, 16, 32, 64))}
M_OD1 = 9
M_ONES = 10
NMASK = 11


class Guard:
    __slots__ = ("w", "r")

    def __init__(self):
        self.w = []
        self.r = []

    def rd(self):
        return list(self.w)

    def wr(self):
        return list(self.w) + list(self.r)

    def setw(self, nm, val):
        self.w = [(nm, val)]
        self.r = []

    def addr(self, nm, val):
        self.r.append((nm, val))
        if len(self.r) > 24:
            best = {}
            for nm2, v2 in self.r:
                best[nm2] = max(best.get(nm2, 0), v2)
            self.r = sorted(best.items())


class Emitter:
    STREAMS = ("t", "v", "s", "g", "y")
    ENG_SEM = {"t": "t", "v": "v", "s": "s"}
    DROP_OWN = False

    def __init__(self):
        self.ops = {k: [] for k in self.STREAMS}
        self.counts = defaultdict(int)
        self.pending = {k: [] for k in self.STREAMS}
        self.dma_sems = set()
        self.hiwater = {k: defaultdict(int) for k in self.STREAMS}

    def op(self, stream, build, waits=(), sem=None, amt=1):
        allw = list(waits) + self.pending[stream]
        self.pending[stream] = []
        ws = {}
        own = self.ENG_SEM.get(stream)
        hw = self.hiwater[stream]
        for nm, vv in allw:
            if vv <= 0 or (self.DROP_OWN and nm == own) or vv <= hw[nm]:
                continue
            ws[nm] = max(ws.get(nm, 0), vv)
        for nm, vv in ws.items():
            assert vv <= self.counts[nm], (stream, nm, vv, self.counts[nm])
            hw[nm] = vv
        self.ops[stream].append((sorted(ws.items()), build, sem, amt))
        if sem is not None:
            self.counts[sem] += amt
            return self.counts[sem]
        return None

    def gop(self, stream, build, reads=(), writes=(), extra=(), sem=None):
        if sem is None:
            sem = self.ENG_SEM[stream]
            amt = 1
        else:
            amt = 16
            self.dma_sems.add(sem)
        waits = []
        for gd in writes:
            waits += gd.wr()
        for gd in reads:
            waits += gd.rd()
        waits += list(extra)
        val = self.op(stream, build, waits, sem=sem, amt=amt)
        for gd in writes:
            gd.setw(sem, val)
        for gd in reads:
            gd.addr(sem, val)
        return val

    def group(self, stream, builds, reads=(), writes=(), sem=None):
        if sem is None:
            sem = self.ENG_SEM[stream]
            amt = 1
        else:
            amt = 16
            self.dma_sems.add(sem)
        waits = []
        for gd in writes:
            waits += gd.wr()
        for gd in reads:
            waits += gd.rd()
        n = len(builds)
        for i, b in enumerate(builds):
            self.op(stream, b, waits if i == 0 else (),
                    sem=sem if i == n - 1 else None, amt=amt)
        val = self.counts[sem]
        for gd in writes:
            gd.setw(sem, val)
        for gd in reads:
            gd.addr(sem, val)
        return val

    def barrier(self):
        snap = [(nm, v) for nm, v in self.counts.items() if v > 0]
        for st in self.STREAMS:
            self.pending[st] += snap

    def replay(self, stream, eng, sems):
        for waits, build, incname, amt in self.ops[stream]:
            for nm, v in waits:
                eng.wait_ge(sems[nm], v)
            inst = build(eng)
            if incname is not None:
                inst.then_inc(sems[incname], amt)


def host_masks():
    n = 128
    i = np.arange(n)
    m = np.zeros((n, 128 * NMASK), np.float32)

    def blk(idx):
        return slice(128 * idx, 128 * (idx + 1))

    m[:, blk(M_STRICT)] = (i[:, None] > i[None, :])
    m[:, blk(M_TRIU)] = (i[None, :] >= i[:, None])
    for s, idx in M_ODT.items():
        od = ((i[:, None] // s) % 2 == 1) & ((i[None, :] // s) % 2 == 0) \
            & ((i[:, None] // (2 * s)) == (i[None, :] // (2 * s)))
        m[:, blk(idx)] = od.T
    s = 1
    od1 = ((i[:, None] // s) % 2 == 1) & ((i[None, :] // s) % 2 == 0) \
        & ((i[:, None] // (2 * s)) == (i[None, :] // (2 * s)))
    m[:, blk(M_OD1)] = od1
    m[:, blk(M_ONES)] = 1.0
    return m


def build_nc(T=2048, phases=(0, 1, 2, 3)):
    TTL = T // 128
    NAB = T // AB
    TBF = min(512, T)
    NFB = T // TBF
    NTB = TBF // 128
    CPAB = AB // CH

    nc = bass.Bass("TRN2", target_bir_lowering=False)

    def din(name, shape, dt=F32):
        return nc.dram_tensor(name, shape, dt, kind="ExternalInput").ap()

    x_d = din("x", [T, D])
    wqkv_d = din("wqkv_b", [16, 128, 16 * 512])
    wz_d = din("wz_b", [8, 128, 16 * 512])
    wb_d = din("wb", [D, HV])
    wa_d = din("wa", [D, HV])
    cw_d = din("cw", [CONVD, KC])
    dtb_d = din("dtb", [HV])
    alog_d = din("alog", [HV])
    nw_d = din("nw", [128])
    wout_d = din("wout_b", [8, 128, 32 * 256])
    gatew_d = din("gatew", [D, NE])
    w1_d = din("w1_b", [NE, 7, 128, 16 * 256])
    w2_d = din("w2_b", [NE, 4, 128, 14 * 512])
    w3_d = din("w3_b", [NE, 7, 128, 16 * 256])
    sw1_d = din("sw1_b", [7, 128, 16 * 256])
    sw2_d = din("sw2_b", [4, 128, 14 * 512])
    sw3_d = din("sw3_b", [7, 128, 16 * 256])
    ebias_d = din("ebias", [NE])
    consts_d = din("i_consts", [128, 8], F32)
    identb_d = din("i_identb", [128, 128], BF16)
    identf_d = din("i_identf", [128, 128], F32)
    ones1f_d = din("i_ones1f", [1, 128], F32)
    onescolb_d = din("i_onescolb", [128, 1], BF16)
    masks_d = din("i_masks", [128, 128 * NMASK], F32)
    iotaE_d = din("i_iotaE", [128, NE], F32)

    out_d = nc.dram_tensor("out", [T, D], F32, kind="ExternalOutput").ap()
    xnT_d = nc.dram_tensor("xnT", [16, 128, T], BF16, kind="Internal").ap()
    g_d = nc.dram_tensor("gd", [T, D], BF16, kind="Internal").ap()

    es = ExitStack()

    def sb(name, shape, dt):
        return es.enter_context(nc.sbuf_tensor(name, shape, dt))

    constcol = sb("constcol", [128, 8], F32)
    identb = sb("identb", [128, 128], BF16)
    identf = sb("identf", [128, 128], F32)
    ones1f = sb("ones1f", [1, 128], F32)
    onescolb = sb("onescolb", [128, 1], BF16)
    masks = sb("masks", [128, 128 * NMASK], F32)
    iotaE = sb("iotaE", [128, NE], F32)
    nwbc = sb("nwbc", [128, 128], F32)
    ebbc = sb("ebbc", [128, NE], F32)
    nwrow = sb("nwrow", [1, 128], F32)
    ebrow = sb("ebrow", [1, NE], F32)
    gatew_sb = sb("gatew_sb", [128, 16 * NE], F32)
    wb_sb = sb("wb_sb", [128, 16 * HV], BF16)
    wa_sb = sb("wa_sb", [128, 16 * HV], BF16)
    cw_sb = sb("cw_sb", [128, 64 * KC], F32)
    dtb_sb = sb("dtb_sb", [HV, 1], F32)
    alog_sb = sb("alog_sb", [HV, 1], F32)
    negea_sb = sb("negea_sb", [HV, 1], F32)
    st_bf = sb("st_bf", [128, HV * 128], BF16)
    wexp = sb("wexp", [128, TTL * NE], F32)
    mixstash = sb("mixstash", [128, 64 * (KC - 1)], F32)
    gjt = sb("gjt", [128, TTL * HV], F32)
    bt = sb("bt", [128, TTL * HV], F32)
    negbt = sb("negbt", [128, TTL * HV], F32)
    egjt = sb("egjt", [128, TTL * HV], F32)
    eglmt = sb("eglmt", [128, TTL * HV], F32)
    begjt = sb("begjt", [128, TTL * HV], F32)
    eglall = sb("eglall", [128, TTL * HV], F32)
    colsc = sb("colsc", [128, 16], F32)
    rowsc = sb("rowsc", [1, 512], F32)
    hmA = sb("hmA", [HV, TBF], F32)
    hmB = sb("hmB", [HV, TBF], F32)
    hmC = sb("hmC", [HV, TBF], F32)
    onesHM = sb("onesHM", [HV, CH], F32)
    gsm = sb("gsm", [128, 10 * NE], F32)
    sq_sb = sb("sq_sb", [128, AB], BF16)

    arN1 = sb("arN1", [128, 16384], BF16)
    arN2 = sb("arN2", [128, 16384], BF16)
    arN3 = sb("arN3", [128, 8192], BF16)
    arN4 = sb("arN4", [128, 8192], F32)
    arN5 = sb("arN5", [128, 16384], BF16)

    c_cbc = sb("c_cbc", [128, 256], F32)
    c_e1 = sb("c_e1", [128, 256], F32)
    c_e2 = sb("c_e2", [128, 256], F32)
    c_dcy = sb("c_dcy", [128, 256], F32)
    c_dcyT = sb("c_dcyT", [128, 256], F32)
    c_xg = sb("c_xg", [128, 256], F32)
    c_A = sb("c_A", [128, 256], BF16)
    c_AT = sb("c_AT", [128, 256], BF16)
    c_aod = sb("c_aod", [128, 256], BF16)
    c_W = sb("c_W", [128, 256], BF16)
    c_intraT = sb("c_intraT", [128, 256], BF16)
    c_kdec = sb("c_kdec", [128, 256], BF16)
    c_vnew = sb("c_vnew", [128, 256], BF16)
    c_X0 = sb("c_X0", [128, 512], BF16)
    c_Xsb = sb("c_Xsb", [128, 512], BF16)
    c_T0 = sb("c_T0", [128, 512], BF16)
    c_T1 = sb("c_T1", [128, 512], BF16)
    c_ntok = sb("c_ntok", [128, 256], BF16)
    c_scrap = sb("c_scrap", [128, 256], BF16)
    c_kcdT = sb("c_kcdT", [128, 256], BF16)
    c_cols = sb("c_cols", [128, 8], F32)

    pBk = [es.enter_context(nc.psum_tensor(f"pb{i}", [128, 512], F32)) for i in range(7)]
    pBF = es.enter_context(nc.psum_tensor("pbf", [128, 1024], BF16))

    em = Emitter()
    gB = [Guard() for _ in range(7)]
    gBF = Guard()
    _g = defaultdict(Guard)

    def G(nm):
        return _g[nm]

    def dma_y(build, sem, reads=(), writes=()):
        return em.gop("y", build, reads=reads, writes=writes, sem=sem)

    def dma_g(build, sem, reads=(), writes=()):
        return em.gop("g", build, reads=reads, writes=writes, sem=sem)

    CONST_VALS = (LN_EPS, RMS_EPS, 1.0, -0.5 * LN128, 0.0)
    for ci, cv in enumerate(CONST_VALS):
        nc.const_aps.register(F32, cv, constcol[:, ci:ci + 1])
    dma_y(lambda e, d=constcol, s=consts_d: e.dma_start(out=d[:], in_=s), "dxi", writes=[G("constcol")])
    dma_y(lambda e, d=identb, s=identb_d: e.dma_start(out=d[:], in_=s), "dxi", writes=[G("identb")])
    dma_y(lambda e, d=identf, s=identf_d: e.dma_start(out=d[:], in_=s), "dxi", writes=[G("identf")])
    dma_y(lambda e, d=ones1f, s=ones1f_d: e.dma_start(out=d[:], in_=s), "dxi", writes=[G("ones1f")])
    dma_y(lambda e, d=onescolb, s=onescolb_d: e.dma_start(out=d[:], in_=s), "dxi", writes=[G("onescolb")])
    dma_y(lambda e, d=masks, s=masks_d: e.dma_start(out=d[:], in_=s), "dxi", writes=[G("masks")])
    dma_y(lambda e, d=iotaE, s=iotaE_d: e.dma_start(out=d[:], in_=s), "dxi", writes=[G("iotaE")])
    dma_y(lambda e, d=nwrow, s=nw_d: e.dma_start(out=d[:], in_=s.rearrange("(o f) -> o f", o=1)),
          "dxi", writes=[G("nwrow")])
    dma_y(lambda e, d=ebrow, s=ebias_d: e.dma_start(out=d[:], in_=s.rearrange("(o f) -> o f", o=1)),
          "dxi", writes=[G("ebrow")])
    dma_y(lambda e, d=gatew_sb, s=gatew_d: e.dma_start(
        out=d[:].rearrange("p (k f) -> p k f", k=16),
        in_=s.rearrange("(k p) f -> p k f", p=128)), "dxi", writes=[G("gatew_sb")])
    dma_g(lambda e, d=wb_sb, s=wb_d: e.dma_start(
        out=d[:].rearrange("p (k f) -> p k f", k=16),
        in_=s.rearrange("(k p) f -> p k f", p=128)), "dwi", writes=[G("wb_sb")])
    dma_g(lambda e, d=wa_sb, s=wa_d: e.dma_start(
        out=d[:].rearrange("p (k f) -> p k f", k=16),
        in_=s.rearrange("(k p) f -> p k f", p=128)), "dwi", writes=[G("wa_sb")])
    dma_y(lambda e, d=cw_sb, s=cw_d: e.dma_start(
        out=d[:].rearrange("p (k f) -> p k f", k=64),
        in_=s.rearrange("(k p) f -> p k f", p=128)), "dxi", writes=[G("cw_sb")])
    dma_y(lambda e, d=dtb_sb, s=dtb_d: e.dma_start(out=d[:], in_=s.rearrange("(p o) -> p o", o=1)),
          "dxi", writes=[G("dtb_sb")])
    dma_y(lambda e, d=alog_sb, s=alog_d: e.dma_start(out=d[:], in_=s.rearrange("(p o) -> p o", o=1)),
          "dxi", writes=[G("alog_sb")])

    em.barrier()
    em.gop("t", lambda e: e.matmul(pBk[5][0:128, 0:128], ones1f[:], nwrow[:], start=True, stop=True),
           reads=[G("ones1f"), G("nwrow")], writes=[gB[5]])
    em.gop("s", lambda e: e.copy(out=nwbc[:], in_=pBk[5][0:128, 0:128]),
           reads=[gB[5]], writes=[G("nwbc")])
    em.gop("t", lambda e: e.matmul(pBk[5][0:128, 128:128 + NE], ones1f[:], ebrow[:],
                                   start=True, stop=True),
           reads=[G("ones1f"), G("ebrow")], writes=[gB[5]])
    em.gop("s", lambda e: e.copy(out=ebbc[:], in_=pBk[5][0:128, 128:128 + NE]),
           reads=[gB[5]], writes=[G("ebbc")])
    em.gop("s", lambda e: e.activation(out=negea_sb[:], in_=alog_sb[:], func=AF.Exp),
           reads=[G("alog_sb")], writes=[G("negea_sb")])
    em.gop("v", lambda e: e.tensor_scalar(out=negea_sb[:], in0=negea_sb[:], scalar1=-1.0,
                                          scalar2=None, op0=ALU.mult),
           reads=[], writes=[G("negea_sb")])
    em.gop("v", lambda e: e.memset(onesHM[:], 1.0), writes=[G("onesHM")])
    em.gop("v", lambda e: e.memset(st_bf[:], 0.0), writes=[G("st_bf")])
    em.gop("v", lambda e: e.memset(mixstash[:], 0.0), writes=[G("mixstash")])
    em.barrier()

    def phase0():
        xa = [arN4[:, 0:2048], arN4[:, 2048:4096]]
        gXa = [G("p0x0"), G("p0x1")]
        xc = arN4[:, 4096:6144]
        xn = arN4[:, 6144:8192]
        gXc, gXn = G("p0xc"), G("p0xn")
        for tt in range(TTL):
            xab, gx = xa[tt % 2], gXa[tt % 2]
            dma_y(lambda e, d=xab, s=x_d[tt * 128:(tt + 1) * 128, :]: e.dma_start(out=d, in_=s),
                  f"dxp0{tt % 2}", writes=[gx])
            em.gop("v", lambda e, s=xab: e.reduce_sum(out=colsc[:, 0:1], in_=s, axis=AX.X),
                   reads=[gx], writes=[G("c0")])
            em.gop("v", lambda e: e.tensor_scalar(out=colsc[:, 1:2], in0=colsc[:, 0:1],
                                                  scalar1=1.0 / D, scalar2=None, op0=ALU.mult),
                   reads=[G("c0")], writes=[G("c1")])
            em.gop("v", lambda e, s=xab: e.tensor_scalar(out=xc, in0=s, scalar1=colsc[:, 1:2],
                                                         scalar2=None, op0=ALU.subtract),
                   reads=[gx, G("c1")], writes=[gXc])
            em.gop("v", lambda e: e.scalar_tensor_tensor(out=xn, in0=xc, scalar=1.0, in1=xc,
                                                         op0=ALU.mult, op1=ALU.mult,
                                                         accum_out=colsc[:, 2:3]),
                   reads=[gXc], writes=[gXn, G("c2")])
            em.gop("s", lambda e: e.activation(out=colsc[:, 3:4], in_=colsc[:, 2:3], func=AF.Ln,
                                               scale=1.0 / D, bias=LN_EPS),
                   reads=[G("c2")], writes=[G("c3")])
            em.gop("s", lambda e: e.activation(out=colsc[:, 4:5], in_=colsc[:, 3:4], func=AF.Exp,
                                               scale=-0.5),
                   reads=[G("c3")], writes=[G("c4")])
            em.gop("v", lambda e: e.tensor_scalar(out=xn, in0=xc, scalar1=colsc[:, 4:5],
                                                  scalar2=None, op0=ALU.mult),
                   reads=[gXc, G("c4")], writes=[gXn])
            for k in range(16):
                stf = xc[:, 128 * (k % 2):128 * (k % 2 + 1)]
                stb = arN5[:, 128 * (k % 2):128 * (k % 2 + 1)]
                gstf, gstb = G(f"p0sf{k % 2}"), G(f"p0sb{k % 2}")
                em.gop("t", lambda e, kk=k: e.transpose(pBk[0][0:128, 0:128],
                                                        xn[:, kk * 128:(kk + 1) * 128], identf[:]),
                       reads=[gXn, G("identf")], writes=[gB[0]])
                em.gop("s", lambda e, d=stf: e.copy(out=d, in_=pBk[0][0:128, 0:128]),
                       reads=[gB[0]], writes=[gstf, gXc])
                em.gop("t", lambda e, d=stf, kk=k: e.matmul(
                    pBk[1][0:128, 0:NE], d, gatew_sb[:, kk * NE:(kk + 1) * NE],
                    start=(kk == 0), stop=(kk == 15)),
                    reads=[gstf, G("gatew_sb")], writes=[gB[1]])
                em.gop("v", lambda e, d=stb, s2=stf: e.tensor_copy(out=d, in_=s2),
                       reads=[gstf], writes=[gstb])
                dma_y(lambda e, s2=stb, d=xnT_d[k, :, tt * 128:(tt + 1) * 128]:
                      e.dma_start(out=d, in_=s2), f"dxst{k % 2}", reads=[gstb])
            gl = gsm[:, 0:NE]; e0 = gsm[:, NE:2 * NE]; t1 = gsm[:, 2 * NE:3 * NE]
            sc = gsm[:, 3 * NE:4 * NE]; bsd = gsm[:, 4 * NE:5 * NE]
            eqv = gsm[:, 5 * NE:6 * NE]; cnd = gsm[:, 6 * NE:7 * NE]
            msk1 = gsm[:, 7 * NE:8 * NE]; msk2 = gsm[:, 8 * NE:9 * NE]
            l0 = gsm[:, 9 * NE:10 * NE]
            gg = G("gsm")
            em.gop("v", lambda e: e.tensor_copy(out=gl, in_=pBk[1][0:128, 0:NE]),
                   reads=[gB[1]], writes=[gg])
            em.gop("v", lambda e: e.tensor_reduce(out=colsc[:, 5:6], in_=gl, axis=AX.X,
                                                  op=ALU.max, negate=True),
                   reads=[gg], writes=[G("c5")])
            em.gop("s", lambda e: e.activation(out=e0, in_=gl, func=AF.Exp, bias=colsc[:, 5:6]),
                   reads=[gg, G("c5")], writes=[gg])
            em.gop("s", lambda e: e.activation(out=l0, in_=e0, func=AF.Ln),
                   reads=[gg], writes=[gg])
            em.gop("v", lambda e: e.tensor_scalar(out=t1, in0=gl, scalar1=colsc[:, 5:6],
                                                  scalar2=None, op0=ALU.add),
                   reads=[gg, G("c5")], writes=[gg])
            em.gop("v", lambda e: e.tensor_tensor(out=t1, in0=t1, in1=l0, op=ALU.subtract),
                   reads=[gg], writes=[gg])
            em.gop("v", lambda e: e.tensor_scalar(out=t1, in0=t1, scalar1=1.0, scalar2=None,
                                                  op0=ALU.add), reads=[gg], writes=[gg])
            em.gop("v", lambda e: e.tensor_tensor(out=e0, in0=e0, in1=t1, op=ALU.mult),
                   reads=[gg], writes=[gg])
            em.gop("v", lambda e: e.reduce_sum(out=colsc[:, 6:7], in_=e0, axis=AX.X),
                   reads=[gg], writes=[G("c6")])
            em.gop("v", lambda e: e.reciprocal(out=colsc[:, 7:8], in_=colsc[:, 6:7]),
                   reads=[G("c6")], writes=[G("c7")])
            em.gop("v", lambda e: e.tensor_scalar(out=sc, in0=e0, scalar1=colsc[:, 7:8],
                                                  scalar2=None, op0=ALU.mult),
                   reads=[gg, G("c7")], writes=[gg])
            em.gop("v", lambda e: e.tensor_tensor(out=bsd, in0=sc, in1=ebbc[:, 0:NE], op=ALU.add),
                   reads=[gg, G("ebbc")], writes=[gg])
            for r, (mskT, c_a, c_b) in enumerate(((msk1, 8, 9), (msk2, 10, 11))):
                em.gop("v", lambda e: e.tensor_reduce(out=colsc[:, 12:13], in_=bsd, axis=AX.X,
                                                      op=ALU.max),
                       reads=[gg], writes=[G("c12")])
                em.gop("v", lambda e: e.tensor_scalar(out=eqv, in0=bsd, scalar1=colsc[:, 12:13],
                                                      scalar2=None, op0=ALU.is_ge),
                       reads=[gg, G("c12")], writes=[gg])
                em.gop("v", lambda e: e.scalar_tensor_tensor(out=cnd, in0=eqv, scalar=-1000.0,
                                                             in1=iotaE[:, 0:NE], op0=ALU.mult,
                                                             op1=ALU.add),
                       reads=[gg, G("iotaE")], writes=[gg])
                em.gop("v", lambda e, ca=c_a: e.tensor_reduce(out=colsc[:, ca:ca + 1], in_=cnd,
                                                              axis=AX.X, op=ALU.min),
                       reads=[gg], writes=[G("c8")])
                em.gop("v", lambda e, ca=c_a, cb=c_b: e.tensor_scalar(
                    out=colsc[:, cb:cb + 1], in0=colsc[:, ca:ca + 1], scalar1=1000.0,
                    scalar2=None, op0=ALU.add), reads=[G("c8")], writes=[G("c9")])
                em.gop("v", lambda e, m=mskT, cb=c_b: e.tensor_scalar(
                    out=m, in0=iotaE[:, 0:NE], scalar1=colsc[:, cb:cb + 1], scalar2=None,
                    op0=ALU.is_equal), reads=[G("iotaE"), G("c9")], writes=[gg])
                if r == 0:
                    em.gop("v", lambda e: e.scalar_tensor_tensor(out=bsd, in0=msk1, scalar=-1e30,
                                                                 in1=bsd, op0=ALU.mult,
                                                                 op1=ALU.add),
                           reads=[gg], writes=[gg])
            em.gop("v", lambda e: e.tensor_tensor(out=msk1, in0=msk1, in1=msk2, op=ALU.add),
                   reads=[gg], writes=[gg])
            em.gop("v", lambda e, tt_=tt: e.tensor_tensor(out=wexp[:, tt_ * NE:(tt_ + 1) * NE],
                                                          in0=sc, in1=msk1, op=ALU.mult),
                   reads=[gg], writes=[G("wexp")])

    def xt_load(dst, gdst, c0, c1, nseg=4, extra_reads=()):
        w = c1 - c0
        per = 16 // nseg
        for i in range(nseg):
            dma_y(lambda e, i_=i, d=dst, w_=w, p_=per: e.dma_start(
                out=d[:, i_ * p_ * w_:(i_ + 1) * p_ * w_].rearrange("p (k c) -> p k c", k=p_),
                in_=xnT_d[i_ * p_:(i_ + 1) * p_, :, c0:c1].rearrange("k p c -> p k c")),
                "dxt", reads=list(extra_reads), writes=[gdst])

    def wblock_load(dst_ap, gdst, srcblk, b, width, sem):
        gds = list(gdst) if isinstance(gdst, (list, tuple)) else [gdst]
        dma_g(lambda e, d=dst_ap, s=srcblk[b]: e.dma_start(out=d[:, 0:width], in_=s),
              sem, writes=gds)

    def phaseB():
        for tb in range(NFB):
            W = TBF
            c0 = tb * W
            xts = arN3[:, 0:16 * W]
            gx = G("pbxts")
            xt_load(xts, gx, c0, c0 + W)
            for (wsb, gnm, po) in ((wb_sb, "wb_sb", 0), (wa_sb, "wa_sb", 32)):
                em.group("t", [
                    (lambda e, kk=k, ws=wsb, po_=po, W_=W: e.matmul(
                        pBk[2][po_:po_ + 32, 0:W_], ws[:, kk * HV:kk * HV + HV],
                        xts[:, kk * W_:(kk + 1) * W_], start=(kk == 0), stop=(kk == 15)))
                    for k in range(16)],
                    reads=[gx, G(gnm)], writes=[gB[2]])
            em.gop("s", lambda e, W_=W: e.activation(out=hmA[:, 0:W_], in_=pBk[2][0:32, 0:W_],
                                                     func=AF.Sigmoid),
                   reads=[gB[2]], writes=[G("hmA")])
            em.gop("s", lambda e, W_=W: e.activation(out=hmB[:, 0:W_], in_=pBk[2][32:64, 0:W_],
                                                     func=AF.Exp, bias=dtb_sb[0:32, :]),
                   reads=[gB[2], G("dtb_sb")], writes=[G("hmB")])
            em.gop("s", lambda e, W_=W: e.activation(out=hmC[:, 0:W_], in_=hmB[:, 0:W_],
                                                     func=AF.Ln, bias=1.0),
                   reads=[G("hmB")], writes=[G("hmC")])
            em.gop("v", lambda e, W_=W: e.tensor_scalar(out=hmB[:, 0:W_], in0=hmC[:, 0:W_],
                                                        scalar1=negea_sb[0:32, :], scalar2=None,
                                                        op0=ALU.mult),
                   reads=[G("hmC"), G("negea_sb")], writes=[G("hmB")])
            ncpb = W // CH
            for c in range(ncpb):
                em.gop("v", lambda e, cc=c: e.tensor_tensor_scan(
                    out=hmC[:, cc * CH:(cc + 1) * CH], data0=onesHM[:, 0:CH],
                    data1=hmB[:, cc * CH:(cc + 1) * CH], initial=0.0,
                    op0=ALU.mult, op1=ALU.add),
                    reads=[G("hmB"), G("onesHM")], writes=[G("hmC")])
            for c in range(ncpb):
                em.gop("v", lambda e, cc=c: e.tensor_scalar(
                    out=hmB[:, cc * CH:(cc + 1) * CH], in0=hmC[:, cc * CH:(cc + 1) * CH],
                    scalar1=hmC[:, cc * CH + CH - 1:cc * CH + CH], scalar2=-1.0,
                    op0=ALU.subtract, op1=ALU.mult),
                    reads=[G("hmC")], writes=[G("hmB")])
            em.gop("s", lambda e, W_=W: e.activation(out=hmB[:, 0:W_], in_=hmB[:, 0:W_],
                                                     func=AF.Exp),
                   reads=[G("hmB")], writes=[G("hmB")])
            for c in range(ncpb):
                tt = tb * ncpb + c
                cs = slice(tt * HV, (tt + 1) * HV)
                for (src, gs, dst, gd, ofs) in (
                        (hmA, "hmA", bt, "bt", 0),
                        (hmC, "hmC", gjt, "gjt", 128),
                        (hmB, "hmB", eglmt, "eglmt", 256)):
                    em.gop("t", lambda e, s2=src, cc=c, o=ofs: e.transpose(
                        pBk[0][0:128, o:o + HV], s2[:, cc * CH:(cc + 1) * CH],
                        identf[0:HV, 0:HV]),
                        reads=[G(gs), G("identf")], writes=[gB[0]])
                    em.gop("s", lambda e, d=dst, cs_=cs, o=ofs: e.copy(
                        out=d[:, cs_], in_=pBk[0][0:128, o:o + HV]),
                        reads=[gB[0]], writes=[G(gd)])
                em.gop("t", lambda e, cc=c: e.transpose(
                    pBk[0][0:1, 384:384 + HV],
                    hmC[:, cc * CH + CH - 1:cc * CH + CH], identf[0:HV, 0:HV]),
                    reads=[G("hmC"), G("identf")], writes=[gB[0]])
                em.gop("s", lambda e: e.copy(out=rowsc[0:1, 256:256 + HV],
                                             in_=pBk[0][0:1, 384:384 + HV]),
                       reads=[gB[0]], writes=[G("rowsc")])
                em.gop("t", lambda e: e.matmul(pBk[0][0:128, 384:384 + HV], ones1f[:],
                                               rowsc[0:1, 256:256 + HV],
                                               start=True, stop=True),
                       reads=[G("rowsc"), G("ones1f")], writes=[gB[0]])
                em.gop("s", lambda e, cs_=cs: e.activation(out=eglall[:, cs_],
                                                           in_=pBk[0][0:128, 384:384 + HV],
                                                           func=AF.Exp),
                       reads=[gB[0]], writes=[G("eglall")])
                em.gop("v", lambda e, cs_=cs: e.tensor_scalar(out=negbt[:, cs_], in0=bt[:, cs_],
                                                              scalar1=-1.0, scalar2=None,
                                                              op0=ALU.mult),
                       reads=[G("bt")], writes=[G("negbt")])
                em.gop("s", lambda e, cs_=cs: e.activation(out=egjt[:, cs_], in_=gjt[:, cs_],
                                                           func=AF.Exp),
                       reads=[G("gjt")], writes=[G("egjt")])
                em.gop("v", lambda e, cs_=cs: e.tensor_tensor(out=begjt[:, cs_], in0=bt[:, cs_],
                                                              in1=egjt[:, cs_], op=ALU.mult),
                       reads=[G("bt"), G("egjt")], writes=[G("begjt")])

    def phaseF():
        W = TBF
        NT = W // 128
        assert 16 * W <= 8192
        g_acc = arN4[:, 0:16 * W]
        gGacc = G("gacc")
        xts = arN3[:, 0:16 * W]
        gx = G("pfxts")
        Xe = arN1[:, 0:16 * W]
        hbuf = arN1[:, 8192:8192 + 14 * W]
        h1a = arN1[:, 8192 + 14 * W:8192 + 15 * W]
        gXe, gH, gH1 = G("Xe"), G("hbuf"), G("h1a")
        wbc = [arN5[:, 0:W], arN5[:, W:2 * W]]
        gtokst = arN5[:, 2048:4096]
        gWbc = [G("wbc0"), G("wbc1")]
        gGt = G("gtokst")
        Xe2 = [Xe, arN5[:, 4096:4096 + 16 * W]]
        gXe2 = [gXe, G("Xe1")]
        wA = [arN2[:, 0:4096], arN2[:, 4096:8192]]
        wB = [arN2[:, 8192:12288], arN2[:, 12288:16384]]
        w2bl = [arN2[:, 0:7168], arN2[:, 8192:8192 + 7168]]
        gwA = [G("wA0"), G("wA1")]
        gwB = [G("wB0"), G("wB1")]
        gw2 = [G("w20"), G("w21")]
        NFG = HID // 256

        for tb in range(NFB):
            c0 = tb * W
            xt_load(xts, gx, c0, c0 + W)
            em.gop("v", lambda e, d=g_acc: e.memset(d, 0.0), reads=[], writes=[gGacc])
            def emit_xe(ei2):
                par = ei2 % 2
                for i in range(NT):
                    tt = tb * NT + i
                    em.gop("v", lambda e, tt_=tt, ei_=ei2: e.tensor_scalar(
                        out=c_cbc[:, 0:128], in0=masks[:, 128 * M_ONES:128 * (M_ONES + 1)],
                        scalar1=wexp[:, tt_ * NE + ei_:tt_ * NE + ei_ + 1],
                        scalar2=None, op0=ALU.mult),
                        reads=[G("masks"), G("wexp")], writes=[G("c_cbc")])
                    em.gop("t", lambda e: e.transpose(pBk[5][0:128, 0:128], c_cbc[:, 0:128],
                                                      identf[:]),
                           reads=[G("c_cbc"), G("identf")], writes=[gB[5]])
                    em.gop("s", lambda e, i_=i, w_=wbc[par]: e.copy(
                        out=w_[:, i_ * 128:(i_ + 1) * 128], in_=pBk[5][0:128, 0:128]),
                           reads=[gB[5]], writes=[gWbc[par]])
                for k in range(16):
                    em.gop("v", lambda e, kk=k, W_=W, d=Xe2[par], w_=wbc[par]: e.tensor_tensor(
                        out=d[:, kk * W_:(kk + 1) * W_], in0=xts[:, kk * W_:(kk + 1) * W_],
                        in1=w_[:, 0:W_], op=ALU.mult),
                        reads=[gx, gWbc[par]], writes=[gXe2[par]])

            for ei in range(-1, NE):
                shared = ei < 0
                mw1, mw3, mw2 = (sw1_d, sw3_d, sw2_d) if shared else (w1_d[ei], w3_d[ei], w2_d[ei])
                if ei + 1 < NE:
                    emit_xe(ei + 1)
                if not shared:
                    xin, gxin = Xe2[ei % 2], gXe2[ei % 2]
                else:
                    xin, gxin = xts, gx
                for fg in range(NFG):
                    bsl = fg % 2
                    wblock_load(wA[bsl], gwA[bsl], mw1, fg, 16 * 256, f"dwW{bsl}")
                    wblock_load(wB[bsl], gwB[bsl], mw3, fg, 16 * 256, f"dwB{bsl}")
                    for fi in range(2):
                        f = fg * 2 + fi
                        pb1, gp1 = (pBk[2], gB[2]) if fi == 0 else (pBk[4], gB[4])
                        pb3, gp3 = (pBk[3], gB[3]) if fi == 0 else (pBk[6], gB[6])
                        em.group("t", [
                            (lambda e, kk=k, fi_=fi, b=wA[bsl], p=pb1, xi=xin, W_=W: e.matmul(
                                p[0:128, 0:W_],
                                b[:, kk * 256 + fi_ * 128:kk * 256 + (fi_ + 1) * 128],
                                xi[:, kk * W_:(kk + 1) * W_], start=(kk == 0), stop=(kk == 15)))
                            for k in range(16)],
                            reads=[gwA[bsl], gxin], writes=[gp1])
                        em.group("t", [
                            (lambda e, kk=k, fi_=fi, b=wB[bsl], p=pb3, xi=xin, W_=W: e.matmul(
                                p[0:128, 0:W_],
                                b[:, kk * 256 + fi_ * 128:kk * 256 + (fi_ + 1) * 128],
                                xi[:, kk * W_:(kk + 1) * W_], start=(kk == 0), stop=(kk == 15)))
                            for k in range(16)],
                            reads=[gwB[bsl], gxin], writes=[gp3])
                        em.gop("s", lambda e, p=pb1, W_=W: e.activation(
                            out=h1a, in_=p[0:128, 0:W_], func=AF.Silu),
                            reads=[gp1], writes=[gH1])
                        em.gop("v", lambda e, p=pb3, f_=f, W_=W: e.tensor_tensor(
                            out=hbuf[:, f_ * W_:(f_ + 1) * W_], in0=p[0:128, 0:W_], in1=h1a,
                            op=ALU.mult),
                            reads=[gp3, gH1], writes=[gH])
                for cb in range(4):
                    bsl = cb % 2
                    w2g = [gwA[0], gwA[1]] if bsl == 0 else [gwB[0], gwB[1]]
                    wblock_load(w2bl[bsl], w2g, mw2, cb, 14 * 512, f"dw2{bsl}")
                    for fi in range(4):
                        f2 = cb * 4 + fi
                        pbo, gpo = (pBk[0], gB[0]) if fi % 2 == 0 else (pBk[1], gB[1])
                        em.group("t", [
                            (lambda e, kk=k, fi_=fi, b=w2bl[bsl], p=pbo, W_=W: e.matmul(
                                p[0:128, 0:W_],
                                b[:, kk * 512 + fi_ * 128:kk * 512 + (fi_ + 1) * 128],
                                hbuf[:, kk * W_:(kk + 1) * W_], start=(kk == 0), stop=(kk == 13)))
                            for k in range(14)],
                            reads=w2g + [gH], writes=[gpo])
                        em.gop("v", lambda e, p=pbo, f2_=f2, W_=W: e.tensor_tensor(
                            out=g_acc[:, f2_ * W_:(f2_ + 1) * W_],
                            in0=g_acc[:, f2_ * W_:(f2_ + 1) * W_],
                            in1=p[0:128, 0:W_], op=ALU.add),
                            reads=[gpo], writes=[gGacc])
            for i in range(NT):
                for kf in range(16):
                    em.gop("t", lambda e, i_=i, kf_=kf, W_=W: e.transpose(
                        pBk[5][0:128, 0:128],
                        g_acc[:, kf_ * W_ + i_ * 128:kf_ * W_ + (i_ + 1) * 128], identf[:]),
                        reads=[gGacc, G("identf")], writes=[gB[5]])
                    em.gop("s", lambda e, kf_=kf: e.activation(
                        out=gtokst[:, kf_ * 128:(kf_ + 1) * 128], in_=pBk[5][0:128, 0:128],
                        func=AF.Silu),
                        reads=[gB[5]], writes=[gGt])
                dma_y(lambda e, i_=i, t0=c0: e.dma_start(
                    out=g_d[t0 + i_ * 128:t0 + (i_ + 1) * 128, :], in_=gtokst[:, 0:2048]),
                    "dxgd", reads=[gGt])

    def core_front(ab, c, hv):
        j = ab * CPAB + c
        tt = j
        kh = hv // 2
        par = hv % 2
        qT = arN1[:, kh * AB + c * 128:kh * AB + (c + 1) * 128]
        kT = arN1[:, (16 + kh) * AB + c * 128:(16 + kh) * AB + (c + 1) * 128]
        vT = arN1[:, (32 + hv) * AB + c * 128:(32 + hv) * AB + (c + 1) * 128]
        gQKV = G("qkvT")
        col = lambda t_: t_[:, tt * HV + hv:tt * HV + hv + 1]
        sil = arN5[:, c * 4096 + hv * 128:c * 4096 + (hv + 1) * 128]
        sthv = st_bf[:, hv * 128:(hv + 1) * 128]
        cb0 = par * 4
        mk = lambda idx: masks[:, 128 * idx:128 * (idx + 1)]
        P = lambda t_: t_[:, par * 128:(par + 1) * 128]
        P2 = lambda t_: t_[:, par * 256:(par + 1) * 256]
        Gn = lambda nm: G(f"{nm}{par}")
        bRGQ, gRGQ = (pBk[0], gB[0]) if par == 0 else (pBk[1], gB[1])
        bWX, gWX = (pBk[2], gB[2]) if par == 0 else (pBk[5], gB[5])
        bLAD, gLAD = (pBk[3], gB[3]) if par == 0 else (pBk[6], gB[6])
        ctx = dict(j=j, tt=tt, c=c, hv=hv, par=par, qT=qT, kT=kT, sil=sil, sthv=sthv,
                   cb0=cb0, bWX=bWX, gWX=gWX, col=col, mk=mk, P=P, P2=P2, Gn=Gn)

        em.gop("v", lambda e: e.tensor_scalar(out=P(c_cbc), in0=mk(M_ONES),
                                              scalar1=col(gjt), scalar2=None, op0=ALU.mult),
               reads=[G("masks"), G("gjt")], writes=[Gn("c_cbc")])
        em.gop("t", lambda e: e.transpose(bRGQ[0:128, 0:128], P(c_cbc), identf[:]),
               reads=[Gn("c_cbc"), G("identf")], writes=[gRGQ])
        em.gop("t", lambda e: e.matmul(bRGQ[0:128, 128:256], kT, kT, start=True, stop=True),
               reads=[gQKV], writes=[gRGQ])
        em.gop("t", lambda e: e.matmul(bRGQ[0:128, 256:384], kT, qT, start=True, stop=True),
               reads=[gQKV], writes=[gRGQ])
        em.gop("v", lambda e: e.tensor_scalar(out=P(c_e1), in0=bRGQ[0:128, 0:128],
                                              scalar1=col(gjt), scalar2=0.0,
                                              op0=ALU.subtract, op1=ALU.max),
               reads=[gRGQ, G("gjt")], writes=[Gn("c_e1")])
        em.gop("v", lambda e: e.tensor_scalar(out=P(c_e2), in0=bRGQ[0:128, 0:128],
                                              scalar1=col(gjt), scalar2=0.0,
                                              op0=ALU.subtract, op1=ALU.min),
               reads=[gRGQ, G("gjt")], writes=[Gn("c_e2")])
        em.gop("s", lambda e: e.activation(out=P(c_dcy), in_=P(c_e1), func=AF.Exp, scale=-1.0),
               reads=[Gn("c_e1")], writes=[Gn("c_dcy")])
        em.gop("s", lambda e: e.activation(out=P(c_dcyT), in_=P(c_e2), func=AF.Exp),
               reads=[Gn("c_e2")], writes=[Gn("c_dcyT")])
        if CORE_STEPS < 2:
            return ctx
        em.gop("v", lambda e: e.scalar_tensor_tensor(out=P(c_scrap), in0=bRGQ[0:128, 128:256],
                                                     scalar=col(negbt), in1=P(c_dcy),
                                                     op0=ALU.mult, op1=ALU.mult),
               reads=[gRGQ, G("negbt"), Gn("c_dcy")], writes=[Gn("c_scrap")])
        em.gop("v", lambda e: e.tensor_tensor(out=P(c_A), in0=P(c_scrap), in1=mk(M_STRICT),
                                              op=ALU.mult),
               reads=[Gn("c_scrap"), G("masks")], writes=[Gn("c_A")])
        em.gop("v", lambda e: e.tensor_tensor(out=P(c_scrap), in0=bRGQ[0:128, 256:384],
                                              in1=P(c_dcyT), op=ALU.mult),
               reads=[gRGQ, Gn("c_dcyT")], writes=[Gn("c_scrap")])
        em.gop("v", lambda e: e.tensor_tensor(out=P(c_intraT), in0=P(c_scrap), in1=mk(M_TRIU),
                                              op=ALU.mult),
               reads=[Gn("c_scrap"), G("masks")], writes=[Gn("c_intraT")])
        if CORE_STEPS < 3:
            return ctx
        oAT, oKT, oVT = par * 128, 256 + par * 128, 512 + par * 128
        em.gop("t", lambda e, o=oAT: e.transpose(pBF[0:128, o:o + 128], P(c_A), identb[:]),
               reads=[Gn("c_A"), G("identb")], writes=[gBF])
        em.gop("s", lambda e, o=oAT: e.copy(out=P(c_AT), in_=pBF[0:128, o:o + 128]),
               reads=[gBF], writes=[Gn("c_AT")])
        em.gop("t", lambda e, o=oKT: e.transpose(pBF[0:128, o:o + 128], kT, identb[:]),
               reads=[gQKV, G("identb")], writes=[gBF])
        em.gop("t", lambda e, o=oVT: e.transpose(pBF[0:128, o:o + 128], vT, identb[:]),
               reads=[gQKV, G("identb")], writes=[gBF])
        em.gop("v", lambda e, o=oVT: e.tensor_scalar(out=P2(c_X0)[:, 0:128],
                                                     in0=pBF[0:128, o:o + 128],
                                                     scalar1=col(bt), scalar2=None, op0=ALU.mult),
               reads=[gBF, G("bt")], writes=[Gn("c_X0")])
        em.gop("v", lambda e, o=oKT: e.tensor_scalar(out=P2(c_X0)[:, 128:256],
                                                     in0=pBF[0:128, o:o + 128],
                                                     scalar1=col(begjt), scalar2=None,
                                                     op0=ALU.mult),
               reads=[gBF, G("begjt")], writes=[Gn("c_X0")])
        em.gop("v", lambda e, o=oKT: e.tensor_scalar(out=P(c_kdec), in0=pBF[0:128, o:o + 128],
                                                     scalar1=col(eglmt), scalar2=None,
                                                     op0=ALU.mult),
               reads=[gBF, G("eglmt")], writes=[Gn("c_kdec")])
        if CORE_STEPS < 4:
            return ctx
        TB = [P2(c_T0), P2(c_T1)]
        gTB = [Gn("c_T0"), Gn("c_T1")]
        em.gop("v", lambda e: e.tensor_tensor(out=P(c_scrap), in0=P(c_A), in1=mk(M_OD1),
                                              op=ALU.mult),
               reads=[Gn("c_A"), G("masks")], writes=[Gn("c_scrap")])
        em.gop("v", lambda e: e.tensor_tensor(out=TB[0][:, 0:128], in0=P(c_scrap), in1=identf[:],
                                              op=ALU.add),
               reads=[Gn("c_scrap"), G("identf")], writes=[gTB[0]])
        em.gop("v", lambda e: e.tensor_tensor(out=P(c_scrap), in0=P(c_AT), in1=mk(M_ODT[1]),
                                              op=ALU.mult),
               reads=[Gn("c_AT"), G("masks")], writes=[Gn("c_scrap")])
        em.gop("v", lambda e: e.tensor_tensor(out=TB[0][:, 128:256], in0=P(c_scrap), in1=identf[:],
                                              op=ALU.add),
               reads=[Gn("c_scrap"), G("identf")], writes=[gTB[0]])
        cur = 0
        for s in (2, 4, 8, 16, 32, 64):
            nxt = 1 - cur
            Tc, gTc = TB[cur], gTB[cur]
            Tn, gTn = TB[nxt], gTB[nxt]
            em.gop("v", lambda e, s_=s: e.tensor_tensor(out=P(c_aod), in0=P(c_AT),
                                                        in1=mk(M_ODT[s_]), op=ALU.mult),
                   reads=[Gn("c_AT"), G("masks")], writes=[Gn("c_aod")])
            em.gop("t", lambda e, Tc_=Tc: e.matmul(bWX[0:128, 0:128], P(c_aod), Tc_[:, 0:128],
                                                   start=True, stop=True),
                   reads=[Gn("c_aod"), gTc], writes=[gWX])
            em.gop("s", lambda e: e.copy(out=P(c_W), in_=bWX[0:128, 0:128]),
                   reads=[gWX], writes=[Gn("c_W")])
            em.group("t", [
                (lambda e, Tc_=Tc: e.matmul(bLAD[0:128, 0:128], Tc_[:, 128:256], P(c_W),
                                            start=True, stop=False)),
                (lambda e, Tc_=Tc: e.matmul(bLAD[0:128, 0:128], identb[:], Tc_[:, 0:128],
                                            start=False, stop=True)),
                (lambda e, Tc_=Tc: e.matmul(bLAD[0:128, 128:256], P(c_W), Tc_[:, 128:256],
                                            start=True, stop=False)),
                (lambda e, Tc_=Tc: e.matmul(bLAD[0:128, 128:256], identb[:],
                                            Tc_[:, 128:256], start=False, stop=True))],
                reads=[Gn("c_W"), gTc, G("identb")], writes=[gLAD])
            em.gop("v", lambda e, Tn_=Tn: e.tensor_copy(out=Tn_[:, 0:256],
                                                        in_=bLAD[0:128, 0:256]),
                   reads=[gLAD], writes=[gTn])
            cur = nxt
        if CORE_STEPS < 5:
            return ctx
        Tfin, gTfin = TB[cur], gTB[cur]
        em.gop("t", lambda e, Tf=Tfin: e.matmul(bWX[0:128, 256:512], Tf[:, 128:256], P2(c_X0),
                                                start=True, stop=True),
               reads=[gTfin, Gn("c_X0")], writes=[gWX])
        em.gop("v", lambda e: e.tensor_copy(out=P2(c_Xsb), in_=bWX[0:128, 256:512]),
               reads=[gWX], writes=[Gn("c_Xsb")])
        return ctx

    def core_back(ctx):
        if CORE_STEPS < 6:
            return
        j = ctx["j"]; par = ctx["par"]; hv = ctx["hv"]; c = ctx["c"]; cb0 = ctx["cb0"]
        qT = ctx["qT"]; sil = ctx["sil"]; sthv = ctx["sthv"]; col = ctx["col"]
        P = ctx["P"]; P2 = ctx["P2"]; Gn = ctx["Gn"]
        gQKV = G("qkvT")
        gSil = G("siluz")
        gSt = G(f"st{hv}")
        gNT = G("normedT")
        oKC = 768 + par * 128
        em.gop("t", lambda e, o=oKC: e.transpose(pBF[0:128, o:o + 128], P2(c_Xsb)[:, 128:256],
                                                 identb[:]),
               reads=[Gn("c_Xsb"), G("identb")], writes=[gBF])
        em.gop("s", lambda e, o=oKC: e.copy(out=P(c_kcdT), in_=pBF[0:128, o:o + 128]),
               reads=[gBF], writes=[Gn("c_kcdT")])
        if j > 0:
            em.gop("t", lambda e: e.matmul(pBk[4][0:128, 0:128], P(c_kcdT), sthv,
                                           start=True, stop=True),
                   reads=[Gn("c_kcdT"), gSt], writes=[gB[4]])
            em.gop("t", lambda e: e.matmul(pBk[4][0:128, 128:256], qT, sthv,
                                           start=True, stop=True),
                   reads=[gQKV, gSt], writes=[gB[4]])
            em.gop("v", lambda e: e.tensor_tensor(out=P(c_vnew), in0=P2(c_Xsb)[:, 0:128],
                                                  in1=pBk[4][0:128, 0:128], op=ALU.subtract),
                   reads=[Gn("c_Xsb"), gB[4]], writes=[Gn("c_vnew")])
            em.gop("v", lambda e: e.tensor_scalar(out=P(c_e1), in0=pBk[4][0:128, 128:256],
                                                  scalar1=col(egjt), scalar2=None,
                                                  op0=ALU.mult),
                   reads=[G("egjt"), gB[4]], writes=[Gn("c_e1")])
            em.gop("t", lambda e: e.matmul(pBk[4][0:128, 384:512], P(c_intraT), P(c_vnew),
                                           start=True, stop=True),
                   reads=[Gn("c_intraT"), Gn("c_vnew")], writes=[gB[4]])
            em.gop("v", lambda e: e.scalar_tensor_tensor(out=P(c_xg), in0=pBk[4][0:128, 384:512],
                                                         scalar=1.0, in1=P(c_e1),
                                                         op0=ALU.mult, op1=ALU.add),
                   reads=[gB[4], Gn("c_e1")], writes=[Gn("c_xg")])
        else:
            em.gop("v", lambda e: e.tensor_copy(out=P(c_vnew), in_=P2(c_Xsb)[:, 0:128]),
                   reads=[Gn("c_Xsb")], writes=[Gn("c_vnew")])
            em.gop("t", lambda e: e.matmul(pBk[4][0:128, 384:512], P(c_intraT), P(c_vnew),
                                           start=True, stop=True),
                   reads=[Gn("c_intraT"), Gn("c_vnew")], writes=[gB[4]])
            em.gop("v", lambda e: e.tensor_copy(out=P(c_xg), in_=pBk[4][0:128, 384:512]),
                   reads=[gB[4]], writes=[Gn("c_xg")])
        if CORE_STEPS < 7:
            return
        em.gop("v", lambda e: e.tensor_tensor(out=P(c_xg), in0=P(c_xg), in1=sil, op=ALU.mult),
               reads=[gSil], writes=[Gn("c_xg")])
        em.gop("v", lambda e, cb=cb0: e.scalar_tensor_tensor(out=P(c_scrap), in0=P(c_xg),
                                                             scalar=1.0, in1=P(c_xg),
                                                             op0=ALU.mult, op1=ALU.mult,
                                                             accum_out=c_cols[:, cb + 1:cb + 2]),
               reads=[Gn("c_xg")], writes=[Gn("c_scrap"), G(f"c_ssq{par}")])
        em.gop("s", lambda e, cb=cb0: e.activation(out=c_cols[:, cb + 2:cb + 3],
                                                   in_=c_cols[:, cb + 1:cb + 2], func=AF.Ln,
                                                   scale=1.0 / 128.0, bias=RMS_EPS),
               reads=[G(f"c_ssq{par}")], writes=[G(f"c_lnssq{par}")])
        em.gop("s", lambda e, cb=cb0: e.activation(out=c_cols[:, cb + 3:cb + 4],
                                                   in_=c_cols[:, cb + 2:cb + 3], func=AF.Exp,
                                                   scale=-0.5),
               reads=[G(f"c_lnssq{par}")], writes=[G(f"c_rstd{par}")])
        em.gop("v", lambda e, cb=cb0: e.scalar_tensor_tensor(out=P(c_ntok), in0=P(c_xg),
                                                             scalar=c_cols[:, cb + 3:cb + 4],
                                                             in1=nwbc[:], op0=ALU.mult,
                                                             op1=ALU.mult),
               reads=[Gn("c_xg"), G(f"c_rstd{par}"), G("nwbc")], writes=[Gn("c_ntok")])
        em.gop("t", lambda e, o=oKC: e.transpose(pBF[0:128, o:o + 128], P(c_ntok), identb[:]),
               reads=[Gn("c_ntok"), G("identb")], writes=[gBF])
        em.gop("s", lambda e, o=oKC,
               d=arN5[:, 8192 + hv * AB + c * 128:8192 + hv * AB + (c + 1) * 128]:
               e.copy(out=d, in_=pBF[0:128, o:o + 128]),
               reads=[gBF], writes=[gNT])
        if CORE_STEPS < 8:
            return
        em.gop("t", lambda e: e.matmul(pBk[4][0:128, 256:384], P(c_kdec), P(c_vnew),
                                       start=True, stop=True),
               reads=[Gn("c_kdec"), Gn("c_vnew")], writes=[gB[4]])
        if j > 0:
            em.gop("v", lambda e: e.scalar_tensor_tensor(out=sthv, in0=sthv,
                                                         scalar=col(eglall),
                                                         in1=pBk[4][0:128, 256:384],
                                                         op0=ALU.mult, op1=ALU.add),
                   reads=[gB[4], G("eglall")], writes=[gSt])
        else:
            em.gop("v", lambda e: e.tensor_copy(out=sthv, in_=pBk[4][0:128, 256:384]),
                   reads=[gB[4]], writes=[gSt])

    def phaseA():
        xts = arN3[:, 0:16 * AB]
        gx = G("paxts")
        qkvT = arN1[:, 0:64 * AB]
        gQKV = G("qkvT")
        wA = [arN2[:, 0:8192], arN2[:, 8192:16384]]
        gwA = [G("pawA0"), G("pawA1")]
        mixsb = [arN4[:, 0:AB + 3], arN4[:, 512:512 + AB + 3]]
        gMix = [G("pamix0"), G("pamix1")]
        convacc = arN4[:, 1024:1024 + AB]
        gCv = G("paconv")
        ct0 = arN4[:, 1280:1280 + AB]
        ct1 = arN4[:, 1536:1536 + AB]
        gCt = G("pact")
        xsl = [arN4[:, 2048:4096], arN4[:, 4096:6144]]
        gXsl = [G("paxsl0"), G("paxsl1")]
        outst = [[arN4[:, 6144:6400], arN4[:, 6400:6656]],
                 [arN4[:, 7168:7424], arN4[:, 7424:7680]]]
        gOut = [[G("paout00"), G("paout01")], [G("paout10"), G("paout11")]]
        gbuf = [arN3[:, 4096:6144], arN3[:, 6144:8192]]
        gGbuf = [G("pag0"), G("pag1")]
        siluz = arN5[:, 0:2 * 4096]
        gSil = G("siluz")
        normedT = arN5[:, 8192:8192 + 32 * AB]
        gNT = G("normedT")

        for ab in range(NAB):
            t0 = ab * AB
            xt_load(xts, gx, t0, t0 + AB)
            for m in range(16 if "mix" in SUBA else 0):
                bsl = m % 2
                wblock_load(wA[bsl], gwA[bsl], wqkv_d, m, 16 * 512, f"dwW{bsl}")
                for fl in range(4):
                    f = m * 4 + fl
                    pb, gp = (pBk[0], gB[0]) if fl % 2 == 0 else (pBk[1], gB[1])
                    em.group("t", [
                        (lambda e, kk=k, fl_=fl, b=wA[bsl], p=pb: e.matmul(
                            p[0:128, 0:AB],
                            b[:, kk * 512 + fl_ * 128:kk * 512 + (fl_ + 1) * 128],
                            xts[:, kk * AB:(kk + 1) * AB], start=(kk == 0), stop=(kk == 15)))
                        for k in range(16)],
                        reads=[gwA[bsl], gx], writes=[gp])
                    ms, gms = mixsb[fl % 2], gMix[fl % 2]
                    em.gop("v", lambda e, p=pb, d=ms: e.tensor_copy(out=d[:, 3:3 + AB],
                                                                    in_=p[0:128, 0:AB]),
                           reads=[gp], writes=[gms])
                    em.gop("v", lambda e, d=ms, f_=f: e.tensor_copy(
                        out=d[:, 0:3], in_=mixstash[:, f_ * 3:(f_ + 1) * 3]),
                        reads=[G("mixstash")], writes=[gms])
                    em.gop("v", lambda e, d=ms, f_=f: e.tensor_copy(
                        out=mixstash[:, f_ * 3:(f_ + 1) * 3], in_=d[:, AB:AB + 3]),
                        reads=[gms], writes=[G("mixstash")])
                    em.gop("s", lambda e, d=ms, f_=f: e.activation(
                        out=ct0, in_=d[:, 0:AB], func=AF.Copy,
                        scale=cw_sb[:, f_ * KC + 0:f_ * KC + 1]),
                        reads=[gms, G("cw_sb")], writes=[gCt])
                    em.gop("s", lambda e, d=ms, f_=f: e.activation(
                        out=ct1, in_=d[:, 1:1 + AB], func=AF.Copy,
                        scale=cw_sb[:, f_ * KC + 1:f_ * KC + 2]),
                        reads=[gms, G("cw_sb")], writes=[gCt])
                    em.gop("v", lambda e: e.tensor_tensor(out=convacc, in0=ct0, in1=ct1,
                                                          op=ALU.add),
                           reads=[gCt], writes=[gCv])
                    em.gop("v", lambda e, d=ms, f_=f: e.scalar_tensor_tensor(
                        out=convacc, in0=d[:, 2:2 + AB],
                        scalar=cw_sb[:, f_ * KC + 2:f_ * KC + 3], in1=convacc,
                        op0=ALU.mult, op1=ALU.add),
                        reads=[gms, G("cw_sb")], writes=[gCv])
                    em.gop("v", lambda e, d=ms, f_=f: e.scalar_tensor_tensor(
                        out=convacc, in0=d[:, 3:3 + AB],
                        scalar=cw_sb[:, f_ * KC + 3:f_ * KC + 4], in1=convacc,
                        op0=ALU.mult, op1=ALU.add),
                        reads=[gms, G("cw_sb")], writes=[gCv])
                    em.gop("s", lambda e, f_=f: e.activation(
                        out=qkvT[:, f_ * AB:(f_ + 1) * AB], in_=convacc, func=AF.Silu),
                        reads=[gCv], writes=[gQKV])
            for f in range(32 if "l2n" in SUBA else 0):
                isq = f < 16
                em.gop("v", lambda e, f_=f: e.tensor_tensor(
                    out=sq_sb[:, 0:AB], in0=qkvT[:, f_ * AB:(f_ + 1) * AB],
                    in1=qkvT[:, f_ * AB:(f_ + 1) * AB], op=ALU.mult),
                    reads=[gQKV], writes=[G("sq_sb")])
                em.gop("t", lambda e: e.matmul(pBk[2][0:1, 0:AB], onescolb[:], sq_sb[:, 0:AB],
                                               start=True, stop=True),
                       reads=[G("sq_sb"), G("onescolb")], writes=[gB[2]])
                em.gop("s", lambda e: e.activation(out=rowsc[0:1, 0:AB], in_=pBk[2][0:1, 0:AB],
                                                   func=AF.Ln, bias=L2_EPS),
                       reads=[gB[2]], writes=[G("rowsc")])
                em.gop("s", lambda e, b_=(-0.5 * LN128 if isq else 0.0): e.activation(
                    out=rowsc[0:1, AB:2 * AB], in_=rowsc[0:1, 0:AB], func=AF.Exp,
                    scale=-0.5, bias=b_),
                    reads=[G("rowsc")], writes=[G("rowsc")])
                em.gop("t", lambda e: e.matmul(pBk[2][0:128, 256:256 + AB], ones1f[:],
                                               rowsc[0:1, AB:2 * AB], start=True, stop=True),
                       reads=[G("rowsc"), G("ones1f")], writes=[gB[2]])
                em.gop("v", lambda e, f_=f: e.tensor_tensor(
                    out=qkvT[:, f_ * AB:(f_ + 1) * AB], in0=qkvT[:, f_ * AB:(f_ + 1) * AB],
                    in1=pBk[2][0:128, 256:256 + AB], op=ALU.mult),
                    reads=[gB[2]], writes=[gQKV])
            for fb in range(8 if "z" in SUBA else 0):
                bsl = fb % 2
                wblock_load(wA[bsl], gwA[bsl], wz_d, fb, 16 * 512, f"dwW{bsl}")
                for c in range(CPAB):
                    pb, gp = (pBk[3], gB[3]) if c % 2 == 0 else (pBk[4], gB[4])
                    em.group("t", [
                        (lambda e, kk=k, b=wA[bsl], p=pb, c_=c: e.matmul(
                            p[0:128, 0:512], xts[:, kk * AB + c_ * 128:kk * AB + (c_ + 1) * 128],
                            b[:, kk * 512:(kk + 1) * 512], start=(kk == 0), stop=(kk == 15)))
                        for k in range(16)],
                        reads=[gwA[bsl], gx], writes=[gp])
                    em.gop("s", lambda e, p=pb, c_=c, fb_=fb: e.activation(
                        out=siluz[:, c_ * 4096 + fb_ * 512:c_ * 4096 + (fb_ + 1) * 512],
                        in_=p[0:128, 0:512], func=AF.Silu),
                        reads=[gp], writes=[gSil])
            if "core" in SUBA:
                pend = None
                for c in range(CPAB):
                    for hv in range(NHV):
                        ctx = core_front(ab, c, hv)
                        if pend is not None:
                            core_back(pend)
                        pend = ctx
                core_back(pend)
            for c in range(CPAB):
                dma_y(lambda e, d=xsl[c], s=x_d[t0 + c * 128:t0 + (c + 1) * 128, :]:
                      e.dma_start(out=d, in_=s), f"dxl{c}", writes=[gXsl[c]])
                dma_y(lambda e, d=gbuf[c], s=g_d[t0 + c * 128:t0 + (c + 1) * 128, :]:
                      e.dma_start(out=d, in_=s), f"dxg{c}", writes=[gGbuf[c]])
            for blk in range(8 if "smm" in SUBA else 0):
                bsl = blk % 2
                wblock_load(wA[bsl], gwA[bsl], wout_d, blk, 32 * 256, f"dwW{bsl}")
                for c in range(CPAB):
                    pb, gp = (pBk[5], gB[5]) if c % 2 == 0 else (pBk[6], gB[6])
                    em.group("t", [
                        (lambda e, kv_=kv, b=wA[bsl], p=pb, c_=c: e.matmul(
                            p[0:128, 0:256],
                            normedT[:, kv_ * AB + c_ * 128:kv_ * AB + (c_ + 1) * 128],
                            b[:, kv_ * 256:(kv_ + 1) * 256], start=(kv_ == 0), stop=(kv_ == 31)))
                        for kv in range(32)],
                        reads=[gwA[bsl], gNT], writes=[gp])
                    sl = blk % 2
                    osl = outst[c][sl]
                    go = gOut[c][sl]
                    em.gop("v", lambda e, p=pb, blk_=blk, c_=c, o=osl: e.tensor_tensor(
                        out=o, in0=p[0:128, 0:256],
                        in1=gbuf[c_][:, blk_ * 256:(blk_ + 1) * 256], op=ALU.mult),
                        reads=[gp, gGbuf[c]], writes=[go])
                    em.gop("v", lambda e, blk_=blk, c_=c, o=osl: e.tensor_tensor(
                        out=o, in0=o,
                        in1=xsl[c_][:, blk_ * 256:(blk_ + 1) * 256], op=ALU.add),
                        reads=[gXsl[c]], writes=[go])
                    dma_y(lambda e, blk_=blk, c_=c, o=osl, t0_=t0: e.dma_start(
                        out=out_d[t0_ + c_ * 128:t0_ + (c_ + 1) * 128,
                                  blk_ * 256:(blk_ + 1) * 256], in_=o),
                        f"dxo{c}{sl}", reads=[go])

    if 0 in phases:
        phase0()
    em.barrier()
    if 1 in phases:
        phaseB()
    em.barrier()
    if 2 in phases:
        phaseF()
    em.barrier()
    if 3 in phases:
        phaseA()
    em.op("y", lambda e: None,
          waits=[(nm, em.counts[nm]) for nm in sorted(em.dma_sems)], sem=None)

    sems = {}
    for nm in ("t", "v", "s"):
        sems[nm] = es.enter_context(nc.semaphore(f"sem_{nm}"))
    for nm in sorted(em.dma_sems):
        sems[nm] = es.enter_context(nc.semaphore(f"sem_{nm}"))

    with nc.Block() as block:
        @block.tensor
        def _(t):
            em.replay("t", t, sems)

        @block.vector
        def _(v):
            em.replay("v", v, sems)

        @block.scalar
        def _(s):
            em.replay("s", s, sems)

        @block.gpsimd
        def _(g):
            em.replay("g", g, sems)

        @block.sync
        def _(y):
            em.replay("y", y, sems)

    es.close()
    return nc


_NC_CACHE = {}


def _band(w, cb):
    w = np.ascontiguousarray(np.asarray(w, dtype=np.float32))
    R, C = w.shape
    kt, nb = R // 128, C // cb
    return np.ascontiguousarray(
        w.reshape(kt, 128, nb, cb).transpose(2, 1, 0, 3).reshape(nb, 128, kt * cb))


def kernel(L0_attn_hidden, L0_attn_W_qkv, L0_attn_W_z, L0_attn_W_b, L0_attn_W_a,
           L0_attn_conv_weight, L0_attn_dt_bias, L0_attn_A_log, L0_attn_norm_w,
           L0_attn_W_out, L0_ffn_gate_w, L0_ffn_w1, L0_ffn_w2, L0_ffn_w3,
           L0_ffn_sw1, L0_ffn_sw2, L0_ffn_sw3, L0_ffn_expert_bias):
    import ml_dtypes
    B, T, _ = L0_attn_hidden.shape
    if T not in _NC_CACHE:
        _NC_CACHE[T] = build_nc(T)
    nc = _NC_CACHE[T]
    bfv = lambda a: np.ascontiguousarray(a).astype(ml_dtypes.bfloat16).view(np.uint16)
    f = lambda a: np.ascontiguousarray(np.asarray(a, dtype=np.float32))
    common = {
        "wqkv_b": _band(L0_attn_W_qkv, 512), "wz_b": _band(L0_attn_W_z, 512),
        "wb": f(L0_attn_W_b),
        "wa": f(L0_attn_W_a), "cw": f(L0_attn_conv_weight), "dtb": f(L0_attn_dt_bias),
        "alog": f(L0_attn_A_log), "nw": f(L0_attn_norm_w),
        "wout_b": _band(L0_attn_W_out, 256),
        "gatew": f(L0_ffn_gate_w),
        "w1_b": np.stack([_band(L0_ffn_w1[i], 256) for i in range(NE)]),
        "w2_b": np.stack([_band(L0_ffn_w2[i], 512) for i in range(NE)]),
        "w3_b": np.stack([_band(L0_ffn_w3[i], 256) for i in range(NE)]),
        "sw1_b": _band(L0_ffn_sw1, 256), "sw2_b": _band(L0_ffn_sw2, 512),
        "sw3_b": _band(L0_ffn_sw3, 256), "ebias": f(L0_ffn_expert_bias),
        "i_consts": np.tile(np.array([LN_EPS, RMS_EPS, 1.0, -0.5 * LN128, 0.0, 0.0, 0.0, 0.0],
                                     np.float32)[None, :], (128, 1)),
        "i_identb": bfv(np.eye(128, dtype=np.float32)),
        "i_identf": np.eye(128, dtype=np.float32),
        "i_ones1f": np.ones((1, 128), np.float32),
        "i_onescolb": bfv(np.ones((128, 1), np.float32)),
        "i_masks": host_masks(),
        "i_iotaE": np.tile(np.arange(NE, dtype=np.float32)[None, :], (128, 1)),
    }
    in_maps = []
    for b in range(B):
        m = dict(common)
        m["x"] = f(L0_attn_hidden[b])
        in_maps.append(m)
    res = run_bass_kernel_spmd(nc, in_maps, core_ids=list(range(B)))
    out = np.stack([np.asarray(res.results[i]["out"]) for i in range(B)], axis=0)
    return out.astype(np.float32)


# version 08 (final)
# speedup vs baseline: 1.7092x; 1.0077x over previous
from collections import defaultdict
from contextlib import ExitStack

import numpy as np
import concourse.bass as bass
import concourse.mybir as mybir
from concourse.bass_utils import run_bass_kernel_spmd

F32 = mybir.dt.float32
BF16 = mybir.dt.bfloat16
AF = mybir.ActivationFunctionType
ALU = mybir.AluOpType
AX = mybir.AxisListType

D = 2048
CONVD = 8192
KEY = 2048
VAL = 4096
HK, HV = 16, 32
NE, HID = 8, 1792
KC = 4
CH = 128
AB = 256
LN_EPS = 1e-5
L2_EPS = 1e-6
RMS_EPS = 1e-6
LN128 = float(np.log(128.0))
SUBA = {"mix", "l2n", "z", "core", "smm"}
CORE_STEPS = 99
NHV = HV
ST8 = "all"

M_STRICT = 0
M_TRIU = 1
M_ODT = {s: 2 + i for i, s in enumerate((1, 2, 4, 8, 16, 32, 64))}
M_OD1 = 9
M_ONES = 10
NMASK = 11


class Guard:
    __slots__ = ("w", "r")

    def __init__(self):
        self.w = []
        self.r = []

    def rd(self):
        return list(self.w)

    def wr(self):
        return list(self.w) + list(self.r)

    def setw(self, nm, val):
        self.w = [(nm, val)]
        self.r = []

    def addr(self, nm, val):
        self.r.append((nm, val))
        if len(self.r) > 24:
            best = {}
            for nm2, v2 in self.r:
                best[nm2] = max(best.get(nm2, 0), v2)
            self.r = sorted(best.items())


class Emitter:
    STREAMS = ("t", "v", "s", "g", "y")
    ENG_SEM = {"t": "t", "v": "v", "s": "s"}
    DROP_OWN = False

    def __init__(self):
        self.ops = {k: [] for k in self.STREAMS}
        self.counts = defaultdict(int)
        self.pending = {k: [] for k in self.STREAMS}
        self.dma_sems = set()
        self.hiwater = {k: defaultdict(int) for k in self.STREAMS}

    def op(self, stream, build, waits=(), sem=None, amt=1):
        allw = list(waits) + self.pending[stream]
        self.pending[stream] = []
        ws = {}
        own = self.ENG_SEM.get(stream)
        hw = self.hiwater[stream]
        for nm, vv in allw:
            if vv <= 0 or (self.DROP_OWN and nm == own) or vv <= hw[nm]:
                continue
            ws[nm] = max(ws.get(nm, 0), vv)
        for nm, vv in ws.items():
            assert vv <= self.counts[nm], (stream, nm, vv, self.counts[nm])
            hw[nm] = vv
        self.ops[stream].append((sorted(ws.items()), build, sem, amt))
        if sem is not None:
            self.counts[sem] += amt
            return self.counts[sem]
        return None

    def gop(self, stream, build, reads=(), writes=(), extra=(), sem=None):
        if sem is None:
            sem = self.ENG_SEM[stream]
            amt = 1
        else:
            amt = 16
            self.dma_sems.add(sem)
        waits = []
        for gd in writes:
            waits += gd.wr()
        for gd in reads:
            waits += gd.rd()
        waits += list(extra)
        val = self.op(stream, build, waits, sem=sem, amt=amt)
        for gd in writes:
            gd.setw(sem, val)
        for gd in reads:
            gd.addr(sem, val)
        return val

    def group(self, stream, builds, reads=(), writes=(), sem=None):
        if sem is None:
            sem = self.ENG_SEM[stream]
            amt = 1
        else:
            amt = 16
            self.dma_sems.add(sem)
        waits = []
        for gd in writes:
            waits += gd.wr()
        for gd in reads:
            waits += gd.rd()
        n = len(builds)
        for i, b in enumerate(builds):
            self.op(stream, b, waits if i == 0 else (),
                    sem=sem if i == n - 1 else None, amt=amt)
        val = self.counts[sem]
        for gd in writes:
            gd.setw(sem, val)
        for gd in reads:
            gd.addr(sem, val)
        return val

    def barrier(self):
        snap = [(nm, v) for nm, v in self.counts.items() if v > 0]
        for st in self.STREAMS:
            self.pending[st] += snap

    def replay(self, stream, eng, sems):
        for waits, build, incname, amt in self.ops[stream]:
            for nm, v in waits:
                eng.wait_ge(sems[nm], v)
            inst = build(eng)
            if incname is not None:
                inst.then_inc(sems[incname], amt)


def host_masks():
    n = 128
    i = np.arange(n)
    m = np.zeros((n, 128 * NMASK), np.float32)

    def blk(idx):
        return slice(128 * idx, 128 * (idx + 1))

    m[:, blk(M_STRICT)] = (i[:, None] > i[None, :])
    m[:, blk(M_TRIU)] = (i[None, :] >= i[:, None])
    for s, idx in M_ODT.items():
        od = ((i[:, None] // s) % 2 == 1) & ((i[None, :] // s) % 2 == 0) \
            & ((i[:, None] // (2 * s)) == (i[None, :] // (2 * s)))
        m[:, blk(idx)] = od.T
    s = 1
    od1 = ((i[:, None] // s) % 2 == 1) & ((i[None, :] // s) % 2 == 0) \
        & ((i[:, None] // (2 * s)) == (i[None, :] // (2 * s)))
    m[:, blk(M_OD1)] = od1
    m[:, blk(M_ONES)] = 1.0
    return m


def build_nc(T=2048, phases=(0, 1, 2, 3)):
    TTL = T // 128
    NAB = T // AB
    TBF = min(512, T)
    NFB = T // TBF
    NTB = TBF // 128
    CPAB = AB // CH

    nc = bass.Bass("TRN2", target_bir_lowering=False)

    def din(name, shape, dt=F32):
        return nc.dram_tensor(name, shape, dt, kind="ExternalInput").ap()

    x_d = din("x", [T, D])
    wqkv_d = din("wqkv_b", [16, 128, 16 * 512])
    wz_d = din("wz_b", [8, 128, 16 * 512])
    wb_d = din("wb", [D, HV])
    wa_d = din("wa", [D, HV])
    cw_d = din("cw", [CONVD, KC])
    dtb_d = din("dtb", [HV])
    alog_d = din("alog", [HV])
    nw_d = din("nw", [128])
    wout_d = din("wout_b", [8, 128, 32 * 256])
    gatew_d = din("gatew", [D, NE])
    w1_d = din("w1_b", [NE, 7, 128, 16 * 256])
    w2_d = din("w2_b", [NE, 4, 128, 14 * 512])
    w3_d = din("w3_b", [NE, 7, 128, 16 * 256])
    sw1_d = din("sw1_b", [7, 128, 16 * 256])
    sw2_d = din("sw2_b", [4, 128, 14 * 512])
    sw3_d = din("sw3_b", [7, 128, 16 * 256])
    ebias_d = din("ebias", [NE])
    consts_d = din("i_consts", [128, 8], F32)
    identb_d = din("i_identb", [128, 128], BF16)
    identf_d = din("i_identf", [128, 128], F32)
    ones1f_d = din("i_ones1f", [1, 128], F32)
    onescolb_d = din("i_onescolb", [128, 1], BF16)
    masks_d = din("i_masks", [128, 128 * NMASK], F32)
    iotaE_d = din("i_iotaE", [128, NE], F32)

    out_d = nc.dram_tensor("out", [T, D], F32, kind="ExternalOutput").ap()
    xnT_d = nc.dram_tensor("xnT", [16, 128, T], BF16, kind="Internal").ap()
    g_d = nc.dram_tensor("gd", [T, D], BF16, kind="Internal").ap()

    es = ExitStack()

    def sb(name, shape, dt):
        return es.enter_context(nc.sbuf_tensor(name, shape, dt))

    constcol = sb("constcol", [128, 8], F32)
    identb = sb("identb", [128, 128], BF16)
    identf = sb("identf", [128, 128], F32)
    ones1f = sb("ones1f", [1, 128], F32)
    onescolb = sb("onescolb", [128, 1], BF16)
    masks = sb("masks", [128, 128 * NMASK], F32)
    iotaE = sb("iotaE", [128, NE], F32)
    nwbc = sb("nwbc", [128, 128], F32)
    ebbc = sb("ebbc", [128, NE], F32)
    nwrow = sb("nwrow", [1, 128], F32)
    ebrow = sb("ebrow", [1, NE], F32)
    gatew_sb = sb("gatew_sb", [128, 16 * NE], F32)
    wb_sb = sb("wb_sb", [128, 16 * HV], BF16)
    wa_sb = sb("wa_sb", [128, 16 * HV], BF16)
    cw_sb = sb("cw_sb", [128, 64 * KC], F32)
    dtb_sb = sb("dtb_sb", [HV, 1], F32)
    alog_sb = sb("alog_sb", [HV, 1], F32)
    negea_sb = sb("negea_sb", [HV, 1], F32)
    st_bf = sb("st_bf", [128, HV * 128], BF16)
    wexp = sb("wexp", [128, TTL * NE], F32)
    mixstash = sb("mixstash", [128, 64 * (KC - 1)], F32)
    gjt = sb("gjt", [128, TTL * HV], F32)
    bt = sb("bt", [128, TTL * HV], F32)
    negbt = sb("negbt", [128, TTL * HV], F32)
    egjt = sb("egjt", [128, TTL * HV], F32)
    eglmt = sb("eglmt", [128, TTL * HV], F32)
    begjt = sb("begjt", [128, TTL * HV], F32)
    eglall = sb("eglall", [128, TTL * HV], F32)
    colsc = sb("colsc", [128, 16], F32)
    rowsc = sb("rowsc", [1, 512], F32)
    hmA = sb("hmA", [HV, TBF], F32)
    hmB = sb("hmB", [HV, TBF], F32)
    hmC = sb("hmC", [HV, TBF], F32)
    onesHM = sb("onesHM", [HV, CH], F32)
    gsm = sb("gsm", [128, 10 * NE], F32)
    sq_sb = sb("sq_sb", [128, AB], BF16)

    arN1 = sb("arN1", [128, 16384], BF16)
    arN2 = sb("arN2", [128, 16384], BF16)
    arN3 = sb("arN3", [128, 8192], BF16)
    arN4 = sb("arN4", [128, 8192], F32)
    arN5 = sb("arN5", [128, 16384], BF16)

    c_cbc = sb("c_cbc", [128, 256], F32)
    c_e1 = sb("c_e1", [128, 256], F32)
    c_e2 = sb("c_e2", [128, 256], F32)
    c_dcy = sb("c_dcy", [128, 256], F32)
    c_dcyT = sb("c_dcyT", [128, 256], F32)
    c_xg = sb("c_xg", [128, 256], F32)
    c_A = sb("c_A", [128, 256], BF16)
    c_AT = sb("c_AT", [128, 256], BF16)
    c_aod = sb("c_aod", [128, 256], BF16)
    c_W = sb("c_W", [128, 256], BF16)
    c_intraT = sb("c_intraT", [128, 256], BF16)
    c_kdec = sb("c_kdec", [128, 256], BF16)
    c_vnew = sb("c_vnew", [128, 256], BF16)
    c_X0 = sb("c_X0", [128, 512], BF16)
    c_Xsb = sb("c_Xsb", [128, 512], BF16)
    c_T0 = sb("c_T0", [128, 512], BF16)
    c_T1 = sb("c_T1", [128, 512], BF16)
    c_ntok = sb("c_ntok", [128, 256], BF16)
    c_scrap = sb("c_scrap", [128, 256], BF16)
    c_kcdT = sb("c_kcdT", [128, 256], BF16)
    c_cols = sb("c_cols", [128, 8], F32)

    pBk = [es.enter_context(nc.psum_tensor(f"pb{i}", [128, 512], F32)) for i in range(7)]
    pBF = es.enter_context(nc.psum_tensor("pbf", [128, 1024], BF16))

    em = Emitter()
    gB = [Guard() for _ in range(7)]
    gBF = Guard()
    _g = defaultdict(Guard)

    def G(nm):
        return _g[nm]

    def dma_y(build, sem, reads=(), writes=()):
        return em.gop("y", build, reads=reads, writes=writes, sem=sem)

    def dma_g(build, sem, reads=(), writes=()):
        return em.gop("g", build, reads=reads, writes=writes, sem=sem)

    CONST_VALS = (LN_EPS, RMS_EPS, 1.0, -0.5 * LN128, 0.0)
    for ci, cv in enumerate(CONST_VALS):
        nc.const_aps.register(F32, cv, constcol[:, ci:ci + 1])
    dma_y(lambda e, d=constcol, s=consts_d: e.dma_start(out=d[:], in_=s), "dxi", writes=[G("constcol")])
    dma_y(lambda e, d=identb, s=identb_d: e.dma_start(out=d[:], in_=s), "dxi", writes=[G("identb")])
    dma_y(lambda e, d=identf, s=identf_d: e.dma_start(out=d[:], in_=s), "dxi", writes=[G("identf")])
    dma_y(lambda e, d=ones1f, s=ones1f_d: e.dma_start(out=d[:], in_=s), "dxi", writes=[G("ones1f")])
    dma_y(lambda e, d=onescolb, s=onescolb_d: e.dma_start(out=d[:], in_=s), "dxi", writes=[G("onescolb")])
    dma_y(lambda e, d=masks, s=masks_d: e.dma_start(out=d[:], in_=s), "dxi", writes=[G("masks")])
    dma_y(lambda e, d=iotaE, s=iotaE_d: e.dma_start(out=d[:], in_=s), "dxi", writes=[G("iotaE")])
    dma_y(lambda e, d=nwrow, s=nw_d: e.dma_start(out=d[:], in_=s.rearrange("(o f) -> o f", o=1)),
          "dxi", writes=[G("nwrow")])
    dma_y(lambda e, d=ebrow, s=ebias_d: e.dma_start(out=d[:], in_=s.rearrange("(o f) -> o f", o=1)),
          "dxi", writes=[G("ebrow")])
    dma_y(lambda e, d=gatew_sb, s=gatew_d: e.dma_start(
        out=d[:].rearrange("p (k f) -> p k f", k=16),
        in_=s.rearrange("(k p) f -> p k f", p=128)), "dxi", writes=[G("gatew_sb")])
    dma_g(lambda e, d=wb_sb, s=wb_d: e.dma_start(
        out=d[:].rearrange("p (k f) -> p k f", k=16),
        in_=s.rearrange("(k p) f -> p k f", p=128)), "dwi", writes=[G("wb_sb")])
    dma_g(lambda e, d=wa_sb, s=wa_d: e.dma_start(
        out=d[:].rearrange("p (k f) -> p k f", k=16),
        in_=s.rearrange("(k p) f -> p k f", p=128)), "dwi", writes=[G("wa_sb")])
    dma_y(lambda e, d=cw_sb, s=cw_d: e.dma_start(
        out=d[:].rearrange("p (k f) -> p k f", k=64),
        in_=s.rearrange("(k p) f -> p k f", p=128)), "dxi", writes=[G("cw_sb")])
    dma_y(lambda e, d=dtb_sb, s=dtb_d: e.dma_start(out=d[:], in_=s.rearrange("(p o) -> p o", o=1)),
          "dxi", writes=[G("dtb_sb")])
    dma_y(lambda e, d=alog_sb, s=alog_d: e.dma_start(out=d[:], in_=s.rearrange("(p o) -> p o", o=1)),
          "dxi", writes=[G("alog_sb")])

    em.barrier()
    em.gop("t", lambda e: e.matmul(pBk[5][0:128, 0:128], ones1f[:], nwrow[:], start=True, stop=True),
           reads=[G("ones1f"), G("nwrow")], writes=[gB[5]])
    em.gop("s", lambda e: e.copy(out=nwbc[:], in_=pBk[5][0:128, 0:128]),
           reads=[gB[5]], writes=[G("nwbc")])
    em.gop("t", lambda e: e.matmul(pBk[5][0:128, 128:128 + NE], ones1f[:], ebrow[:],
                                   start=True, stop=True),
           reads=[G("ones1f"), G("ebrow")], writes=[gB[5]])
    em.gop("s", lambda e: e.copy(out=ebbc[:], in_=pBk[5][0:128, 128:128 + NE]),
           reads=[gB[5]], writes=[G("ebbc")])
    em.gop("s", lambda e: e.activation(out=negea_sb[:], in_=alog_sb[:], func=AF.Exp),
           reads=[G("alog_sb")], writes=[G("negea_sb")])
    em.gop("v", lambda e: e.tensor_scalar(out=negea_sb[:], in0=negea_sb[:], scalar1=-1.0,
                                          scalar2=None, op0=ALU.mult),
           reads=[], writes=[G("negea_sb")])
    em.gop("v", lambda e: e.memset(onesHM[:], 1.0), writes=[G("onesHM")])
    em.gop("v", lambda e: e.memset(st_bf[:], 0.0), writes=[G("st_bf")])
    em.gop("v", lambda e: e.memset(mixstash[:], 0.0), writes=[G("mixstash")])
    em.barrier()

    def phase0():
        xa = [arN4[:, 0:2048], arN4[:, 2048:4096]]
        gXa = [G("p0x0"), G("p0x1")]
        xc = arN4[:, 4096:6144]
        xn = arN4[:, 6144:8192]
        gXc, gXn = G("p0xc"), G("p0xn")
        for tt in range(TTL):
            xab, gx = xa[tt % 2], gXa[tt % 2]
            dma_y(lambda e, d=xab, s=x_d[tt * 128:(tt + 1) * 128, :]: e.dma_start(out=d, in_=s),
                  f"dxp0{tt % 2}", writes=[gx])
            em.gop("v", lambda e, s=xab: e.reduce_sum(out=colsc[:, 0:1], in_=s, axis=AX.X),
                   reads=[gx], writes=[G("c0")])
            em.gop("v", lambda e: e.tensor_scalar(out=colsc[:, 1:2], in0=colsc[:, 0:1],
                                                  scalar1=1.0 / D, scalar2=None, op0=ALU.mult),
                   reads=[G("c0")], writes=[G("c1")])
            em.gop("v", lambda e, s=xab: e.tensor_scalar(out=xc, in0=s, scalar1=colsc[:, 1:2],
                                                         scalar2=None, op0=ALU.subtract),
                   reads=[gx, G("c1")], writes=[gXc])
            em.gop("v", lambda e: e.scalar_tensor_tensor(out=xn, in0=xc, scalar=1.0, in1=xc,
                                                         op0=ALU.mult, op1=ALU.mult,
                                                         accum_out=colsc[:, 2:3]),
                   reads=[gXc], writes=[gXn, G("c2")])
            em.gop("s", lambda e: e.activation(out=colsc[:, 3:4], in_=colsc[:, 2:3], func=AF.Ln,
                                               scale=1.0 / D, bias=LN_EPS),
                   reads=[G("c2")], writes=[G("c3")])
            em.gop("s", lambda e: e.activation(out=colsc[:, 4:5], in_=colsc[:, 3:4], func=AF.Exp,
                                               scale=-0.5),
                   reads=[G("c3")], writes=[G("c4")])
            em.gop("v", lambda e: e.tensor_scalar(out=xn, in0=xc, scalar1=colsc[:, 4:5],
                                                  scalar2=None, op0=ALU.mult),
                   reads=[gXc, G("c4")], writes=[gXn])
            for k in range(16):
                stf = xc[:, 128 * (k % 2):128 * (k % 2 + 1)]
                stb = arN5[:, 128 * (k % 2):128 * (k % 2 + 1)]
                gstf, gstb = G(f"p0sf{k % 2}"), G(f"p0sb{k % 2}")
                em.gop("t", lambda e, kk=k: e.transpose(pBk[0][0:128, 0:128],
                                                        xn[:, kk * 128:(kk + 1) * 128], identf[:]),
                       reads=[gXn, G("identf")], writes=[gB[0]])
                em.gop("s", lambda e, d=stf: e.copy(out=d, in_=pBk[0][0:128, 0:128]),
                       reads=[gB[0]], writes=[gstf, gXc])
                em.gop("t", lambda e, d=stf, kk=k: e.matmul(
                    pBk[1][0:128, 0:NE], d, gatew_sb[:, kk * NE:(kk + 1) * NE],
                    start=(kk == 0), stop=(kk == 15)),
                    reads=[gstf, G("gatew_sb")], writes=[gB[1]])
                em.gop("v", lambda e, d=stb, s2=stf: e.tensor_copy(out=d, in_=s2),
                       reads=[gstf], writes=[gstb])
                dma_y(lambda e, s2=stb, d=xnT_d[k, :, tt * 128:(tt + 1) * 128]:
                      e.dma_start(out=d, in_=s2), f"dxst{k % 2}", reads=[gstb])
            gl = gsm[:, 0:NE]; e0 = gsm[:, NE:2 * NE]; t1 = gsm[:, 2 * NE:3 * NE]
            sc = gsm[:, 3 * NE:4 * NE]; bsd = gsm[:, 4 * NE:5 * NE]
            eqv = gsm[:, 5 * NE:6 * NE]; cnd = gsm[:, 6 * NE:7 * NE]
            msk1 = gsm[:, 7 * NE:8 * NE]; msk2 = gsm[:, 8 * NE:9 * NE]
            l0 = gsm[:, 9 * NE:10 * NE]
            gg = G("gsm")
            em.gop("v", lambda e: e.tensor_copy(out=gl, in_=pBk[1][0:128, 0:NE]),
                   reads=[gB[1]], writes=[gg])
            em.gop("v", lambda e: e.tensor_reduce(out=colsc[:, 5:6], in_=gl, axis=AX.X,
                                                  op=ALU.max, negate=True),
                   reads=[gg], writes=[G("c5")])
            em.gop("s", lambda e: e.activation(out=e0, in_=gl, func=AF.Exp, bias=colsc[:, 5:6]),
                   reads=[gg, G("c5")], writes=[gg])
            em.gop("s", lambda e: e.activation(out=l0, in_=e0, func=AF.Ln),
                   reads=[gg], writes=[gg])
            em.gop("v", lambda e: e.tensor_scalar(out=t1, in0=gl, scalar1=colsc[:, 5:6],
                                                  scalar2=None, op0=ALU.add),
                   reads=[gg, G("c5")], writes=[gg])
            em.gop("v", lambda e: e.tensor_tensor(out=t1, in0=t1, in1=l0, op=ALU.subtract),
                   reads=[gg], writes=[gg])
            em.gop("v", lambda e: e.tensor_scalar(out=t1, in0=t1, scalar1=1.0, scalar2=None,
                                                  op0=ALU.add), reads=[gg], writes=[gg])
            em.gop("v", lambda e: e.tensor_tensor(out=e0, in0=e0, in1=t1, op=ALU.mult),
                   reads=[gg], writes=[gg])
            em.gop("v", lambda e: e.reduce_sum(out=colsc[:, 6:7], in_=e0, axis=AX.X),
                   reads=[gg], writes=[G("c6")])
            em.gop("v", lambda e: e.reciprocal(out=colsc[:, 7:8], in_=colsc[:, 6:7]),
                   reads=[G("c6")], writes=[G("c7")])
            em.gop("v", lambda e: e.tensor_scalar(out=sc, in0=e0, scalar1=colsc[:, 7:8],
                                                  scalar2=None, op0=ALU.mult),
                   reads=[gg, G("c7")], writes=[gg])
            em.gop("v", lambda e: e.tensor_tensor(out=bsd, in0=sc, in1=ebbc[:, 0:NE], op=ALU.add),
                   reads=[gg, G("ebbc")], writes=[gg])
            for r, (mskT, c_a, c_b) in enumerate(((msk1, 8, 9), (msk2, 10, 11))):
                em.gop("v", lambda e: e.tensor_reduce(out=colsc[:, 12:13], in_=bsd, axis=AX.X,
                                                      op=ALU.max),
                       reads=[gg], writes=[G("c12")])
                em.gop("v", lambda e: e.tensor_scalar(out=eqv, in0=bsd, scalar1=colsc[:, 12:13],
                                                      scalar2=None, op0=ALU.is_ge),
                       reads=[gg, G("c12")], writes=[gg])
                em.gop("v", lambda e: e.scalar_tensor_tensor(out=cnd, in0=eqv, scalar=-1000.0,
                                                             in1=iotaE[:, 0:NE], op0=ALU.mult,
                                                             op1=ALU.add),
                       reads=[gg, G("iotaE")], writes=[gg])
                em.gop("v", lambda e, ca=c_a: e.tensor_reduce(out=colsc[:, ca:ca + 1], in_=cnd,
                                                              axis=AX.X, op=ALU.min),
                       reads=[gg], writes=[G("c8")])
                em.gop("v", lambda e, ca=c_a, cb=c_b: e.tensor_scalar(
                    out=colsc[:, cb:cb + 1], in0=colsc[:, ca:ca + 1], scalar1=1000.0,
                    scalar2=None, op0=ALU.add), reads=[G("c8")], writes=[G("c9")])
                em.gop("v", lambda e, m=mskT, cb=c_b: e.tensor_scalar(
                    out=m, in0=iotaE[:, 0:NE], scalar1=colsc[:, cb:cb + 1], scalar2=None,
                    op0=ALU.is_equal), reads=[G("iotaE"), G("c9")], writes=[gg])
                if r == 0:
                    em.gop("v", lambda e: e.scalar_tensor_tensor(out=bsd, in0=msk1, scalar=-1e30,
                                                                 in1=bsd, op0=ALU.mult,
                                                                 op1=ALU.add),
                           reads=[gg], writes=[gg])
            em.gop("v", lambda e: e.tensor_tensor(out=msk1, in0=msk1, in1=msk2, op=ALU.add),
                   reads=[gg], writes=[gg])
            em.gop("v", lambda e, tt_=tt: e.tensor_tensor(out=wexp[:, tt_ * NE:(tt_ + 1) * NE],
                                                          in0=sc, in1=msk1, op=ALU.mult),
                   reads=[gg], writes=[G("wexp")])

    def xt_load(dst, gdst, c0, c1, nseg=4, extra_reads=()):
        w = c1 - c0
        per = 16 // nseg
        for i in range(nseg):
            dma_y(lambda e, i_=i, d=dst, w_=w, p_=per: e.dma_start(
                out=d[:, i_ * p_ * w_:(i_ + 1) * p_ * w_].rearrange("p (k c) -> p k c", k=p_),
                in_=xnT_d[i_ * p_:(i_ + 1) * p_, :, c0:c1].rearrange("k p c -> p k c")),
                "dxt", reads=list(extra_reads), writes=[gdst])

    def wblock_load(dst_ap, gdst, srcblk, b, width, sem):
        gds = list(gdst) if isinstance(gdst, (list, tuple)) else [gdst]
        dma_g(lambda e, d=dst_ap, s=srcblk[b]: e.dma_start(out=d[:, 0:width], in_=s),
              sem, writes=gds)

    def phaseB():
        for tb in range(NFB):
            W = TBF
            c0 = tb * W
            xts = arN3[:, 0:16 * W]
            gx = G("pbxts")
            xt_load(xts, gx, c0, c0 + W)
            for (wsb, gnm, po) in ((wb_sb, "wb_sb", 0), (wa_sb, "wa_sb", 32)):
                em.group("t", [
                    (lambda e, kk=k, ws=wsb, po_=po, W_=W: e.matmul(
                        pBk[2][po_:po_ + 32, 0:W_], ws[:, kk * HV:kk * HV + HV],
                        xts[:, kk * W_:(kk + 1) * W_], start=(kk == 0), stop=(kk == 15)))
                    for k in range(16)],
                    reads=[gx, G(gnm)], writes=[gB[2]])
            em.gop("s", lambda e, W_=W: e.activation(out=hmA[:, 0:W_], in_=pBk[2][0:32, 0:W_],
                                                     func=AF.Sigmoid),
                   reads=[gB[2]], writes=[G("hmA")])
            em.gop("s", lambda e, W_=W: e.activation(out=hmB[:, 0:W_], in_=pBk[2][32:64, 0:W_],
                                                     func=AF.Exp, bias=dtb_sb[0:32, :]),
                   reads=[gB[2], G("dtb_sb")], writes=[G("hmB")])
            em.gop("s", lambda e, W_=W: e.activation(out=hmC[:, 0:W_], in_=hmB[:, 0:W_],
                                                     func=AF.Ln, bias=1.0),
                   reads=[G("hmB")], writes=[G("hmC")])
            em.gop("v", lambda e, W_=W: e.tensor_scalar(out=hmB[:, 0:W_], in0=hmC[:, 0:W_],
                                                        scalar1=negea_sb[0:32, :], scalar2=None,
                                                        op0=ALU.mult),
                   reads=[G("hmC"), G("negea_sb")], writes=[G("hmB")])
            ncpb = W // CH
            for c in range(ncpb):
                em.gop("v", lambda e, cc=c: e.tensor_tensor_scan(
                    out=hmC[:, cc * CH:(cc + 1) * CH], data0=onesHM[:, 0:CH],
                    data1=hmB[:, cc * CH:(cc + 1) * CH], initial=0.0,
                    op0=ALU.mult, op1=ALU.add),
                    reads=[G("hmB"), G("onesHM")], writes=[G("hmC")])
            for c in range(ncpb):
                em.gop("v", lambda e, cc=c: e.tensor_scalar(
                    out=hmB[:, cc * CH:(cc + 1) * CH], in0=hmC[:, cc * CH:(cc + 1) * CH],
                    scalar1=hmC[:, cc * CH + CH - 1:cc * CH + CH], scalar2=-1.0,
                    op0=ALU.subtract, op1=ALU.mult),
                    reads=[G("hmC")], writes=[G("hmB")])
            em.gop("s", lambda e, W_=W: e.activation(out=hmB[:, 0:W_], in_=hmB[:, 0:W_],
                                                     func=AF.Exp),
                   reads=[G("hmB")], writes=[G("hmB")])
            for c in range(ncpb):
                tt = tb * ncpb + c
                cs = slice(tt * HV, (tt + 1) * HV)
                for (src, gs, dst, gd, ofs) in (
                        (hmA, "hmA", bt, "bt", 0),
                        (hmC, "hmC", gjt, "gjt", 128),
                        (hmB, "hmB", eglmt, "eglmt", 256)):
                    em.gop("t", lambda e, s2=src, cc=c, o=ofs: e.transpose(
                        pBk[0][0:128, o:o + HV], s2[:, cc * CH:(cc + 1) * CH],
                        identf[0:HV, 0:HV]),
                        reads=[G(gs), G("identf")], writes=[gB[0]])
                    em.gop("s", lambda e, d=dst, cs_=cs, o=ofs: e.copy(
                        out=d[:, cs_], in_=pBk[0][0:128, o:o + HV]),
                        reads=[gB[0]], writes=[G(gd)])
                em.gop("t", lambda e, cc=c: e.matmul(
                    pBk[0][0:1, 384:384 + HV],
                    hmC[:, cc * CH + CH - 1:cc * CH + CH], identf[0:HV, 0:HV],
                    start=True, stop=True),
                    reads=[G("hmC"), G("identf")], writes=[gB[0]])
                em.gop("s", lambda e: e.copy(out=rowsc[0:1, 256:256 + HV],
                                             in_=pBk[0][0:1, 384:384 + HV]),
                       reads=[gB[0]], writes=[G("rowsc")])
                em.gop("t", lambda e: e.matmul(pBk[0][0:128, 384:384 + HV], ones1f[:],
                                               rowsc[0:1, 256:256 + HV],
                                               start=True, stop=True),
                       reads=[G("rowsc"), G("ones1f")], writes=[gB[0]])
                em.gop("s", lambda e, cs_=cs: e.activation(out=eglall[:, cs_],
                                                           in_=pBk[0][0:128, 384:384 + HV],
                                                           func=AF.Exp),
                       reads=[gB[0]], writes=[G("eglall")])
                em.gop("v", lambda e, cs_=cs: e.tensor_scalar(out=negbt[:, cs_], in0=bt[:, cs_],
                                                              scalar1=-1.0, scalar2=None,
                                                              op0=ALU.mult),
                       reads=[G("bt")], writes=[G("negbt")])
                em.gop("s", lambda e, cs_=cs: e.activation(out=egjt[:, cs_], in_=gjt[:, cs_],
                                                           func=AF.Exp),
                       reads=[G("gjt")], writes=[G("egjt")])
                em.gop("v", lambda e, cs_=cs: e.tensor_tensor(out=begjt[:, cs_], in0=bt[:, cs_],
                                                              in1=egjt[:, cs_], op=ALU.mult),
                       reads=[G("bt"), G("egjt")], writes=[G("begjt")])

    def phaseF():
        W = TBF
        NT = W // 128
        assert 16 * W <= 8192
        g_acc = arN4[:, 0:16 * W]
        gGacc = G("gacc")
        xts = arN3[:, 0:16 * W]
        gx = G("pfxts")
        Xe = arN1[:, 0:16 * W]
        hbuf = arN1[:, 8192:8192 + 14 * W]
        h1a = arN1[:, 8192 + 14 * W:8192 + 15 * W]
        gXe, gH, gH1 = G("Xe"), G("hbuf"), G("h1a")
        wbc = [arN5[:, 0:W], arN5[:, W:2 * W]]
        gtokst = arN5[:, 2048:4096]
        gWbc = [G("wbc0"), G("wbc1")]
        gGt = G("gtokst")
        Xe2 = [Xe, arN5[:, 4096:4096 + 16 * W]]
        gXe2 = [gXe, G("Xe1")]
        wA = [arN2[:, 0:4096], arN2[:, 4096:8192]]
        wB = [arN2[:, 8192:12288], arN2[:, 12288:16384]]
        w2bl = [arN2[:, 0:7168], arN2[:, 8192:8192 + 7168]]
        gwA = [G("wA0"), G("wA1")]
        gwB = [G("wB0"), G("wB1")]
        gw2 = [G("w20"), G("w21")]
        NFG = HID // 256

        for tb in range(NFB):
            c0 = tb * W
            xt_load(xts, gx, c0, c0 + W)
            em.gop("v", lambda e, d=g_acc: e.memset(d, 0.0), reads=[], writes=[gGacc])
            def emit_xe(ei2):
                par = ei2 % 2
                for i in range(NT):
                    tt = tb * NT + i
                    em.gop("v", lambda e, tt_=tt, ei_=ei2: e.tensor_scalar(
                        out=c_cbc[:, 0:128], in0=masks[:, 128 * M_ONES:128 * (M_ONES + 1)],
                        scalar1=wexp[:, tt_ * NE + ei_:tt_ * NE + ei_ + 1],
                        scalar2=None, op0=ALU.mult),
                        reads=[G("masks"), G("wexp")], writes=[G("c_cbc")])
                    em.gop("t", lambda e: e.transpose(pBk[5][0:128, 0:128], c_cbc[:, 0:128],
                                                      identf[:]),
                           reads=[G("c_cbc"), G("identf")], writes=[gB[5]])
                    em.gop("s", lambda e, i_=i, w_=wbc[par]: e.copy(
                        out=w_[:, i_ * 128:(i_ + 1) * 128], in_=pBk[5][0:128, 0:128]),
                           reads=[gB[5]], writes=[gWbc[par]])
                for k in range(16):
                    em.gop("v", lambda e, kk=k, W_=W, d=Xe2[par], w_=wbc[par]: e.tensor_tensor(
                        out=d[:, kk * W_:(kk + 1) * W_], in0=xts[:, kk * W_:(kk + 1) * W_],
                        in1=w_[:, 0:W_], op=ALU.mult),
                        reads=[gx, gWbc[par]], writes=[gXe2[par]])

            for ei in range(-1, NE):
                shared = ei < 0
                mw1, mw3, mw2 = (sw1_d, sw3_d, sw2_d) if shared else (w1_d[ei], w3_d[ei], w2_d[ei])
                if ei + 1 < NE:
                    emit_xe(ei + 1)
                if not shared:
                    xin, gxin = Xe2[ei % 2], gXe2[ei % 2]
                else:
                    xin, gxin = xts, gx
                for fg in range(NFG):
                    bsl = fg % 2
                    wblock_load(wA[bsl], gwA[bsl], mw1, fg, 16 * 256, f"dwW{bsl}")
                    wblock_load(wB[bsl], gwB[bsl], mw3, fg, 16 * 256, f"dwB{bsl}")
                    for fi in range(2):
                        f = fg * 2 + fi
                        pb1, gp1 = (pBk[2], gB[2]) if fi == 0 else (pBk[4], gB[4])
                        pb3, gp3 = (pBk[3], gB[3]) if fi == 0 else (pBk[6], gB[6])
                        em.group("t", [
                            (lambda e, kk=k, fi_=fi, b=wA[bsl], p=pb1, xi=xin, W_=W: e.matmul(
                                p[0:128, 0:W_],
                                b[:, kk * 256 + fi_ * 128:kk * 256 + (fi_ + 1) * 128],
                                xi[:, kk * W_:(kk + 1) * W_], start=(kk == 0), stop=(kk == 15)))
                            for k in range(16)],
                            reads=[gwA[bsl], gxin], writes=[gp1])
                        em.group("t", [
                            (lambda e, kk=k, fi_=fi, b=wB[bsl], p=pb3, xi=xin, W_=W: e.matmul(
                                p[0:128, 0:W_],
                                b[:, kk * 256 + fi_ * 128:kk * 256 + (fi_ + 1) * 128],
                                xi[:, kk * W_:(kk + 1) * W_], start=(kk == 0), stop=(kk == 15)))
                            for k in range(16)],
                            reads=[gwB[bsl], gxin], writes=[gp3])
                        em.gop("s", lambda e, p=pb1, W_=W: e.activation(
                            out=h1a, in_=p[0:128, 0:W_], func=AF.Silu),
                            reads=[gp1], writes=[gH1])
                        em.gop("v", lambda e, p=pb3, f_=f, W_=W: e.tensor_tensor(
                            out=hbuf[:, f_ * W_:(f_ + 1) * W_], in0=p[0:128, 0:W_], in1=h1a,
                            op=ALU.mult),
                            reads=[gp3, gH1], writes=[gH])
                for cb in range(4):
                    bsl = cb % 2
                    w2g = [gwA[0], gwA[1]] if bsl == 0 else [gwB[0], gwB[1]]
                    wblock_load(w2bl[bsl], w2g, mw2, cb, 14 * 512, f"dw2{bsl}")
                    for fi in range(4):
                        f2 = cb * 4 + fi
                        pbo, gpo = (pBk[0], gB[0]) if fi % 2 == 0 else (pBk[1], gB[1])
                        em.group("t", [
                            (lambda e, kk=k, fi_=fi, b=w2bl[bsl], p=pbo, W_=W: e.matmul(
                                p[0:128, 0:W_],
                                b[:, kk * 512 + fi_ * 128:kk * 512 + (fi_ + 1) * 128],
                                hbuf[:, kk * W_:(kk + 1) * W_], start=(kk == 0), stop=(kk == 13)))
                            for k in range(14)],
                            reads=w2g + [gH], writes=[gpo])
                        em.gop("v", lambda e, p=pbo, f2_=f2, W_=W: e.tensor_tensor(
                            out=g_acc[:, f2_ * W_:(f2_ + 1) * W_],
                            in0=g_acc[:, f2_ * W_:(f2_ + 1) * W_],
                            in1=p[0:128, 0:W_], op=ALU.add),
                            reads=[gpo], writes=[gGacc])
            for i in range(NT):
                for kf in range(16):
                    em.gop("t", lambda e, i_=i, kf_=kf, W_=W: e.transpose(
                        pBk[5][0:128, 0:128],
                        g_acc[:, kf_ * W_ + i_ * 128:kf_ * W_ + (i_ + 1) * 128], identf[:]),
                        reads=[gGacc, G("identf")], writes=[gB[5]])
                    em.gop("s", lambda e, kf_=kf: e.activation(
                        out=gtokst[:, kf_ * 128:(kf_ + 1) * 128], in_=pBk[5][0:128, 0:128],
                        func=AF.Silu),
                        reads=[gB[5]], writes=[gGt])
                dma_y(lambda e, i_=i, t0=c0: e.dma_start(
                    out=g_d[t0 + i_ * 128:t0 + (i_ + 1) * 128, :], in_=gtokst[:, 0:2048]),
                    "dxgd", reads=[gGt])

    def core_front(ab, c, hv):
        j = ab * CPAB + c
        tt = j
        kh = hv // 2
        par = hv % 2
        qT = arN1[:, kh * AB + c * 128:kh * AB + (c + 1) * 128]
        kT = arN1[:, (16 + kh) * AB + c * 128:(16 + kh) * AB + (c + 1) * 128]
        vT = arN1[:, (32 + hv) * AB + c * 128:(32 + hv) * AB + (c + 1) * 128]
        gQKV = G("qkvT")
        col = lambda t_: t_[:, tt * HV + hv:tt * HV + hv + 1]
        sil = arN5[:, c * 4096 + hv * 128:c * 4096 + (hv + 1) * 128]
        sthv = st_bf[:, hv * 128:(hv + 1) * 128]
        cb0 = par * 4
        mk = lambda idx: masks[:, 128 * idx:128 * (idx + 1)]
        P = lambda t_: t_[:, par * 128:(par + 1) * 128]
        P2 = lambda t_: t_[:, par * 256:(par + 1) * 256]
        Gn = lambda nm: G(f"{nm}{par}")
        bRGQ, gRGQ = (pBk[0], gB[0]) if par == 0 else (pBk[1], gB[1])
        bWX, gWX = (pBk[2], gB[2]) if par == 0 else (pBk[5], gB[5])
        bLAD, gLAD = (pBk[3], gB[3]) if par == 0 else (pBk[6], gB[6])
        ctx = dict(j=j, tt=tt, c=c, hv=hv, par=par, qT=qT, kT=kT, sil=sil, sthv=sthv,
                   cb0=cb0, bWX=bWX, gWX=gWX, col=col, mk=mk, P=P, P2=P2, Gn=Gn)

        em.gop("v", lambda e: e.tensor_scalar(out=P(c_cbc), in0=mk(M_ONES),
                                              scalar1=col(gjt), scalar2=None, op0=ALU.mult),
               reads=[G("masks"), G("gjt")], writes=[Gn("c_cbc")])
        em.gop("t", lambda e: e.transpose(bRGQ[0:128, 0:128], P(c_cbc), identf[:]),
               reads=[Gn("c_cbc"), G("identf")], writes=[gRGQ])
        em.gop("t", lambda e: e.matmul(bRGQ[0:128, 128:256], kT, kT, start=True, stop=True),
               reads=[gQKV], writes=[gRGQ])
        em.gop("t", lambda e: e.matmul(bRGQ[0:128, 256:384], kT, qT, start=True, stop=True),
               reads=[gQKV], writes=[gRGQ])
        em.gop("v", lambda e: e.tensor_scalar(out=P(c_e1), in0=bRGQ[0:128, 0:128],
                                              scalar1=col(gjt), scalar2=0.0,
                                              op0=ALU.subtract, op1=ALU.max),
               reads=[gRGQ, G("gjt")], writes=[Gn("c_e1")])
        em.gop("v", lambda e: e.tensor_scalar(out=P(c_e2), in0=bRGQ[0:128, 0:128],
                                              scalar1=col(gjt), scalar2=0.0,
                                              op0=ALU.subtract, op1=ALU.min),
               reads=[gRGQ, G("gjt")], writes=[Gn("c_e2")])
        em.gop("s", lambda e: e.activation(out=P(c_dcy), in_=P(c_e1), func=AF.Exp, scale=-1.0),
               reads=[Gn("c_e1")], writes=[Gn("c_dcy")])
        em.gop("s", lambda e: e.activation(out=P(c_dcyT), in_=P(c_e2), func=AF.Exp),
               reads=[Gn("c_e2")], writes=[Gn("c_dcyT")])
        if CORE_STEPS < 2:
            return ctx
        em.gop("v", lambda e: e.scalar_tensor_tensor(out=P(c_scrap), in0=bRGQ[0:128, 128:256],
                                                     scalar=col(negbt), in1=P(c_dcy),
                                                     op0=ALU.mult, op1=ALU.mult),
               reads=[gRGQ, G("negbt"), Gn("c_dcy")], writes=[Gn("c_scrap")])
        em.gop("v", lambda e: e.tensor_tensor(out=P(c_A), in0=P(c_scrap), in1=mk(M_STRICT),
                                              op=ALU.mult),
               reads=[Gn("c_scrap"), G("masks")], writes=[Gn("c_A")])
        em.gop("v", lambda e: e.tensor_tensor(out=P(c_scrap), in0=bRGQ[0:128, 256:384],
                                              in1=P(c_dcyT), op=ALU.mult),
               reads=[gRGQ, Gn("c_dcyT")], writes=[Gn("c_scrap")])
        em.gop("v", lambda e: e.tensor_tensor(out=P(c_intraT), in0=P(c_scrap), in1=mk(M_TRIU),
                                              op=ALU.mult),
               reads=[Gn("c_scrap"), G("masks")], writes=[Gn("c_intraT")])
        if CORE_STEPS < 3:
            return ctx
        oAT, oKT, oVT = par * 128, 256 + par * 128, 512 + par * 128
        em.gop("t", lambda e, o=oAT: e.transpose(pBF[0:128, o:o + 128], P(c_A), identb[:]),
               reads=[Gn("c_A"), G("identb")], writes=[gBF])
        em.gop("s", lambda e, o=oAT: e.copy(out=P(c_AT), in_=pBF[0:128, o:o + 128]),
               reads=[gBF], writes=[Gn("c_AT")])
        em.gop("t", lambda e, o=oKT: e.transpose(pBF[0:128, o:o + 128], kT, identb[:]),
               reads=[gQKV, G("identb")], writes=[gBF])
        em.gop("t", lambda e, o=oVT: e.transpose(pBF[0:128, o:o + 128], vT, identb[:]),
               reads=[gQKV, G("identb")], writes=[gBF])
        em.gop("v", lambda e, o=oVT: e.tensor_scalar(out=P2(c_X0)[:, 0:128],
                                                     in0=pBF[0:128, o:o + 128],
                                                     scalar1=col(bt), scalar2=None, op0=ALU.mult),
               reads=[gBF, G("bt")], writes=[Gn("c_X0")])
        em.gop("v", lambda e, o=oKT: e.tensor_scalar(out=P2(c_X0)[:, 128:256],
                                                     in0=pBF[0:128, o:o + 128],
                                                     scalar1=col(begjt), scalar2=None,
                                                     op0=ALU.mult),
               reads=[gBF, G("begjt")], writes=[Gn("c_X0")])
        em.gop("v", lambda e, o=oKT: e.tensor_scalar(out=P(c_kdec), in0=pBF[0:128, o:o + 128],
                                                     scalar1=col(eglmt), scalar2=None,
                                                     op0=ALU.mult),
               reads=[gBF, G("eglmt")], writes=[Gn("c_kdec")])
        if CORE_STEPS < 4:
            return ctx
        TB = [P2(c_T0), P2(c_T1)]
        gTB = [Gn("c_T0"), Gn("c_T1")]
        em.gop("v", lambda e: e.tensor_tensor(out=P(c_scrap), in0=P(c_A), in1=mk(M_OD1),
                                              op=ALU.mult),
               reads=[Gn("c_A"), G("masks")], writes=[Gn("c_scrap")])
        em.gop("v", lambda e: e.tensor_tensor(out=TB[0][:, 0:128], in0=P(c_scrap), in1=identf[:],
                                              op=ALU.add),
               reads=[Gn("c_scrap"), G("identf")], writes=[gTB[0]])
        em.gop("v", lambda e: e.tensor_tensor(out=P(c_scrap), in0=P(c_AT), in1=mk(M_ODT[1]),
                                              op=ALU.mult),
               reads=[Gn("c_AT"), G("masks")], writes=[Gn("c_scrap")])
        em.gop("v", lambda e: e.tensor_tensor(out=TB[0][:, 128:256], in0=P(c_scrap), in1=identf[:],
                                              op=ALU.add),
               reads=[Gn("c_scrap"), G("identf")], writes=[gTB[0]])
        cur = 0
        for s in (2, 4, 8, 16, 32, 64):
            nxt = 1 - cur
            Tc, gTc = TB[cur], gTB[cur]
            Tn, gTn = TB[nxt], gTB[nxt]
            em.gop("v", lambda e, s_=s: e.tensor_tensor(out=P(c_aod), in0=P(c_AT),
                                                        in1=mk(M_ODT[s_]), op=ALU.mult),
                   reads=[Gn("c_AT"), G("masks")], writes=[Gn("c_aod")])
            em.gop("t", lambda e, Tc_=Tc: e.matmul(bWX[0:128, 0:128], P(c_aod), Tc_[:, 0:128],
                                                   start=True, stop=True),
                   reads=[Gn("c_aod"), gTc], writes=[gWX])
            em.gop("s", lambda e: e.copy(out=P(c_W), in_=bWX[0:128, 0:128]),
                   reads=[gWX], writes=[Gn("c_W")])
            em.group("t", [
                (lambda e, Tc_=Tc: e.matmul(bLAD[0:128, 0:128], Tc_[:, 128:256], P(c_W),
                                            start=True, stop=False)),
                (lambda e, Tc_=Tc: e.matmul(bLAD[0:128, 0:128], identb[:], Tc_[:, 0:128],
                                            start=False, stop=True)),
                (lambda e, Tc_=Tc: e.matmul(bLAD[0:128, 128:256], P(c_W), Tc_[:, 128:256],
                                            start=True, stop=False)),
                (lambda e, Tc_=Tc: e.matmul(bLAD[0:128, 128:256], identb[:],
                                            Tc_[:, 128:256], start=False, stop=True))],
                reads=[Gn("c_W"), gTc, G("identb")], writes=[gLAD])
            em.gop("v", lambda e, Tn_=Tn: e.tensor_copy(out=Tn_[:, 0:256],
                                                        in_=bLAD[0:128, 0:256]),
                   reads=[gLAD], writes=[gTn])
            cur = nxt
        if CORE_STEPS < 5:
            return ctx
        Tfin, gTfin = TB[cur], gTB[cur]
        em.gop("t", lambda e, Tf=Tfin: e.matmul(bWX[0:128, 256:512], Tf[:, 128:256], P2(c_X0),
                                                start=True, stop=True),
               reads=[gTfin, Gn("c_X0")], writes=[gWX])
        em.gop("v", lambda e: e.tensor_copy(out=P2(c_Xsb), in_=bWX[0:128, 256:512]),
               reads=[gWX], writes=[Gn("c_Xsb")])
        return ctx

    def core_back(ctx):
        if CORE_STEPS < 6:
            return
        j = ctx["j"]; par = ctx["par"]; hv = ctx["hv"]; c = ctx["c"]; cb0 = ctx["cb0"]
        qT = ctx["qT"]; sil = ctx["sil"]; sthv = ctx["sthv"]; col = ctx["col"]
        P = ctx["P"]; P2 = ctx["P2"]; Gn = ctx["Gn"]
        gQKV = G("qkvT")
        gSil = G("siluz")
        gSt = G(f"st{hv}")
        gNT = G("normedT")
        oKC = 768 + par * 128
        em.gop("t", lambda e, o=oKC: e.transpose(pBF[0:128, o:o + 128], P2(c_Xsb)[:, 128:256],
                                                 identb[:]),
               reads=[Gn("c_Xsb"), G("identb")], writes=[gBF])
        em.gop("s", lambda e, o=oKC: e.copy(out=P(c_kcdT), in_=pBF[0:128, o:o + 128]),
               reads=[gBF], writes=[Gn("c_kcdT")])
        if j > 0:
            em.gop("t", lambda e: e.matmul(pBk[4][0:128, 0:128], P(c_kcdT), sthv,
                                           start=True, stop=True),
                   reads=[Gn("c_kcdT"), gSt], writes=[gB[4]])
            em.gop("t", lambda e: e.matmul(pBk[4][0:128, 128:256], qT, sthv,
                                           start=True, stop=True),
                   reads=[gQKV, gSt], writes=[gB[4]])
            em.gop("v", lambda e: e.tensor_tensor(out=P(c_vnew), in0=P2(c_Xsb)[:, 0:128],
                                                  in1=pBk[4][0:128, 0:128], op=ALU.subtract),
                   reads=[Gn("c_Xsb"), gB[4]], writes=[Gn("c_vnew")])
            em.gop("v", lambda e: e.tensor_scalar(out=P(c_e1), in0=pBk[4][0:128, 128:256],
                                                  scalar1=col(egjt), scalar2=None,
                                                  op0=ALU.mult),
                   reads=[G("egjt"), gB[4]], writes=[Gn("c_e1")])
            em.gop("t", lambda e: e.matmul(pBk[4][0:128, 384:512], P(c_intraT), P(c_vnew),
                                           start=True, stop=True),
                   reads=[Gn("c_intraT"), Gn("c_vnew")], writes=[gB[4]])
            em.gop("v", lambda e: e.scalar_tensor_tensor(out=P(c_xg), in0=pBk[4][0:128, 384:512],
                                                         scalar=1.0, in1=P(c_e1),
                                                         op0=ALU.mult, op1=ALU.add),
                   reads=[gB[4], Gn("c_e1")], writes=[Gn("c_xg")])
        else:
            em.gop("v", lambda e: e.tensor_copy(out=P(c_vnew), in_=P2(c_Xsb)[:, 0:128]),
                   reads=[Gn("c_Xsb")], writes=[Gn("c_vnew")])
            em.gop("t", lambda e: e.matmul(pBk[4][0:128, 384:512], P(c_intraT), P(c_vnew),
                                           start=True, stop=True),
                   reads=[Gn("c_intraT"), Gn("c_vnew")], writes=[gB[4]])
            em.gop("v", lambda e: e.tensor_copy(out=P(c_xg), in_=pBk[4][0:128, 384:512]),
                   reads=[gB[4]], writes=[Gn("c_xg")])
        if CORE_STEPS < 7:
            return
        em.gop("v", lambda e: e.tensor_tensor(out=P(c_xg), in0=P(c_xg), in1=sil, op=ALU.mult),
               reads=[gSil], writes=[Gn("c_xg")])
        em.gop("v", lambda e, cb=cb0: e.scalar_tensor_tensor(out=P(c_scrap), in0=P(c_xg),
                                                             scalar=1.0, in1=P(c_xg),
                                                             op0=ALU.mult, op1=ALU.mult,
                                                             accum_out=c_cols[:, cb + 1:cb + 2]),
               reads=[Gn("c_xg")], writes=[Gn("c_scrap"), G(f"c_ssq{par}")])
        em.gop("s", lambda e, cb=cb0: e.activation(out=c_cols[:, cb + 2:cb + 3],
                                                   in_=c_cols[:, cb + 1:cb + 2], func=AF.Ln,
                                                   scale=1.0 / 128.0, bias=RMS_EPS),
               reads=[G(f"c_ssq{par}")], writes=[G(f"c_lnssq{par}")])
        em.gop("s", lambda e, cb=cb0: e.activation(out=c_cols[:, cb + 3:cb + 4],
                                                   in_=c_cols[:, cb + 2:cb + 3], func=AF.Exp,
                                                   scale=-0.5),
               reads=[G(f"c_lnssq{par}")], writes=[G(f"c_rstd{par}")])
        em.gop("v", lambda e, cb=cb0: e.scalar_tensor_tensor(out=P(c_ntok), in0=P(c_xg),
                                                             scalar=c_cols[:, cb + 3:cb + 4],
                                                             in1=nwbc[:], op0=ALU.mult,
                                                             op1=ALU.mult),
               reads=[Gn("c_xg"), G(f"c_rstd{par}"), G("nwbc")], writes=[Gn("c_ntok")])
        em.gop("t", lambda e, o=oKC: e.transpose(pBF[0:128, o:o + 128], P(c_ntok), identb[:]),
               reads=[Gn("c_ntok"), G("identb")], writes=[gBF])
        em.gop("s", lambda e, o=oKC,
               d=arN5[:, 8192 + hv * AB + c * 128:8192 + hv * AB + (c + 1) * 128]:
               e.copy(out=d, in_=pBF[0:128, o:o + 128]),
               reads=[gBF], writes=[gNT])
        if CORE_STEPS < 8:
            return
        em.gop("t", lambda e: e.matmul(pBk[4][0:128, 256:384], P(c_kdec), P(c_vnew),
                                       start=True, stop=True),
               reads=[Gn("c_kdec"), Gn("c_vnew")], writes=[gB[4]])
        if j > 0:
            em.gop("v", lambda e: e.scalar_tensor_tensor(out=sthv, in0=sthv,
                                                         scalar=col(eglall),
                                                         in1=pBk[4][0:128, 256:384],
                                                         op0=ALU.mult, op1=ALU.add),
                   reads=[gB[4], G("eglall")], writes=[gSt])
        else:
            em.gop("v", lambda e: e.tensor_copy(out=sthv, in_=pBk[4][0:128, 256:384]),
                   reads=[gB[4]], writes=[gSt])

    def phaseA():
        xts = arN3[:, 0:16 * AB]
        gx = G("paxts")
        qkvT = arN1[:, 0:64 * AB]
        gQKV = G("qkvT")
        wA = [arN2[:, 0:8192], arN2[:, 8192:16384]]
        gwA = [G("pawA0"), G("pawA1")]
        mixsb = [arN4[:, 0:AB + 3], arN4[:, 512:512 + AB + 3]]
        gMix = [G("pamix0"), G("pamix1")]
        convacc = arN4[:, 1024:1024 + AB]
        gCv = G("paconv")
        ct0 = arN4[:, 1280:1280 + AB]
        ct1 = arN4[:, 1536:1536 + AB]
        gCt = G("pact")
        xsl = [arN4[:, 2048:4096], arN4[:, 4096:6144]]
        gXsl = [G("paxsl0"), G("paxsl1")]
        outst = [[arN4[:, 6144:6400], arN4[:, 6400:6656]],
                 [arN4[:, 7168:7424], arN4[:, 7424:7680]]]
        gOut = [[G("paout00"), G("paout01")], [G("paout10"), G("paout11")]]
        gbuf = [arN3[:, 4096:6144], arN3[:, 6144:8192]]
        gGbuf = [G("pag0"), G("pag1")]
        siluz = arN5[:, 0:2 * 4096]
        gSil = G("siluz")
        normedT = arN5[:, 8192:8192 + 32 * AB]
        gNT = G("normedT")

        for ab in range(NAB):
            t0 = ab * AB
            xt_load(xts, gx, t0, t0 + AB)
            for m in range(16 if "mix" in SUBA else 0):
                bsl = m % 2
                wblock_load(wA[bsl], gwA[bsl], wqkv_d, m, 16 * 512, f"dwW{bsl}")
                for fl in range(4):
                    f = m * 4 + fl
                    pb, gp = (pBk[0], gB[0]) if fl % 2 == 0 else (pBk[1], gB[1])
                    em.group("t", [
                        (lambda e, kk=k, fl_=fl, b=wA[bsl], p=pb: e.matmul(
                            p[0:128, 0:AB],
                            b[:, kk * 512 + fl_ * 128:kk * 512 + (fl_ + 1) * 128],
                            xts[:, kk * AB:(kk + 1) * AB], start=(kk == 0), stop=(kk == 15)))
                        for k in range(16)],
                        reads=[gwA[bsl], gx], writes=[gp])
                    ms, gms = mixsb[fl % 2], gMix[fl % 2]
                    em.gop("v", lambda e, p=pb, d=ms: e.tensor_copy(out=d[:, 3:3 + AB],
                                                                    in_=p[0:128, 0:AB]),
                           reads=[gp], writes=[gms])
                    em.gop("v", lambda e, d=ms, f_=f: e.tensor_copy(
                        out=d[:, 0:3], in_=mixstash[:, f_ * 3:(f_ + 1) * 3]),
                        reads=[G("mixstash")], writes=[gms])
                    em.gop("v", lambda e, d=ms, f_=f: e.tensor_copy(
                        out=mixstash[:, f_ * 3:(f_ + 1) * 3], in_=d[:, AB:AB + 3]),
                        reads=[gms], writes=[G("mixstash")])
                    em.gop("s", lambda e, d=ms, f_=f: e.activation(
                        out=ct0, in_=d[:, 0:AB], func=AF.Copy,
                        scale=cw_sb[:, f_ * KC + 0:f_ * KC + 1]),
                        reads=[gms, G("cw_sb")], writes=[gCt])
                    em.gop("s", lambda e, d=ms, f_=f: e.activation(
                        out=ct1, in_=d[:, 1:1 + AB], func=AF.Copy,
                        scale=cw_sb[:, f_ * KC + 1:f_ * KC + 2]),
                        reads=[gms, G("cw_sb")], writes=[gCt])
                    em.gop("v", lambda e: e.tensor_tensor(out=convacc, in0=ct0, in1=ct1,
                                                          op=ALU.add),
                           reads=[gCt], writes=[gCv])
                    em.gop("v", lambda e, d=ms, f_=f: e.scalar_tensor_tensor(
                        out=convacc, in0=d[:, 2:2 + AB],
                        scalar=cw_sb[:, f_ * KC + 2:f_ * KC + 3], in1=convacc,
                        op0=ALU.mult, op1=ALU.add),
                        reads=[gms, G("cw_sb")], writes=[gCv])
                    em.gop("v", lambda e, d=ms, f_=f: e.scalar_tensor_tensor(
                        out=convacc, in0=d[:, 3:3 + AB],
                        scalar=cw_sb[:, f_ * KC + 3:f_ * KC + 4], in1=convacc,
                        op0=ALU.mult, op1=ALU.add),
                        reads=[gms, G("cw_sb")], writes=[gCv])
                    em.gop("s", lambda e, f_=f: e.activation(
                        out=qkvT[:, f_ * AB:(f_ + 1) * AB], in_=convacc, func=AF.Silu),
                        reads=[gCv], writes=[gQKV])
            for f in range(32 if "l2n" in SUBA else 0):
                isq = f < 16
                em.gop("v", lambda e, f_=f: e.tensor_tensor(
                    out=sq_sb[:, 0:AB], in0=qkvT[:, f_ * AB:(f_ + 1) * AB],
                    in1=qkvT[:, f_ * AB:(f_ + 1) * AB], op=ALU.mult),
                    reads=[gQKV], writes=[G("sq_sb")])
                em.gop("t", lambda e: e.matmul(pBk[2][0:1, 0:AB], onescolb[:], sq_sb[:, 0:AB],
                                               start=True, stop=True),
                       reads=[G("sq_sb"), G("onescolb")], writes=[gB[2]])
                em.gop("s", lambda e: e.activation(out=rowsc[0:1, 0:AB], in_=pBk[2][0:1, 0:AB],
                                                   func=AF.Ln, bias=L2_EPS),
                       reads=[gB[2]], writes=[G("rowsc")])
                em.gop("s", lambda e, b_=(-0.5 * LN128 if isq else 0.0): e.activation(
                    out=rowsc[0:1, AB:2 * AB], in_=rowsc[0:1, 0:AB], func=AF.Exp,
                    scale=-0.5, bias=b_),
                    reads=[G("rowsc")], writes=[G("rowsc")])
                em.gop("t", lambda e: e.matmul(pBk[2][0:128, 256:256 + AB], ones1f[:],
                                               rowsc[0:1, AB:2 * AB], start=True, stop=True),
                       reads=[G("rowsc"), G("ones1f")], writes=[gB[2]])
                em.gop("v", lambda e, f_=f: e.tensor_tensor(
                    out=qkvT[:, f_ * AB:(f_ + 1) * AB], in0=qkvT[:, f_ * AB:(f_ + 1) * AB],
                    in1=pBk[2][0:128, 256:256 + AB], op=ALU.mult),
                    reads=[gB[2]], writes=[gQKV])
            for fb in range(8 if "z" in SUBA else 0):
                bsl = fb % 2
                wblock_load(wA[bsl], gwA[bsl], wz_d, fb, 16 * 512, f"dwW{bsl}")
                for c in range(CPAB):
                    pb, gp = (pBk[3], gB[3]) if c % 2 == 0 else (pBk[4], gB[4])
                    em.group("t", [
                        (lambda e, kk=k, b=wA[bsl], p=pb, c_=c: e.matmul(
                            p[0:128, 0:512], xts[:, kk * AB + c_ * 128:kk * AB + (c_ + 1) * 128],
                            b[:, kk * 512:(kk + 1) * 512], start=(kk == 0), stop=(kk == 15)))
                        for k in range(16)],
                        reads=[gwA[bsl], gx], writes=[gp])
                    em.gop("s", lambda e, p=pb, c_=c, fb_=fb: e.activation(
                        out=siluz[:, c_ * 4096 + fb_ * 512:c_ * 4096 + (fb_ + 1) * 512],
                        in_=p[0:128, 0:512], func=AF.Silu),
                        reads=[gp], writes=[gSil])
            if "core" in SUBA:
                pend = None
                for c in range(CPAB):
                    for hv in range(NHV):
                        ctx = core_front(ab, c, hv)
                        if pend is not None:
                            core_back(pend)
                        pend = ctx
                core_back(pend)
            for c in range(CPAB):
                dma_y(lambda e, d=xsl[c], s=x_d[t0 + c * 128:t0 + (c + 1) * 128, :]:
                      e.dma_start(out=d, in_=s), f"dxl{c}", writes=[gXsl[c]])
                dma_y(lambda e, d=gbuf[c], s=g_d[t0 + c * 128:t0 + (c + 1) * 128, :]:
                      e.dma_start(out=d, in_=s), f"dxg{c}", writes=[gGbuf[c]])
            for blk in range(8 if "smm" in SUBA else 0):
                bsl = blk % 2
                wblock_load(wA[bsl], gwA[bsl], wout_d, blk, 32 * 256, f"dwW{bsl}")
                for c in range(CPAB):
                    pb, gp = (pBk[5], gB[5]) if c % 2 == 0 else (pBk[6], gB[6])
                    em.group("t", [
                        (lambda e, kv_=kv, b=wA[bsl], p=pb, c_=c: e.matmul(
                            p[0:128, 0:256],
                            normedT[:, kv_ * AB + c_ * 128:kv_ * AB + (c_ + 1) * 128],
                            b[:, kv_ * 256:(kv_ + 1) * 256], start=(kv_ == 0), stop=(kv_ == 31)))
                        for kv in range(32)],
                        reads=[gwA[bsl], gNT], writes=[gp])
                    sl = blk % 2
                    osl = outst[c][sl]
                    go = gOut[c][sl]
                    em.gop("v", lambda e, p=pb, blk_=blk, c_=c, o=osl: e.tensor_tensor(
                        out=o, in0=p[0:128, 0:256],
                        in1=gbuf[c_][:, blk_ * 256:(blk_ + 1) * 256], op=ALU.mult),
                        reads=[gp, gGbuf[c]], writes=[go])
                    em.gop("v", lambda e, blk_=blk, c_=c, o=osl: e.tensor_tensor(
                        out=o, in0=o,
                        in1=xsl[c_][:, blk_ * 256:(blk_ + 1) * 256], op=ALU.add),
                        reads=[gXsl[c]], writes=[go])
                    dma_y(lambda e, blk_=blk, c_=c, o=osl, t0_=t0: e.dma_start(
                        out=out_d[t0_ + c_ * 128:t0_ + (c_ + 1) * 128,
                                  blk_ * 256:(blk_ + 1) * 256], in_=o),
                        f"dxo{c}{sl}", reads=[go])

    if 0 in phases:
        phase0()
    em.barrier()
    if 1 in phases:
        phaseB()
    em.barrier()
    if 2 in phases:
        phaseF()
    em.barrier()
    if 3 in phases:
        phaseA()
    em.op("y", lambda e: None,
          waits=[(nm, em.counts[nm]) for nm in sorted(em.dma_sems)], sem=None)

    sems = {}
    for nm in ("t", "v", "s"):
        sems[nm] = es.enter_context(nc.semaphore(f"sem_{nm}"))
    for nm in sorted(em.dma_sems):
        sems[nm] = es.enter_context(nc.semaphore(f"sem_{nm}"))

    with nc.Block() as block:
        @block.tensor
        def _(t):
            em.replay("t", t, sems)

        @block.vector
        def _(v):
            em.replay("v", v, sems)

        @block.scalar
        def _(s):
            em.replay("s", s, sems)

        @block.gpsimd
        def _(g):
            em.replay("g", g, sems)

        @block.sync
        def _(y):
            em.replay("y", y, sems)

    es.close()
    return nc


_NC_CACHE = {}


def _band(w, cb):
    w = np.ascontiguousarray(np.asarray(w, dtype=np.float32))
    R, C = w.shape
    kt, nb = R // 128, C // cb
    return np.ascontiguousarray(
        w.reshape(kt, 128, nb, cb).transpose(2, 1, 0, 3).reshape(nb, 128, kt * cb))


def kernel(L0_attn_hidden, L0_attn_W_qkv, L0_attn_W_z, L0_attn_W_b, L0_attn_W_a,
           L0_attn_conv_weight, L0_attn_dt_bias, L0_attn_A_log, L0_attn_norm_w,
           L0_attn_W_out, L0_ffn_gate_w, L0_ffn_w1, L0_ffn_w2, L0_ffn_w3,
           L0_ffn_sw1, L0_ffn_sw2, L0_ffn_sw3, L0_ffn_expert_bias):
    import ml_dtypes
    B, T, _ = L0_attn_hidden.shape
    if T not in _NC_CACHE:
        _NC_CACHE[T] = build_nc(T)
    nc = _NC_CACHE[T]
    bfv = lambda a: np.ascontiguousarray(a).astype(ml_dtypes.bfloat16).view(np.uint16)
    f = lambda a: np.ascontiguousarray(np.asarray(a, dtype=np.float32))
    common = {
        "wqkv_b": _band(L0_attn_W_qkv, 512), "wz_b": _band(L0_attn_W_z, 512),
        "wb": f(L0_attn_W_b),
        "wa": f(L0_attn_W_a), "cw": f(L0_attn_conv_weight), "dtb": f(L0_attn_dt_bias),
        "alog": f(L0_attn_A_log), "nw": f(L0_attn_norm_w),
        "wout_b": _band(L0_attn_W_out, 256),
        "gatew": f(L0_ffn_gate_w),
        "w1_b": np.stack([_band(L0_ffn_w1[i], 256) for i in range(NE)]),
        "w2_b": np.stack([_band(L0_ffn_w2[i], 512) for i in range(NE)]),
        "w3_b": np.stack([_band(L0_ffn_w3[i], 256) for i in range(NE)]),
        "sw1_b": _band(L0_ffn_sw1, 256), "sw2_b": _band(L0_ffn_sw2, 512),
        "sw3_b": _band(L0_ffn_sw3, 256), "ebias": f(L0_ffn_expert_bias),
        "i_consts": np.tile(np.array([LN_EPS, RMS_EPS, 1.0, -0.5 * LN128, 0.0, 0.0, 0.0, 0.0],
                                     np.float32)[None, :], (128, 1)),
        "i_identb": bfv(np.eye(128, dtype=np.float32)),
        "i_identf": np.eye(128, dtype=np.float32),
        "i_ones1f": np.ones((1, 128), np.float32),
        "i_onescolb": bfv(np.ones((128, 1), np.float32)),
        "i_masks": host_masks(),
        "i_iotaE": np.tile(np.arange(NE, dtype=np.float32)[None, :], (128, 1)),
    }
    in_maps = []
    for b in range(B):
        m = dict(common)
        m["x"] = f(L0_attn_hidden[b])
        in_maps.append(m)
    res = run_bass_kernel_spmd(nc, in_maps, core_ids=list(range(B)))
    out = np.stack([np.asarray(res.results[i]["out"]) for i in range(B)], axis=0)
    return out.astype(np.float32)
